# version 07
# baseline (speedup 1.0000x reference)
import os as _os
import numpy as np
import ml_dtypes
import concourse.bass as bass
import concourse.mybir as mybir
from concourse.bass_utils import run_bass_kernel_spmd
from concourse.aot_env import allow_ici_all_to_all

F32 = mybir.dt.float32
BF16 = mybir.dt.bfloat16
F8 = mybir.dt.float8e4
AF = mybir.ActivationFunctionType
ALU = mybir.AluOpType
BF = ml_dtypes.bfloat16

D = 2048
DH = 128
CH = 16
GT = 128
NCORE = 8
EPS_A = 1e-6
EPS_M = 1e-5

ENGS = ("sync", "gpsimd", "vector", "scalar", "tensor")


COMPUTE_ENGS = ("vector", "scalar", "tensor")


class Sched:

    def __init__(self):
        self.ops = {e: [] for e in ENGS}
        self.cnt = {e: 0 for e in ENGS}
        self.cnt["pool"] = 0
        self.step = {"sync": 16, "gpsimd": 16, "vector": 1, "scalar": 1,
                     "tensor": 1, "pool": 1}

    def op(self, eng, fn, inc=False, waits=(), sem=None):
        ch = sem or eng
        self.ops[eng].append((fn, bool(inc), tuple(waits), ch))
        if ch == "pool" or eng in COMPUTE_ENGS or inc:
            self.cnt[ch] += self.step[ch]
        return self.cnt[ch]

    def now(self, eng):
        return self.cnt[eng]

    def barrier(self, eng, others):
        w = tuple((o, self.cnt[o]) for o in others if self.cnt[o] > 0)
        if w:
            self.ops[eng].append((None, False, w, eng))

    def emit(self, block, sems):
        import os as _os
        import bisect
        ssv = _os.environ.get("BASS_SELFSYNC", "vector")
        full = ssv == "full"
        full_engs = set(ssv.split(",")) if ssv not in ("min", "full") else (
            set(COMPUTE_ENGS) if full else set())
        waited = {e: set() for e in ENGS}
        waited["pool"] = set()
        for e in list(self.ops):
            for fn, inc, waits, ch in self.ops[e]:
                for we, wv in waits:
                    if wv > 0 and we in waited:
                        waited[we].add(wv)
        flagged = {e: sorted(waited[e]) for e in COMPUTE_ENGS}

        class _TProxy:
            def __init__(self, h):
                self._h = h

            def __getattr__(self, k):
                return getattr(self._h, k)

            def matmul(self, *a, **kw):
                kw.setdefault("skip_group_check", True)
                return self._h.matmul(*a, **kw)

        def rv(we, wv):
            if we not in COMPUTE_ENGS or we in full_engs:
                return wv
            import bisect as _b
            return _b.bisect_right(flagged[we], wv)

        def mk(eng):
            selfsync = eng in COMPUTE_ENGS
            efull = eng in full_engs
            miles = sorted(waited[eng]) if not selfsync else []
            fset = set(flagged.get(eng, ()))

            def body(h):
                n = 0
                npool = 0
                enforced = 0
                mi = 0
                for fn, inc, waits, ch in self.ops[eng]:
                    for we, wv in waits:
                        wv2 = rv(we, wv)
                        if wv2 > 0:
                            h.wait_ge(sems[we], wv2)
                    if fn is not None:
                        if eng == "tensor":
                            h2 = _TProxy(h)
                        else:
                            h2 = h
                        if ch == "pool":
                            ins = fn(h2)
                            npool += 1
                            ins.then_inc(sems["pool"], 1)
                            continue
                        if selfsync:
                            if efull and n > 0:
                                h.wait_ge(sems[eng], n)
                        elif inc:
                            tgt = 0
                            while mi < len(miles) and miles[mi] <= n:
                                tgt = miles[mi]
                                mi += 1
                            if tgt > enforced:
                                h.wait_ge(sems[eng], tgt)
                                enforced = tgt
                        ins = fn(h2)
                        if selfsync:
                            n += 1
                            if efull:
                                ins.then_inc(sems[eng], 1)
                            elif n in fset:
                                ins.then_inc(sems[eng], 1)
                        elif inc:
                            ins.then_inc(sems[eng], self.step[eng])
                            n += self.step[eng]
            return body
        block.sync(mk("sync"))
        block.gpsimd(mk("gpsimd"))
        block.vector(mk("vector"))
        block.scalar(mk("scalar"))
        block.tensor(mk("tensor"))



def blk2(ap0, c0, w, blk):
    a = ap0[:, c0:c0 + w]
    return bass.AP(tensor=a.tensor, offset=a.offset,
                   ap=[list(a.ap[0]), [blk, 2], [1, w]])

def build_program(T):
    NG = T // GT
    Tc = T // NCORE
    NLG = Tc // GT
    NCH = GT // CH
    LGQ = 2 if NLG >= 2 else 1
    NQ = NLG // LGQ
    GQ = NCORE * LGQ
    TS = LGQ * GT
    assert T % (GT * NCORE) == 0 and NLG % LGQ == 0

    import os as _os
    nc = bass.Bass("TRN2", target_bir_lowering=False,
                   detect_race_conditions=not _os.environ.get("KERN_NORACE"))
    s = Sched()

    def din(name, shape, dt):
        return nc.dram_tensor(name, shape, dt, kind="ExternalInput").ap()

    xT = din("xT", [128, 16, T], F32)
    wf = din("wf", [128, 16 * 8 * 128], F32)
    wo = din("wo", [16, 128, 16 * 128], F32)
    wg2 = din("wg2", [128, 256], F32)
    lnw_c = din("lnw_c", [128, 2], F32)
    lnb_c = din("lnb_c", [128, 2], F32)
    nw_c = din("nw_c", [128, 2], F32)
    gwt = din("gwt", [48, 128, 16 * 128], F32)
    uwt = din("uwt", [48, 128, 16 * 128], F32)
    dwt = din("dwt", [16, 128, 48 * 128], F32)
    gsc = din("gsc", [128, 48], F32)
    usc = din("usc", [128, 48], F32)
    dsc = din("dsc", [128, 16], F32)
    idn = din("idn", [128, 128], BF16)
    mkKL = din("mkKL", [64, 128], F32)
    mkKH = din("mkKH", [64, 128], F32)
    mkGL = din("mkGL", [64, 128], F32)
    mkGH = din("mkGH", [64, 128], F32)
    onesc = din("onesc", [128, 1], F32)
    onesf = din("onesf", [128, 128], F32)
    onesrow = din("onesrow", [1, 128], F32)
    xres = din("xres", [128, 16, Tc], F32)

    out = nc.dram_tensor("out", [Tc, D], F32, kind="ExternalOutput").ap()
    o_stage = [nc.dram_tensor("o_stage%d" % q, [NCORE * LGQ * 2 * 128, GT], BF16)
               for q in range(NQ)]
    o_gath = [nc.dram_tensor("o_gath%d" % q, [NCORE * LGQ * 2 * 128, GT], BF16)
              for q in range(NQ)]
    o_st5 = [t.ap().rearrange("(a l hh p) c -> a l hh p c",
                              a=NCORE, l=LGQ, hh=2, p=128) for t in o_stage]
    o_gt5 = [t.ap().rearrange("(a l hh p) c -> a l hh p c",
                              a=NCORE, l=LGQ, hh=2, p=128) for t in o_gath]
    dbg = nc.dram_tensor("dbg", [1, 8 * 512], F32, kind="ExternalOutput").ap() \
        if _os.environ.get("KERN_DBG") else None
    g8_dram = nc.dram_tensor("g8_dram", [128, 48 * 16 * 128], F8)
    u8_dram = nc.dram_tensor("u8_dram", [128, 48 * 16 * 128], F8)
    d8_dram = nc.dram_tensor("d8_dram", [128, 16 * 48 * 128], F8)

    import contextlib
    es = contextlib.ExitStack()
    with es:
        def sb(name, shape, dt):
            return es.enter_context(nc.sbuf_tensor(name, shape, dt))

        pbank = [es.enter_context(nc.psum_tensor("pb%d" % i, [128, 512], F32))
                 for i in range(8)]
        pqA, pqB, pA, pB, pS, pC, pQM, pSM = pbank
        pSC = pSM
        pSMb = pSM.ap().bitcast(BF16)

        sems = {e: es.enter_context(nc.semaphore("sem_" + e)) for e in ENGS}
        sems["cc"] = es.enter_context(nc.semaphore("sem_cc"))
        sems["pool"] = es.enter_context(nc.semaphore("sem_pool"))
        block = es.enter_context(nc.Block())

        c_id = sb("c_id", [128, 128], BF16)
        c_idf = sb("c_idf", [128, 128], F32)
        c_mkKL = sb("c_mkKL", [64, 128], F32)
        c_mkKH = sb("c_mkKH", [64, 128], F32)
        c_mkGL = sb("c_mkGL", [64, 128], F32)
        c_mkGH = sb("c_mkGH", [64, 128], F32)
        c_ones = sb("c_ones", [128, 128], F32)
        c_onesc = sb("c_onesc", [128, 1], F32)
        c_onesb = sb("c_onesb", [128, 1], BF16)
        c_onesrow = sb("c_onesrow", [1, 128], F32)
        c_lnw = sb("c_lnw", [128, 2], F32)
        c_lnb = sb("c_lnb", [128, 2], F32)
        c_nw = sb("c_nw", [128, 2], F32)
        c_w2 = sb("c_w2", [128, 2], F32)
        c_wb = sb("c_wb", [128, 2], F32)
        c_negw = sb("c_negw", [128, 2], F32)
        c_w2B = sb("c_w2B", [128, 32], F32)
        c_epsA = sb("c_epsA", [128, 1], F32)
        c_epsM = sb("c_epsM", [128, 1], F32)

        for dst, src in [(c_id, idn), (c_mkKL, mkKL), (c_mkKH, mkKH),
                         (c_mkGL, mkGL), (c_mkGH, mkGH), (c_ones, onesf),
                         (c_onesc, onesc), (c_onesrow, onesrow),
                         (c_lnw, lnw_c), (c_lnb, lnb_c), (c_nw, nw_c)]:
            s.op("sync", lambda h, d=dst, sr=src: h.dma_start(out=d[:], in_=sr),
                 inc=True)
        ld0 = s.now("sync")

        oslab = sb("oslab", [128, 16 * TS], BF16)
        xinb = sb("xinb", [128, 16 * TS], BF16)
        zdn = sb("zdn", [128, 16 * TS], F32)
        x1sl = sb("x1sl", [128, 16 * TS], BF16)
        x2sl = sb("x2sl", [128, 16 * TS], BF16)
        fusl = sb("fusl", [128, 48 * TS], BF16)
        wrng = sb("wrng", [128, 4 * 2048], BF16)
        drng = sb("drng", [128, 6144], BF16)
        f8r = sb("f8r", [128, 2 * 2048], F8)
        f8d = sb("f8d", [128, 6144], F8)
        zb2 = sb("zb2", [128, 2 * TS], BF16)
        zq2 = sb("zq2", [128, 2 * TS], BF16)
        gth = sb("gth", [128, TS], F32)
        gxh = sb("gxh", [128, TS], F32)
        guv = sb("guv", [128, TS], F32)
        gvv = sb("gvv", [128, TS], F32)
        sac1 = sb("sac1", [1, TS], F32)
        sac2 = sb("sac2", [1, TS], F32)
        stmp = sb("stmp", [1, 4 * TS], F32)
        rB1 = sb("rB1", [128, 3 * TS], F32)
        csc1 = sb("csc1", [128, 3 * TS], F32)
        csc2 = sb("csc2", [128, 2 * TS], F32)
        outp = sb("outp", [128, 2 * 128], F32)
        gscb = sb("gscb", [128, 48], F32)
        uscb = sb("uscb", [128, 48], F32)
        dscb = sb("dscb", [128, 16], F32)
        ngs = sb("ngs", [128, 48], F32)
        dbg16 = sb("dbg16", [1, 64], F32) if _os.environ.get("KERN_DBG") \
            else None

        s.op("sync", lambda h: h.dma_start(out=gscb[:], in_=gsc), inc=True)
        s.op("sync", lambda h: h.dma_start(out=uscb[:], in_=usc), inc=True)
        s.op("sync", lambda h: h.dma_start(out=dscb[:], in_=dsc), inc=True)
        ldsc = s.now("sync")
        s.op("vector", lambda h: h.tensor_scalar(
            out=ngs[:], in0=gscb[:], scalar1=-1.0, scalar2=None, op0=ALU.mult),
            inc=True, waits=[("sync", ldsc)])

        scan_done = {}
        qkv_evac_done = {}
        qkv_pe_done = {}
        for g in range(-4, 0):
            scan_done[g] = 0
            qkv_evac_done[g] = 0
            qkv_pe_done[g] = 0

        esA = contextlib.ExitStack()
        with esA:
            def sba(name, shape, dt):
                return esA.enter_context(nc.sbuf_tensor(name, shape, dt))

            wfb = sba("wfb", [128, 16 * 8 * 128], BF16)
            wg2b = sba("wg2b", [128, 256], BF16)
            csWb = sba("csWb", [1, 8 * 128], BF16)
            W = 2
            xb = sba("xb", [128, W * 2048], BF16)
            sqb = sba("sqb", [128, 2048], BF16)
            qs_f = sba("qs_f", [128, W * 256], F32)
            qs_b = sba("qs_b", [128, W * 256], BF16)
            kn_f = sba("kn_f", [128, W * 256], F32)
            kn_b = sba("kn_b", [128, W * 256], BF16)
            v_f = sba("v_f", [128, W * 256], F32)
            g1_b = sba("g1_b", [128, W * 128], BF16)
            eT = sba("eT", [128, W * 4], F32)
            rowbuf = sba("rowbuf", [128, W * 128], F32)
            s_op = sba("s_op", [128, 256], BF16)
            vnstkL = sba("vnstkL", [64, 256], BF16)
            vnstkH = sba("vnstkH", [64, 256], BF16)
            aktL = sba("aktL", [64, 256], BF16)
            aktH = sba("aktH", [64, 256], BF16)
            agtL = sba("agtL", [64, 256], BF16)
            agtH = sba("agtH", [64, 256], BF16)
            vnstg = sba("vnstg", [128, 64], BF16)
            wbm = sba("wbm", [128, 256], F32)
            ketL = sba("ketL", [64, 256], BF16)
            ketH = sba("ketH", [64, 256], BF16)
            kescr = sba("kescr", [128, 128], BF16)
            scrA = sba("scrA", [128, 128], F32)
            scr1 = sba("scr1", [128, 32], F32)
            scr2 = sba("scr2", [128, 32], F32)
            rstdc = sba("rstdc", [128, 32], F32)
            khh = sba("khh", [128, 32], F32)
            strow = sba("strow", [1, 6 * 128], F32)
            strowb = sba("strowb", [1, 128], BF16)
            rstdB = sba("rstdB", [128, 128], F32)
            qscr1 = sba("qscr1", [128, 128], F32)
            qscr2 = sba("qscr2", [128, 128], F32)
            oscr = sba("oscr", [128, 256], F32)
            oth = sba("oth", [128, 128], F32)
            osig = sba("osig", [128, 128], F32)
            oo = sba("oo", [128, 128], F32)
            oo2 = sba("oo2", [128, 128], F32)
            obuf = sba("obuf", [128, 256], BF16)

            s.op("gpsimd", lambda h: h.dma_start(out=wfb[:], in_=wf), inc=True)
            s.op("gpsimd", lambda h: h.dma_start(out=wg2b[:], in_=wg2), inc=True)
            ld1 = s.now("gpsimd")

            s.op("vector", lambda h: h.tensor_tensor(
                out=c_w2[:], in0=c_lnw[:], in1=c_lnw[:], op=ALU.mult),
                waits=[("sync", ld0)])
            s.op("vector", lambda h: h.tensor_tensor(
                out=c_wb[:], in0=c_lnw[:], in1=c_lnb[:], op=ALU.mult))
            s.op("vector", lambda h: h.tensor_scalar(
                out=c_negw[:], in0=c_lnw[:], scalar1=-1.0, scalar2=None,
                op0=ALU.mult))
            s.op("vector", lambda h: h.tensor_copy(out=c_idf[:], in_=c_id[:]))
            s.op("vector", lambda h: h.tensor_copy(out=c_onesb[:], in_=c_onesc[:]))
            for hh in range(2):
                s.op("vector", lambda h, hh=hh: h.tensor_scalar(
                    out=c_w2B[:, hh * 16:hh * 16 + 16], in0=c_ones[:, 0:16],
                    scalar1=c_w2[:, hh:hh + 1], scalar2=None, op0=ALU.mult))
            s.op("vector", lambda h: h.memset(s_op[:], 0.0))
            s.op("vector", lambda h: h.memset(rowbuf[:], 0.0))
            s.op("vector", lambda h: h.memset(c_epsA[:], EPS_A))
            s.op("vector", lambda h: h.memset(c_epsM[:], EPS_M), inc=True)
            pre0 = s.now("vector")

            for t in range(8):
                for d in range(16):
                    wv = [("gpsimd", ld1), ("vector", pre0)] \
                        if (t == 0 and d == 0) else ()
                    s.op("tensor", lambda h, t=t, d=d: h.matmul(
                        pQM[0:1, 0:128], c_onesb[:],
                        wfb[:, (d * 8 + t) * 128:(d * 8 + t) * 128 + 128],
                        start=(d == 0), stop=(d == 15)), waits=wv)
                pe_cs = s.now("tensor")
                s.op("vector", lambda h, t=t: h.tensor_scalar(
                    out=csWb[0:1, t * 128:t * 128 + 128], in0=pQM[0:1, 0:128],
                    scalar1=-1.0, scalar2=None, op0=ALU.mult),
                    inc=True, waits=[("tensor", pe_cs)])
                s.barrier("tensor", ["vector"])

            def qkv_group(g):
                sg = g % W
                xo = sg * 2048
                ao = sg * 256
                wv = [("tensor", qkv_pe_done[g - W])] if g >= W else ()
                s.op("gpsimd", lambda h, g=g, xo=xo: h.dma_start(
                    out=xb[:, xo:xo + 2048].rearrange("p (d c) -> p d c", d=16),
                    in_=xT[:, :, g * GT:(g + 1) * GT]), inc=True, waits=wv)
                ldx = s.now("gpsimd")
                s.op("scalar", lambda h, xo=xo: h.activation(
                    out=sqb[:, 0:2048], in_=xb[:, xo:xo + 2048],
                    func=AF.Square), inc=True,
                    waits=[("gpsimd", ldx), ("tensor", qkv_pe_done[g - 1])])
                sqd = s.now("scalar")

                for t in range(8):
                    pq = pqA if t < 4 else pqB
                    for d in range(16):
                        wv = [("gpsimd", ldx), ("vector", qkv_evac_done[g - 1]),
                              ("vector", scan_done[g - W])] \
                            if (t == 0 and d == 0) else ()
                        s.op("tensor", lambda h, t=t, d=d, pq=pq, xo=xo: h.matmul(
                            pq[:, (t % 4) * 128:(t % 4) * 128 + 128],
                            wfb[:, (d * 8 + t) * 128:(d * 8 + t) * 128 + 128],
                            xb[:, xo + d * 128: xo + d * 128 + 128],
                            start=(t % 4 == 0 and d == 0), stop=(d == 15)),
                            waits=wv)
                for d in range(16):
                    wv = [("scalar", sqd), ("vector", scan_done[g - 1]),
                          ("scalar", s.now("scalar"))] if d == 0 else ()
                    s.op("tensor", lambda h, d=d, xo=xo: h.matmul(
                        pQM[0:1, 0:128], c_onesb[:],
                        sqb[:, d * 128: d * 128 + 128],
                        start=(d == 0), stop=(d == 15)), waits=wv)
                pe_mm = s.now("tensor")

                ext = pqB[:, 384:512]
                s.op("vector", lambda h, ext=ext: h.tensor_copy(
                    out=strow[0:1, 640:768], in_=ext[0:1, 0:128]),
                    waits=[("tensor", pe_mm)])
                s.op("vector", lambda h: h.tensor_tensor(
                    out=strow[0:1, 128:256], in0=strow[0:1, 640:768],
                    in1=strow[0:1, 640:768], op=ALU.mult))
                s.op("vector", lambda h: h.scalar_tensor_tensor(
                    out=strow[0:1, 128:256], in0=strow[0:1, 128:256],
                    scalar=-1.0 / D, in1=pQM[0:1, 0:128], op0=ALU.mult,
                    op1=ALU.add))
                s.op("vector", lambda h, ext=ext: h.tensor_scalar(
                    out=strowb[0:1, :], in0=ext[0:1, 0:128], scalar1=1.0 / D,
                    scalar2=None, op0=ALU.mult), inc=True)
                dve_r = s.now("vector")
                s.op("scalar", lambda h: h.activation(
                    out=strow[0:1, 256:384], in_=strow[0:1, 128:256], func=AF.Ln,
                    scale=1.0 / D, bias=c_epsM[0:1, :]), waits=[("vector", dve_r)])
                s.op("scalar", lambda h: h.activation(
                    out=strow[0:1, 0:128], in_=strow[0:1, 256:384], func=AF.Exp,
                    scale=-0.5), inc=True)
                act_r = s.now("scalar")

                for t in range(8):
                    pq = pqA if t < 4 else pqB
                    s.op("tensor", lambda h, t=t, pq=pq: h.matmul(
                        pq[:, (t % 4) * 128:(t % 4) * 128 + 128],
                        csWb[0:1, t * 128:t * 128 + 128], strowb[0:1, :],
                        start=False, stop=True),
                        waits=[("scalar", act_r)] if t == 0 else ())
                pe_fix = s.op("tensor", lambda h: h.matmul(
                    pQM[:, 128:256], c_onesrow[0:1, :], strow[0:1, 0:128],
                    start=True, stop=True), inc=True)
                qkv_pe_done[g] = pe_fix
                s.op("vector", lambda h: h.tensor_copy(
                    out=rstdB[:], in_=pQM[:, 128:256]), inc=True,
                    waits=[("tensor", pe_fix), ("vector", scan_done[g - W])])
                s.barrier("tensor", ["vector"])

                for hh in range(2):
                    tq = pqA[:, hh * 128:hh * 128 + 128]
                    qf = qs_f[:, ao + hh * 128: ao + hh * 128 + 128]
                    q_ev = s.op("vector", lambda h, tq=tq, qf=qf: h.tensor_tensor(
                        out=qf, in0=tq, in1=rstdB[:], op=ALU.mult), inc=True)
                    s.op("scalar", lambda h, qf=qf: h.activation(
                        out=oth[:], in_=qf, func=AF.Exp, scale=-1.0),
                        waits=[("vector", q_ev)])
                    s.op("scalar", lambda h: h.activation(
                        out=oth[:], in_=oth[:], func=AF.Ln, scale=1.0,
                        bias=c_onesc[:]))
                    a_t = s.op("scalar", lambda h: h.activation(
                        out=oth[:], in_=oth[:], func=AF.Exp, scale=-1.0),
                        inc=True)
                    s.op("vector", lambda h, qf=qf: h.tensor_tensor(
                        out=qf, in0=qf, in1=oth[:], op=ALU.mult),
                        waits=[("scalar", a_t)])
                    s.op("vector", lambda h, qf=qf, hh=hh, ao=ao: h.tensor_copy(
                        out=qs_b[:, ao + hh * 128: ao + hh * 128 + 128], in_=qf),
                        inc=True)
                    s.barrier("scalar", ["vector"])
                for hh in range(2):
                    tk = pqA[:, (2 + hh) * 128:(2 + hh) * 128 + 128]
                    kf = kn_f[:, ao + hh * 128: ao + hh * 128 + 128]
                    k_ev = s.op("vector", lambda h, tk=tk, kf=kf: h.tensor_tensor(
                        out=kf, in0=tk, in1=rstdB[:], op=ALU.mult), inc=True)
                    k_sq = s.op("scalar", lambda h, kf=kf: h.activation(
                        out=qscr1[:], in_=kf, func=AF.Square),
                        inc=True, waits=[("vector", k_ev)])
                    k_mm = s.op("tensor", lambda h: h.matmul(
                        pQM[0:1, 0:128], c_onesc[:], qscr1[:],
                        start=True, stop=True), inc=True, waits=[("scalar", k_sq)])
                    s.op("scalar", lambda h: h.activation(
                        out=strow[0:1, 384:512], in_=pQM[0:1, 0:128], func=AF.Ln,
                        scale=1.0, bias=c_epsA[0:1, :]), waits=[("tensor", k_mm)])
                    k_rn = s.op("scalar", lambda h: h.activation(
                        out=strow[0:1, 384:512], in_=strow[0:1, 384:512],
                        func=AF.Exp, scale=-0.5), inc=True)
                    k_bc = s.op("tensor", lambda h: h.matmul(
                        pQM[:, 128:256], c_onesrow[0:1, :], strow[0:1, 384:512],
                        start=True, stop=True), inc=True, waits=[("scalar", k_rn)])
                    s.op("vector", lambda h, kf=kf: h.tensor_tensor(
                        out=kf, in0=kf, in1=pQM[:, 128:256], op=ALU.mult),
                        waits=[("tensor", k_bc)])
                    s.op("vector", lambda h, kf=kf, hh=hh, ao=ao: h.tensor_copy(
                        out=kn_b[:, ao + hh * 128: ao + hh * 128 + 128], in_=kf),
                        inc=True)
                    s.barrier("tensor", ["vector"])
                    s.barrier("scalar", ["vector"])
                for hh in range(2):
                    tv = pqB[:, hh * 128:hh * 128 + 128]
                    s.op("vector", lambda h, tv=tv, hh=hh, ao=ao: h.tensor_tensor(
                        out=v_f[:, ao + hh * 128: ao + hh * 128 + 128],
                        in0=tv, in1=rstdB[:], op=ALU.mult))
                s.op("vector", lambda h: h.tensor_tensor(
                    out=qscr2[:], in0=pqB[:, 256:384], in1=rstdB[:], op=ALU.mult))
                s.op("vector", lambda h, sg=sg: h.tensor_copy(
                    out=g1_b[:, sg * 128:sg * 128 + 128], in_=qscr2[:]))
                s.op("vector", lambda h, sg=sg, ext=ext: h.tensor_tensor(
                    out=rowbuf[32:33, sg * 128:sg * 128 + 128], in0=ext[32:33, 0:128],
                    in1=rstdB[32:33, 0:128], op=ALU.mult))
                v_done = s.op("vector", lambda h, sg=sg, ext=ext: h.tensor_tensor(
                    out=rowbuf[64:65, sg * 128:sg * 128 + 128],
                    in0=ext[64:65, 0:128], in1=rstdB[64:65, 0:128], op=ALU.mult),
                    inc=True)
                for i, base in enumerate((32, 64)):
                    kw = [("vector", v_done)] if i == 0 else ()
                    rb = lambda h, base=base, sg=sg: rowbuf[
                        base:base + 1, sg * 128:sg * 128 + 128]
                    s.op("scalar", lambda h, base=base, sg=sg: h.activation(
                        out=rowbuf[base:base + 1, sg * 128:sg * 128 + 128],
                        in_=rowbuf[base:base + 1, sg * 128:sg * 128 + 128],
                        func=AF.Exp, scale=-1.0), waits=kw)
                    s.op("scalar", lambda h, base=base, sg=sg: h.activation(
                        out=rowbuf[base:base + 1, sg * 128:sg * 128 + 128],
                        in_=rowbuf[base:base + 1, sg * 128:sg * 128 + 128],
                        func=AF.Ln, scale=1.0, bias=c_onesc[0:1, :]))
                    s.op("scalar", lambda h, base=base, sg=sg: h.activation(
                        out=rowbuf[base:base + 1, sg * 128:sg * 128 + 128],
                        in_=rowbuf[base:base + 1, sg * 128:sg * 128 + 128],
                        func=AF.Exp, scale=-1.0), inc=(i == 1))
                e_th = s.now("scalar")
                for i, base in enumerate((32, 64)):
                    kw = [("scalar", e_th)] if i == 0 else ()
                    s.op("vector", lambda h, base=base, sg=sg: h.tensor_scalar(
                        out=rowbuf[base:base + 1, sg * 128:sg * 128 + 128],
                        in0=rowbuf[base:base + 1, sg * 128:sg * 128 + 128],
                        scalar1=-2.0, scalar2=None, op0=ALU.mult),
                        inc=(i == 1), waits=kw)
                e_neg = s.now("vector")
                i = 0
                for hhb, base in enumerate((32, 64)):
                    for half in range(2):
                        kw = [("vector", e_neg)] if i == 0 else ()
                        s.op("tensor", lambda h, base=base, sg=sg, i=i, half=half:
                             h.matmul(
                                 pQM[0:64, 256 + i:257 + i],
                                 rowbuf[base:base + 1,
                                        sg * 128 + half * 64:
                                        sg * 128 + half * 64 + 64],
                                 c_ones[base:base + 1, 0:1],
                                 start=True, stop=True),
                             inc=(i == 3), waits=kw)
                        i += 1
                e_tr = s.now("tensor")
                s.op("vector", lambda h, sg=sg: h.tensor_copy(
                    out=eT[0:64, sg * 4: sg * 4 + 4], in_=pQM[0:64, 256:260]),
                    inc=True, waits=[("tensor", e_tr)])
                s.barrier("tensor", ["vector"])
                qkv_evac_done[g] = s.now("vector")

            def scan_group(g):
                sg = g % W
                ao = sg * 256
                qev = qkv_evac_done[g]
                guard = qev

                if g >= 1:
                    for hh in range(2):
                        s.op("vector", lambda h, hh=hh: h.tensor_copy(
                            out=s_op[:, hh * 128:hh * 128 + 128],
                            in_=pS[:, hh * 128:hh * 128 + 128]),
                            inc=(hh == 1),
                            waits=[("vector", qev)] if hh == 0 else ())

                for hh in range(2):
                    kb = kn_b[:, ao + hh * 128: ao + hh * 128 + 128]
                    qb = qs_b[:, ao + hh * 128: ao + hh * 128 + 128]
                    for half, (aktX, agtX, mKX, mGX) in enumerate(
                            ((aktL, agtL, c_mkKL, c_mkGL),
                             (aktH, agtH, c_mkKH, c_mkGH))):
                        ec = eT[0:64, sg * 4 + hh * 2 + half:
                                sg * 4 + hh * 2 + half + 1]
                        mm1 = s.op("tensor", lambda h, kb=kb, half=half: h.matmul(
                            pSC[0:64, 256:384], kb[:, half * 64:half * 64 + 64],
                            kb, start=True, stop=True), inc=True,
                            waits=[("vector", qev)] if (hh == 0 and half == 0)
                            else ())
                        dv1 = s.op("vector", lambda h, hh=hh, ec=ec, aktX=aktX,
                                   mKX=mKX: h.scalar_tensor_tensor(
                                       out=aktX[:, hh * 128:hh * 128 + 128],
                                       in0=pSC[0:64, 256:384], scalar=ec,
                                       in1=mKX[:], op0=ALU.mult, op1=ALU.mult),
                                   inc=True, waits=[("tensor", mm1)])
                        mm2 = s.op("tensor", lambda h, kb=kb, qb=qb, half=half:
                                   h.matmul(
                                       pSC[0:64, 256:384],
                                       kb[:, half * 64:half * 64 + 64], qb,
                                       start=True, stop=True), inc=True,
                                   waits=[("vector", dv1)])
                        s.op("vector", lambda h, hh=hh, ec=ec, agtX=agtX,
                             mGX=mGX: h.scalar_tensor_tensor(
                                 out=agtX[:, hh * 128:hh * 128 + 128],
                                 in0=pSC[0:64, 256:384], scalar=ec, in1=mGX[:],
                                 op0=ALU.mult, op1=ALU.mult),
                             inc=True, waits=[("tensor", mm2)])
                        s.barrier("tensor", ["vector"])

                for hh in range(2):
                    s.op("vector", lambda h, hh=hh, ao=ao: h.tensor_tensor(
                        out=oscr[:, 0:128],
                        in0=v_f[:, ao + hh * 128:ao + hh * 128 + 128],
                        in1=kn_f[:, ao + hh * 128:ao + hh * 128 + 128],
                        op=ALU.subtract))
                    s.op("vector", lambda h, hh=hh: h.tensor_scalar(
                        out=wbm[:, hh * 128:hh * 128 + 128], in0=oscr[:, 0:128],
                        scalar1=c_negw[:, hh:hh + 1], scalar2=c_wb[:, hh:hh + 1],
                        op0=ALU.mult, op1=ALU.add), inc=True)
                    base = 32 + hh * 32
                    ebc = s.op("tensor", lambda h, sg=sg, base=base: h.matmul(
                        pQM[:, 128:256], c_ones[base:base + 1, 0:128],
                        rowbuf[base:base + 1, sg * 128:sg * 128 + 128],
                        start=True, stop=True), inc=True,
                        waits=[("vector", guard)] if hh == 0 else ())
                    kes = s.op("vector", lambda h, hh=hh, ao=ao: h.tensor_tensor(
                        out=kescr[:],
                        in0=kn_f[:, ao + hh * 128:ao + hh * 128 + 128],
                        in1=pQM[:, 128:256], op=ALU.mult), inc=True,
                        waits=[("tensor", ebc)])
                    ktr = 0
                    for half in range(2):
                        ktr = s.op("tensor", lambda h, half=half: h.transpose(
                            pSMb[0:64, half * 128:half * 128 + 128],
                            kescr[:, half * 64:half * 64 + 64], c_id[:]),
                            inc=True,
                            waits=[("vector", kes)] if half == 0 else ())
                    for half, ketX in enumerate((ketL, ketH)):
                        s.op("vector", lambda h, hh=hh, half=half, ketX=ketX:
                             h.tensor_copy(
                                 out=ketX[:, hh * 128:hh * 128 + 128],
                                 in_=pSMb[0:64, half * 128:half * 128 + 128]),
                             inc=(half == 1),
                             waits=[("tensor", ktr)] if half == 0 else ())
                    s.barrier("tensor", ["vector"])
                prep_done = s.now("vector")

                for hh in range(2):
                    s.op("tensor", lambda h, hh=hh, ao=ao: h.matmul(
                        pA[:, hh * 128:hh * 128 + 128],
                        s_op[:, hh * 128:hh * 128 + 128],
                        kn_b[:, ao + hh * 128:ao + hh * 128 + 128],
                        start=(hh == 0), stop=True),
                        waits=[("vector", prep_done)] if hh == 0 else ())
                    s.op("tensor", lambda h, hh=hh, ao=ao: h.matmul(
                        pB[:, hh * 128:hh * 128 + 128],
                        s_op[:, hh * 128:hh * 128 + 128],
                        qs_b[:, ao + hh * 128:ao + hh * 128 + 128],
                        start=(hh == 0), stop=True))
                _ = s.now("tensor")

                last_stk = s.now("vector")
                for j in range(NCH):
                    m = j // 2
                    c0 = j * CH
                    if j > 0:
                        KL = min(16 * j, 64)
                        KH = 16 * j - KL
                        for hh in range(2):
                            s.op("tensor", lambda h, hh=hh, KL=KL, c0=c0: h.matmul(
                                pA[:, hh * 128 + c0: hh * 128 + c0 + CH],
                                vnstkL[0:KL, hh * 128:hh * 128 + 128],
                                aktL[0:KL, hh * 128 + c0:hh * 128 + c0 + CH],
                                start=False, stop=True),
                                waits=[("vector", last_stk)] if hh == 0 else ())
                            if KH > 0:
                                s.op("tensor", lambda h, hh=hh, KH=KH, c0=c0:
                                     h.matmul(
                                         pA[:, hh * 128 + c0:
                                            hh * 128 + c0 + CH],
                                         vnstkH[0:KH, hh * 128:hh * 128 + 128],
                                         aktH[0:KH,
                                              hh * 128 + c0:hh * 128 + c0 + CH],
                                         start=False, stop=True))
                    corr_pe = s.now("tensor")

                    s.op("vector", lambda h, c0=c0: h.tensor_copy(
                        out=scrA[:, 0:32],
                        in_=blk2(pA, c0, CH, 128)),
                        waits=[("tensor", corr_pe)])
                    kh_ev = s.op("vector", lambda h: h.tensor_tensor(
                        out=scrA[:, 32:64], in0=scrA[:, 0:32], in1=scrA[:, 0:32],
                        op=ALU.mult), inc=True)
                    st1 = s.op("tensor", lambda h: h.matmul(
                        pSM[:, 128:192], c_ones[:], scrA[:, 0:64],
                        start=True, stop=True), inc=True, waits=[("vector", kh_ev)])
                    s.op("vector", lambda h: h.tensor_copy(
                        out=scr2[:], in_=pSM[:, 128:160]), waits=[("tensor", st1)])
                    s.op("vector", lambda h: h.tensor_tensor(
                        out=scr1[:], in0=scr2[:], in1=scr2[:], op=ALU.mult))
                    vpre = s.op("vector", lambda h: h.scalar_tensor_tensor(
                        out=scr1[:], in0=scr1[:], scalar=-1.0 / DH,
                        in1=pSM[:, 160:192], op0=ALU.mult, op1=ALU.add), inc=True)
                    s.op("scalar", lambda h: h.activation(
                        out=scr2[:], in_=scr1[:], func=AF.Ln, scale=1.0 / DH,
                        bias=c_epsA[:]), waits=[("vector", vpre)])
                    a_rs = s.op("scalar", lambda h: h.activation(
                        out=rstdc[:], in_=scr2[:], func=AF.Exp, scale=-0.5),
                        inc=True)
                    s.op("vector", lambda h: h.scalar_tensor_tensor(
                        out=khh[:], in0=pSM[:, 128:160], scalar=-1.0 / DH,
                        in1=scrA[:, 0:32], op0=ALU.mult, op1=ALU.add),
                        waits=[("scalar", a_rs)])
                    s.op("vector", lambda h: h.tensor_tensor(
                        out=khh[:], in0=khh[:], in1=rstdc[:], op=ALU.mult))
                    s.op("vector", lambda h: h.tensor_tensor(
                        out=scrA[:, 64:96], in0=khh[:], in1=c_w2B[:], op=ALU.mult))
                    s.op("vector", lambda h, c0=c0: h.tensor_tensor(
                        out=scrA[:, 64:96], in0=scrA[:, 64:96],
                        in1=blk2(wbm, c0, CH, 128), op=ALU.add))
                    gw_d = s.op("vector", lambda h: h.tensor_tensor(
                        out=scrA[:, 96:128], in0=scrA[:, 64:96], in1=khh[:],
                        op=ALU.mult), inc=True)
                    st2 = s.op("tensor", lambda h: h.matmul(
                        pSM[:, 192:256], c_ones[:], scrA[:, 64:128],
                        start=True, stop=True), inc=True, waits=[("vector", gw_d)])
                    s.op("vector", lambda h: h.scalar_tensor_tensor(
                        out=scr1[:], in0=khh[:], scalar=1.0 / DH,
                        in1=pSM[:, 224:256], op0=ALU.mult, op1=ALU.mult),
                        waits=[("tensor", st2)])
                    s.op("vector", lambda h: h.tensor_tensor(
                        out=scr1[:], in0=scrA[:, 64:96], in1=scr1[:],
                        op=ALU.subtract))
                    s.op("vector", lambda h: h.scalar_tensor_tensor(
                        out=scr1[:], in0=pSM[:, 192:224], scalar=-1.0 / DH,
                        in1=scr1[:], op0=ALU.mult, op1=ALU.add))
                    s.op("vector", lambda h: h.tensor_tensor(
                        out=scr2[:], in0=scr1[:], in1=rstdc[:], op=ALU.mult))
                    po = (j % 2) * 16
                    s.op("vector", lambda h, po=po: h.tensor_copy(
                        out=vnstg[:, po:po + 16], in_=scr2[:, 0:16]))
                    vn_d = s.op("vector", lambda h, po=po: h.tensor_copy(
                        out=vnstg[:, 32 + po:32 + po + 16], in_=scr2[:, 16:32]),
                        inc=True)

                    vstk = vnstkL if m < 2 else vnstkH
                    r0 = 32 * (m % 2)
                    nr = 16 if j % 2 == 0 else 32
                    src0 = 0 if j % 2 == 0 else 0
                    trs = 0
                    for hh in range(2):
                        trs = s.op("tensor", lambda h, hh=hh, r0=r0, nr=nr:
                                   h.transpose(
                                       pSMb[r0:r0 + nr,
                                            hh * 128:hh * 128 + 128],
                                       vnstg[:, hh * 32:hh * 32 + nr], c_id[:]),
                                   inc=True,
                                   waits=[("vector", vn_d)] if hh == 0 else ())
                    s.op("vector", lambda h, r0=r0, nr=nr, vstk=vstk:
                         h.tensor_copy(
                             out=vstk[r0:r0 + nr, 0:256],
                             in_=pSMb[r0:r0 + nr, 0:256]),
                         inc=True, waits=[("tensor", trs)])
                    last_stk = s.now("vector")
                    s.barrier("tensor", ["vector"])
                    if globals_pull[0] is not None:
                        globals_pull[0](1)

                for hh in range(2):
                    s.op("tensor", lambda h, hh=hh: h.matmul(
                        pB[:, hh * 128:hh * 128 + 128],
                        vnstkL[:, hh * 128:hh * 128 + 128],
                        agtL[:, hh * 128:hh * 128 + 128], start=False, stop=False),
                        waits=[("vector", last_stk)] if hh == 0 else ())
                    s.op("tensor", lambda h, hh=hh: h.matmul(
                        pB[:, hh * 128:hh * 128 + 128],
                        vnstkH[:, hh * 128:hh * 128 + 128],
                        agtH[:, hh * 128:hh * 128 + 128], start=False, stop=True))
                    s.op("tensor", lambda h, hh=hh, g=g: h.matmul(
                        pS[:, hh * 128:hh * 128 + 128],
                        ketL[:, hh * 128:hh * 128 + 128],
                        vnstkL[:, hh * 128:hh * 128 + 128],
                        start=(g == 0 and hh == 0), stop=False))
                    s.op("tensor", lambda h, hh=hh, g=g: h.matmul(
                        pS[:, hh * 128:hh * 128 + 128],
                        ketH[:, hh * 128:hh * 128 + 128],
                        vnstkH[:, hh * 128:hh * 128 + 128],
                        start=False, stop=True))
                qh_pe = s.now("tensor")

                for hh in range(2):
                    qf = qs_f[:, ao + hh * 128: ao + hh * 128 + 128]
                    s.op("vector", lambda h, hh=hh: h.tensor_copy(
                        out=oscr[:, 0:128], in_=pB[:, hh * 128:hh * 128 + 128]),
                        waits=[("tensor", qh_pe)] if hh == 0 else ())
                    o_ev = s.op("vector", lambda h: h.tensor_tensor(
                        out=oscr[:, 128:256], in0=oscr[:, 0:128],
                        in1=oscr[:, 0:128], op=ALU.mult), inc=True)
                    o_mm = s.op("tensor", lambda h: h.matmul(
                        pSM[:, 256:512], c_ones[:], oscr[:, 0:256],
                        start=True, stop=True), inc=True, waits=[("vector", o_ev)])
                    s.op("vector", lambda h: h.tensor_copy(
                        out=oo[:], in_=pSM[:, 256:384]), waits=[("tensor", o_mm)])
                    s.op("vector", lambda h: h.tensor_tensor(
                        out=oo[:], in0=oo[:], in1=oo[:], op=ALU.mult))
                    o_v = s.op("vector", lambda h: h.scalar_tensor_tensor(
                        out=oo[:], in0=oo[:], scalar=-1.0 / DH,
                        in1=pSM[:, 384:512], op0=ALU.mult, op1=ALU.add), inc=True)
                    s.op("scalar", lambda h: h.activation(
                        out=oo2[:], in_=oo[:], func=AF.Ln, scale=1.0 / DH,
                        bias=c_epsA[:]), waits=[("vector", o_v)])
                    o_rs = s.op("scalar", lambda h: h.activation(
                        out=oo2[:], in_=oo2[:], func=AF.Exp, scale=-0.5), inc=True)
                    s.op("vector", lambda h: h.scalar_tensor_tensor(
                        out=oo[:], in0=pSM[:, 256:384], scalar=-1.0 / DH,
                        in1=oscr[:, 0:128], op0=ALU.mult, op1=ALU.add),
                        waits=[("scalar", o_rs)])
                    s.op("vector", lambda h: h.tensor_tensor(
                        out=oo[:], in0=oo[:], in1=oo2[:], op=ALU.mult))
                    s.op("vector", lambda h, hh=hh: h.tensor_scalar(
                        out=oo[:], in0=oo[:], scalar1=c_lnw[:, hh:hh + 1],
                        scalar2=c_lnb[:, hh:hh + 1], op0=ALU.mult, op1=ALU.add))
                    o_d = s.op("vector", lambda h, qf=qf: h.tensor_tensor(
                        out=oo[:], in0=oo[:], in1=qf, op=ALU.add), inc=True)
                    g_mm = s.op("tensor", lambda h, hh=hh, sg=sg: h.matmul(
                        pQM[:, 384:512], wg2b[:, hh * 128:hh * 128 + 128],
                        g1_b[:, sg * 128:sg * 128 + 128], start=True, stop=True),
                        inc=True, waits=[("vector", o_d), ("vector", guard)])
                    s.op("scalar", lambda h: h.activation(
                        out=oth[:], in_=pQM[:, 384:512], func=AF.Exp, scale=-1.0),
                        waits=[("tensor", g_mm)])
                    s.op("scalar", lambda h: h.activation(
                        out=oth[:], in_=oth[:], func=AF.Ln, scale=1.0,
                        bias=c_onesc[:]))
                    g_th = s.op("scalar", lambda h: h.activation(
                        out=oth[:], in_=oth[:], func=AF.Exp, scale=-1.0), inc=True)
                    og_d = s.op("vector", lambda h: h.tensor_tensor(
                        out=oo[:], in0=oo[:], in1=oth[:], op=ALU.mult), inc=True,
                        waits=[("scalar", g_th)])
                    og_sq = s.op("scalar", lambda h: h.activation(
                        out=oo2[:], in_=oo[:], func=AF.Square), inc=True,
                        waits=[("vector", og_d)])
                    r_mm = s.op("tensor", lambda h: h.matmul(
                        pQM[0:1, 128:256], c_onesc[:], oo2[:],
                        start=True, stop=True), inc=True,
                        waits=[("scalar", og_sq)])
                    s.op("scalar", lambda h: h.activation(
                        out=strow[0:1, 512:640], in_=pQM[0:1, 128:256], func=AF.Ln,
                        scale=1.0 / DH, bias=c_epsA[0:1, :]),
                        waits=[("tensor", r_mm)])
                    r_rs = s.op("scalar", lambda h: h.activation(
                        out=strow[0:1, 512:640], in_=strow[0:1, 512:640],
                        func=AF.Exp, scale=-0.5), inc=True)
                    r_bc = s.op("tensor", lambda h: h.matmul(
                        pQM[:, 128:256], c_onesrow[0:1, :], strow[0:1, 512:640],
                        start=True, stop=True), inc=True, waits=[("scalar", r_rs)])
                    s.op("vector", lambda h: h.tensor_tensor(
                        out=oo[:], in0=oo[:], in1=pQM[:, 128:256], op=ALU.mult),
                        waits=[("tensor", r_bc)])
                    ob_d = s.op("vector", lambda h, hh=hh: h.tensor_scalar(
                        out=obuf[:, hh * 128:hh * 128 + 128], in0=oo[:],
                        scalar1=c_nw[:, hh:hh + 1], scalar2=None, op0=ALU.mult),
                        inc=True)
                    s.barrier("tensor", ["vector"])
                    s.barrier("scalar", ["vector"])
                    s.op("sync", lambda h, g=g, hh=hh: h.dma_start(
                        out=o_st5[g // GQ][g % 8, (g % GQ) // 8, hh, :, :],
                        in_=obuf[:, hh * 128:hh * 128 + 128]), inc=True,
                        waits=[("vector", ob_d)])
                scan_done[g] = s.now("vector")
                s.barrier("vector", ["sync"])

            globals_pull = [None]

            def emit_collective(q):
                ns = s.now("sync")
                import os as _os
                if _os.environ.get("KERN_NO_CC"):
                    s.op("gpsimd", lambda h, q=q: h.dma_start(
                        out=o_gath[q].ap(), in_=o_stage[q].ap()).then_inc(
                            sems["cc"], 1), inc=False, waits=[("sync", ns)])
                else:
                    def do_cc(h, q=q):
                        with allow_ici_all_to_all():
                            return h.collective_compute(
                                "AllToAll", ALU.bypass,
                                replica_groups=[list(range(NCORE))],
                                ins=[o_stage[q].ap().opt()],
                                outs=[o_gath[q].ap().opt()]).then_inc(
                                    sems["cc"], 1)
                    s.op("gpsimd", do_cc, inc=False, waits=[("sync", ns)])

            def c_wave(q):
                t0q = q * TS
                for _ in range(12):
                    yield
                for src in range(NCORE):
                    for hh in range(2):
                        dti = src * 2 + hh
                        wv = [("cc", q + 1), ("tensor", s.now("tensor")),
                              ("pool", s.now("pool"))] \
                            if (src == 0 and hh == 0) else ()
                        s.op("sync", lambda h, src=src, hh=hh, dti=dti, q=q:
                             h.dma_start(
                                 out=oslab[:, dti * TS:dti * TS + TS].rearrange(
                                     "p (l c) -> p l c", l=LGQ),
                                 in_=o_gt5[q][src, :, hh].rearrange(
                                     "l p c -> p l c")), inc=True, waits=wv)
                s.op("gpsimd", lambda h, t0q=t0q: h.dma_start(
                    out=xinb[:].rearrange("p (d c) -> p d c", d=16),
                    in_=xres[:, :, t0q:t0q + TS]), inc=True)
                ldxi = s.now("gpsimd")
                lds = s.now("sync")
                yield

                stat_t = {}
                wo_ld = {}
                pe_done = {}
                for t in range(2):
                    s.op("gpsimd", lambda h, t=t: h.dma_start(
                        out=wrng[:, (t % 4) * 2048:(t % 4) * 2048 + 2048],
                        in_=wo[t]), inc=True)
                    wo_ld[t] = s.now("gpsimd")
                for t in range(16):
                    pq0 = 0
                    for d in range(16):
                        wv = [("gpsimd", wo_ld[t]), ("sync", lds)] \
                            if d == 0 else ()
                        if d == 0 and t >= 1:
                            wv = list(wv) + [("scalar", stat_t[t - 1])]
                        s.op("tensor", lambda h, t=t, d=d: h.matmul(
                            pC[:, 0:TS],
                            wrng[:, (t % 4) * 2048 + d * 128:
                                 (t % 4) * 2048 + d * 128 + 128],
                            oslab[:, d * TS:d * TS + TS],
                            start=(d == 0), stop=(d == 15)), waits=wv)
                    pe_t = s.now("tensor")
                    pe_done[t] = pe_t
                    if t + 2 < 16:
                        s.op("gpsimd", lambda h, t=t: h.dma_start(
                            out=wrng[:, ((t + 2) % 4) * 2048:
                                     ((t + 2) % 4) * 2048 + 2048],
                            in_=wo[t + 2]), inc=True, waits=[("tensor", pe_t)])
                        wo_ld[t + 2] = s.now("gpsimd")
                    s.op("scalar", lambda h, t=t: h.copy(
                        out=zdn[:, t * TS:t * TS + TS], in_=pC[:, 0:TS]),
                        waits=[("tensor", pe_t)])
                    s.op("scalar", lambda h, t=t: h.copy(
                        out=zb2[:, (t % 2) * TS:(t % 2) * TS + TS],
                        in_=pC[:, 0:TS]))
                    zev = s.op("scalar", lambda h, t=t: h.activation(
                        out=zq2[:, (t % 2) * TS:(t % 2) * TS + TS],
                        in_=zb2[:, (t % 2) * TS:(t % 2) * TS + TS],
                        func=AF.Square), inc=True)
                    s.op("tensor", lambda h, t=t: h.matmul(
                        pC[0:1, 256:256 + TS], c_onesb[:],
                        zb2[:, (t % 2) * TS:(t % 2) * TS + TS],
                        start=True, stop=True), waits=[("scalar", zev)])
                    s.op("tensor", lambda h, t=t: h.matmul(
                        pC[32:33, 256:256 + TS], c_onesb[:, 0:1],
                        zq2[:, (t % 2) * TS:(t % 2) * TS + TS],
                        start=True, stop=True))
                    smm = s.now("tensor")
                    s.op("scalar", lambda h: h.copy(
                        out=stmp[0:1, 0:TS], in_=pC[0:1, 256:256 + TS]),
                        waits=[("tensor", smm)])
                    sr = s.op("scalar", lambda h: h.copy(
                        out=stmp[0:1, TS:2 * TS], in_=pC[32:33, 256:256 + TS]),
                        inc=True)
                    if t == 0:
                        s.op("vector", lambda h: h.tensor_copy(
                            out=sac1[0:1, :], in_=stmp[0:1, 0:TS]),
                            inc=True, waits=[("scalar", sr)])
                        s.op("vector", lambda h: h.tensor_copy(
                            out=sac2[0:1, :], in_=stmp[0:1, TS:2 * TS]),
                            inc=True)
                    else:
                        s.op("vector", lambda h: h.tensor_tensor(
                            out=sac1[0:1, :], in0=sac1[0:1, :],
                            in1=stmp[0:1, 0:TS], op=ALU.add),
                            inc=True, waits=[("scalar", sr)])
                        s.op("vector", lambda h: h.tensor_tensor(
                            out=sac2[0:1, :], in0=sac2[0:1, :],
                            in1=stmp[0:1, TS:2 * TS], op=ALU.add), inc=True)
                    stat_t[t] = sr
                    yield

                def ln_rows_bcast(st):
                    pl = s.op("vector", lambda h: h.tensor_tensor(
                        out=stmp[0:1, 2 * TS:3 * TS], in0=sac1[0:1, :],
                        in1=sac1[0:1, :], op=ALU.mult), inc=True)
                    pl = s.op("vector", lambda h: h.scalar_tensor_tensor(
                        out=stmp[0:1, 2 * TS:3 * TS],
                        in0=stmp[0:1, 2 * TS:3 * TS], scalar=-1.0 / D,
                        in1=sac2[0:1, :], op0=ALU.mult, op1=ALU.add), inc=True)
                    s.op("scalar", lambda h: h.activation(
                        out=stmp[0:1, 3 * TS:4 * TS],
                        in_=stmp[0:1, 2 * TS:3 * TS], func=AF.Ln,
                        scale=1.0 / D, bias=c_epsM[0:1, :]),
                        waits=[("vector", pl)])
                    s.op("scalar", lambda h: h.activation(
                        out=stmp[0:1, 3 * TS:4 * TS],
                        in_=stmp[0:1, 3 * TS:4 * TS], func=AF.Exp, scale=-0.5))
                    mr = s.op("scalar", lambda h: h.activation(
                        out=stmp[0:1, 2 * TS:3 * TS], in_=sac1[0:1, :],
                        func=AF.Copy, scale=1.0 / D), inc=True)
                    b1 = s.op("tensor", lambda h: h.matmul(
                        pC[:, 256:256 + TS], c_onesrow[0:1, :],
                        stmp[0:1, 3 * TS:4 * TS], start=True, stop=True),
                        waits=[("scalar", mr)])
                    b1 = s.op("tensor", lambda h: h.matmul(
                        pC[:, 0:TS], c_onesrow[0:1, :],
                        stmp[0:1, 2 * TS:3 * TS], start=True, stop=True))
                    b1 = s.now("tensor")
                    s.op("scalar", lambda h, st=st: h.copy(
                        out=rB1[:, st * TS:st * TS + TS], in_=pC[:, 256:256 + TS]),
                        waits=[("tensor", b1), ("pool", s.now("pool"))])
                    bd = s.op("scalar", lambda h, st=st: h.copy(
                        out=csc1[:, st * TS:st * TS + TS], in_=pC[:, 0:TS]),
                        inc=True)
                    return bd

                bd = ln_rows_bcast(0)
                yield

                for t in range(16):
                    p1 = s.op("gpsimd", lambda h, t=t: h.tensor_tensor(
                        out=csc2[:, 0:TS], in0=zdn[:, t * TS:t * TS + TS],
                        in1=csc1[:, 0:TS], op=ALU.subtract),
                        inc=False, waits=[("scalar", bd)] if t == 0 else (), sem="pool")
                    s.op("gpsimd", lambda h: h.tensor_tensor(
                        out=csc2[:, 0:TS], in0=csc2[:, 0:TS], in1=rB1[:, 0:TS],
                        op=ALU.mult), sem="pool")
                    x1d = s.op("gpsimd", lambda h, t=t: h.tensor_tensor(
                        out=x1sl[:, t * TS:t * TS + TS], in0=csc2[:, 0:TS],
                        in1=xinb[:, t * TS:t * TS + TS], op=ALU.add), inc=True,
                        waits=[("gpsimd", ldxi)] if t == 0 else (), sem="pool")
                    zev = s.op("scalar", lambda h, t=t: h.activation(
                        out=zq2[:, 0:TS], in_=x1sl[:, t * TS:t * TS + TS],
                        func=AF.Square), inc=True, waits=[("pool", x1d)])
                    s.op("tensor", lambda h, t=t: h.matmul(
                        pC[0:1, 256:256 + TS], c_onesb[:],
                        x1sl[:, t * TS:t * TS + TS], start=True, stop=True),
                        waits=[("scalar", zev), ("scalar", s.now("scalar"))])
                    s.op("tensor", lambda h, t=t: h.matmul(
                        pC[32:33, 256:256 + TS], c_onesb[:, 0:1],
                        zq2[:, 0:TS], start=True, stop=True))
                    smm = s.now("tensor")
                    s.op("scalar", lambda h: h.copy(
                        out=stmp[0:1, 0:TS], in_=pC[0:1, 256:256 + TS]),
                        waits=[("tensor", smm)])
                    sr = s.op("scalar", lambda h: h.copy(
                        out=stmp[0:1, TS:2 * TS], in_=pC[32:33, 256:256 + TS]),
                        inc=True)
                    if dbg16 is not None and q == NQ - 1:
                        s.op("vector", lambda h, t=t: h.tensor_copy(
                            out=dbg16[0:1, t:t + 1],
                            in_=stmp[0:1, TS:TS + 1]),
                            inc=True, waits=[("scalar", sr)])
                        s.op("vector", lambda h, t=t: h.tensor_copy(
                            out=dbg16[0:1, 16 + t:16 + t + 1],
                            in_=zq2[0:1, 0:1]), inc=True)
                    op = ALU.add if t > 0 else ALU.bypass
                    if t == 0:
                        s.op("vector", lambda h: h.tensor_copy(
                            out=sac1[0:1, :], in_=stmp[0:1, 0:TS]),
                            inc=True, waits=[("scalar", sr)])
                        s.op("vector", lambda h: h.tensor_copy(
                            out=sac2[0:1, :], in_=stmp[0:1, TS:2 * TS]),
                            inc=True)
                    else:
                        s.op("vector", lambda h: h.tensor_tensor(
                            out=sac1[0:1, :], in0=sac1[0:1, :],
                            in1=stmp[0:1, 0:TS], op=ALU.add),
                            inc=True, waits=[("scalar", sr)])
                        s.op("vector", lambda h: h.tensor_tensor(
                            out=sac2[0:1, :], in0=sac2[0:1, :],
                            in1=stmp[0:1, TS:2 * TS], op=ALU.add), inc=True)
                    if t % 4 == 3:
                        yield

                bd = ln_rows_bcast(1)
                import os as _os2
                if _os2.environ.get("KERN_DBG") and q == NQ - 1:
                    s.op("sync", lambda h: h.dma_start(
                        out=dbg[0:1, 0:TS], in_=sac1[0:1, :]), inc=True,
                        waits=[("scalar", bd)])
                    s.op("sync", lambda h: h.dma_start(
                        out=dbg[0:1, 512:512 + TS], in_=sac2[0:1, :]), inc=True)
                    s.op("sync", lambda h: h.dma_start(
                        out=dbg[0:1, 1024:1024 + TS],
                        in_=rB1[0:1, TS:2 * TS]), inc=True)
                    s.op("sync", lambda h: h.dma_start(
                        out=dbg[0:1, 1536:1536 + TS],
                        in_=csc1[0:1, TS:2 * TS]), inc=True)
                    s.op("gpsimd", lambda h: h.dma_start(
                        out=dbg[0:1, 2048:2048 + TS], in_=zq2[0:1, 0:TS]),
                        inc=True)
                    s.op("gpsimd", lambda h: h.dma_start(
                        out=dbg[0:1, 2560:2560 + TS],
                        in_=x1sl[0:1, 15 * TS:15 * TS + TS]), inc=True)
                yield
                for t in range(16):
                    p1 = s.op("gpsimd", lambda h, t=t: h.tensor_tensor(
                        out=csc2[:, 0:TS], in0=x1sl[:, t * TS:t * TS + TS],
                        in1=csc1[:, TS:2 * TS], op=ALU.subtract),
                        inc=False, waits=[("scalar", bd)] if t == 0 else (), sem="pool")
                    s.op("gpsimd", lambda h, t=t: h.tensor_tensor(
                        out=x2sl[:, t * TS:t * TS + TS], in0=csc2[:, 0:TS],
                        in1=rB1[:, TS:2 * TS], op=ALU.mult), inc=(t == 15),
                        sem="pool")
                    if t % 8 == 7:
                        yield
                x2_d = s.now("pool")

                guv_done = {}
                for ht in range(48):
                    sl4 = ht % 2
                    wv = [("pool", x2_d)]
                    if ht >= 2:
                        wv.append(("tensor", guv_done[ht - 2]))
                    if q == 0:
                        s.op("gpsimd", lambda h, ht=ht, sl4=sl4: h.dma_start(
                            out=f8r[:, sl4 * 2048:sl4 * 2048 + 2048],
                            in_=gwt[ht]), inc=True, waits=wv)
                        w8 = s.now("gpsimd")
                        s.op("sync", lambda h, ht=ht, sl4=sl4: h.dma_start(
                            out=g8_dram.ap()[:, ht * 2048:ht * 2048 + 2048],
                            in_=f8r[:, sl4 * 2048:sl4 * 2048 + 2048]),
                            inc=True, waits=[("gpsimd", w8)])
                    else:
                        s.op("gpsimd", lambda h, ht=ht, sl4=sl4: h.dma_start(
                            out=f8r[:, sl4 * 2048:sl4 * 2048 + 2048],
                            in_=g8_dram.ap()[:, ht * 2048:ht * 2048 + 2048]),
                            inc=True, waits=wv)
                        w8 = s.now("gpsimd")
                    s.op("gpsimd", lambda h, ht=ht, sl4=sl4: h.dma_start(
                        out=wrng[:, sl4 * 2048:sl4 * 2048 + 2048],
                        in_=f8r[:, sl4 * 2048:sl4 * 2048 + 2048]),
                        inc=True, waits=[("gpsimd", w8)])
                    if q == 0:
                        s.op("gpsimd", lambda h, ht=ht, sl4=sl4: h.dma_start(
                            out=f8r[:, sl4 * 2048:sl4 * 2048 + 2048],
                            in_=uwt[ht]), inc=True,
                            waits=[("gpsimd", s.now("gpsimd"))])
                        w8u = s.now("gpsimd")
                        s.op("sync", lambda h, ht=ht, sl4=sl4: h.dma_start(
                            out=u8_dram.ap()[:, ht * 2048:ht * 2048 + 2048],
                            in_=f8r[:, sl4 * 2048:sl4 * 2048 + 2048]),
                            inc=True, waits=[("gpsimd", w8u)])
                    else:
                        s.op("gpsimd", lambda h, ht=ht, sl4=sl4: h.dma_start(
                            out=f8r[:, sl4 * 2048:sl4 * 2048 + 2048],
                            in_=u8_dram.ap()[:, ht * 2048:ht * 2048 + 2048]),
                            inc=True, waits=[("gpsimd", s.now("gpsimd"))])
                        w8u = s.now("gpsimd")
                    s.op("gpsimd", lambda h, ht=ht, sl4=sl4: h.dma_start(
                        out=wrng[:, (2 + sl4) * 2048:(2 + sl4) * 2048 + 2048],
                        in_=f8r[:, sl4 * 2048:sl4 * 2048 + 2048]),
                        inc=True, waits=[("gpsimd", w8u)])
                    wgu = s.now("gpsimd")
                    yield

                    for d in range(16):
                        wv = [("gpsimd", wgu)] if d == 0 else ()
                        s.op("tensor", lambda h, d=d, sl4=sl4: h.matmul(
                            pC[:, 0:TS],
                            wrng[:, sl4 * 2048 + d * 128:
                                 sl4 * 2048 + d * 128 + 128],
                            x2sl[:, d * TS:d * TS + TS],
                            start=(d == 0), stop=(d == 15)), waits=wv)
                    for d in range(16):
                        s.op("tensor", lambda h, d=d, sl4=sl4: h.matmul(
                            pC[:, 256:256 + TS],
                            wrng[:, (2 + sl4) * 2048 + d * 128:
                                 (2 + sl4) * 2048 + d * 128 + 128],
                            x2sl[:, d * TS:d * TS + TS],
                            start=(d == 0), stop=(d == 15)))
                    gu_pe = s.now("tensor")
                    guv_done[ht] = gu_pe
                    s.op("scalar", lambda h, ht=ht: h.activation(
                        out=gth[:], in_=pC[:, 0:TS], func=AF.Exp,
                        scale=ngs[:, ht:ht + 1]), waits=[("tensor", gu_pe)])
                    s.op("scalar", lambda h: h.activation(
                        out=gth[:], in_=gth[:], func=AF.Ln, scale=1.0,
                        bias=c_onesc[:]))
                    s.op("scalar", lambda h: h.activation(
                        out=gth[:], in_=gth[:], func=AF.Exp, scale=-1.0))
                    s.op("scalar", lambda h, ht=ht: h.activation(
                        out=gxh[:], in_=pC[:, 0:TS], func=AF.Copy,
                        scale=gscb[:, ht:ht + 1]))
                    guc = s.op("scalar", lambda h, ht=ht: h.activation(
                        out=guv[:], in_=pC[:, 256:256 + TS], func=AF.Copy,
                        scale=uscb[:, ht:ht + 1]), inc=True)
                    s.op("gpsimd", lambda h: h.tensor_tensor(
                        out=gvv[:], in0=gth[:], in1=gxh[:], op=ALU.mult),
                        inc=False, waits=[("scalar", guc)] + (
                            [("tensor", s.now("tensor"))] if ht == 0 else []),
                        sem="pool")
                    s.op("gpsimd", lambda h, ht=ht: h.tensor_tensor(
                        out=fusl[:, ht * TS:ht * TS + TS], in0=gvv[:],
                        in1=guv[:], op=ALU.mult), inc=True, sem="pool")
                    s.barrier("tensor", ["scalar"])
                    yield
                fu_d = s.now("pool")

                dn_done = {}
                for t in range(16):
                    sl2 = t % 2
                    wv = [("pool", fu_d)]
                    if t >= 1:
                        wv.append(("tensor", dn_done[t - 1]))
                    if q == 0:
                        s.op("gpsimd", lambda h, t=t: h.dma_start(
                            out=f8d[:], in_=dwt[t]), inc=True, waits=wv)
                        d8 = s.now("gpsimd")
                        s.op("sync", lambda h, t=t: h.dma_start(
                            out=d8_dram.ap()[:, t * 6144:t * 6144 + 6144],
                            in_=f8d[:]), inc=True, waits=[("gpsimd", d8)])
                    else:
                        s.op("gpsimd", lambda h, t=t: h.dma_start(
                            out=f8d[:],
                            in_=d8_dram.ap()[:, t * 6144:t * 6144 + 6144]),
                            inc=True, waits=wv)
                        d8 = s.now("gpsimd")
                    s.op("gpsimd", lambda h: h.dma_start(
                        out=drng[:, 0:6144], in_=f8d[:]),
                        inc=True, waits=[("gpsimd", d8),
                                         ("tensor", dn_done.get(t - 1, 0))])
                    dw_d = s.now("gpsimd")
                    yield
                    for half in range(2):
                        for dd in range(24):
                            ddv = half * 24 + dd
                            wv = [("gpsimd", dw_d)] \
                                if (half == 0 and dd == 0) else ()
                            s.op("tensor", lambda h, ddv=ddv: h.matmul(
                                pC[:, 0:TS],
                                drng[:, ddv * 128:ddv * 128 + 128],
                                fusl[:, ddv * TS:ddv * TS + TS],
                                start=(ddv == 0), stop=(ddv == 47)), waits=wv)
                        yield
                    dn_pe = s.now("tensor")
                    dn_done[t] = dn_pe
                    s.op("scalar", lambda h, t=t: h.activation(
                        out=zdn[:, t * TS:t * TS + TS], in_=pC[:, 0:TS],
                        func=AF.Copy, scale=dscb[:, t:t + 1]),
                        waits=[("tensor", dn_pe)])
                    s.op("scalar", lambda h, t=t: h.copy(
                        out=zb2[:, (t % 2) * TS:(t % 2) * TS + TS],
                        in_=zdn[:, t * TS:t * TS + TS]))
                    zev = s.op("scalar", lambda h, t=t: h.activation(
                        out=zq2[:, (t % 2) * TS:(t % 2) * TS + TS],
                        in_=zb2[:, (t % 2) * TS:(t % 2) * TS + TS],
                        func=AF.Square), inc=True)
                    s.op("tensor", lambda h, t=t: h.matmul(
                        pC[0:1, 256:256 + TS], c_onesb[:],
                        zb2[:, (t % 2) * TS:(t % 2) * TS + TS],
                        start=True, stop=True),
                        waits=[("scalar", zev), ("scalar", s.now("scalar"))])
                    s.op("tensor", lambda h, t=t: h.matmul(
                        pC[32:33, 256:256 + TS], c_onesb[:, 0:1],
                        zq2[:, (t % 2) * TS:(t % 2) * TS + TS],
                        start=True, stop=True))
                    smm = s.now("tensor")
                    s.op("scalar", lambda h: h.copy(
                        out=stmp[0:1, 0:TS], in_=pC[0:1, 256:256 + TS]),
                        waits=[("tensor", smm)])
                    sr = s.op("scalar", lambda h: h.copy(
                        out=stmp[0:1, TS:2 * TS], in_=pC[32:33, 256:256 + TS]),
                        inc=True)
                    if t == 0:
                        s.op("vector", lambda h: h.tensor_copy(
                            out=sac1[0:1, :], in_=stmp[0:1, 0:TS]),
                            inc=True, waits=[("scalar", sr)])
                        s.op("vector", lambda h: h.tensor_copy(
                            out=sac2[0:1, :], in_=stmp[0:1, TS:2 * TS]),
                            inc=True)
                    else:
                        s.op("vector", lambda h: h.tensor_tensor(
                            out=sac1[0:1, :], in0=sac1[0:1, :],
                            in1=stmp[0:1, 0:TS], op=ALU.add),
                            inc=True, waits=[("scalar", sr)])
                        s.op("vector", lambda h: h.tensor_tensor(
                            out=sac2[0:1, :], in0=sac2[0:1, :],
                            in1=stmp[0:1, TS:2 * TS], op=ALU.add), inc=True)
                    yield

                bd = ln_rows_bcast(2)
                yield
                trprev = {}
                for t in range(16):
                    cs0 = (t % 2) * TS
                    wv0 = [("scalar", bd)] if t == 0 else []
                    if t >= 2:
                        wv0.append(("tensor", trprev[t - 2]))
                    s.op("gpsimd", lambda h, t=t, cs0=cs0: h.tensor_tensor(
                        out=csc2[:, cs0:cs0 + TS], in0=zdn[:, t * TS:t * TS + TS],
                        in1=csc1[:, 2 * TS:3 * TS], op=ALU.subtract),
                        inc=False, waits=wv0, sem="pool")
                    s.op("gpsimd", lambda h, cs0=cs0: h.tensor_tensor(
                        out=csc2[:, cs0:cs0 + TS], in0=csc2[:, cs0:cs0 + TS],
                        in1=rB1[:, 2 * TS:3 * TS], op=ALU.mult), sem="pool")
                    ov = s.op("gpsimd", lambda h, t=t, cs0=cs0: h.tensor_tensor(
                        out=csc2[:, cs0:cs0 + TS], in0=csc2[:, cs0:cs0 + TS],
                        in1=x1sl[:, t * TS:t * TS + TS], op=ALU.add), inc=True, sem="pool")
                    for iq in range(TS // 128):
                        trq = s.op("tensor", lambda h, iq=iq, cs0=cs0: h.transpose(
                            pC.ap().bitcast(F32)[:, 0:128],
                            csc2[:, cs0 + iq * 128:cs0 + iq * 128 + 128],
                            c_idf[:]), inc=False,
                            waits=[("pool", ov), ("sync", outd.get(iq ^ 1, 0))]
                            if iq == 0 else [("sync", outd.get(iq ^ 1, 0))])
                        trq = s.now("tensor")
                        otd = s.op("scalar", lambda h, iq=iq: h.copy(
                            out=outp[:, iq * 128:iq * 128 + 128],
                            in_=pC.ap().bitcast(F32)[:, 0:128]), inc=True,
                            waits=[("tensor", trq)])
                        s.barrier("tensor", ["scalar"])
                        s.op("sync", lambda h, t=t, iq=iq, t0q=t0q: h.dma_start(
                            out=out[t0q + iq * 128: t0q + iq * 128 + 128,
                                    t * 128:t * 128 + 128],
                            in_=outp[:, iq * 128:iq * 128 + 128]), inc=True,
                            waits=[("scalar", otd)])
                        outd[iq] = s.now("sync")
                    trprev[t] = s.now("tensor")
                    if t % 4 == 3:
                        yield

            outd = {}
            import collections as _c
            pending = _c.deque()

            def pull(k):
                done = 0
                while pending and done < k:
                    try:
                        next(pending[0])
                        done += 1
                    except StopIteration:
                        pending.popleft()

            globals_pull[0] = pull

            for g in range(NG + 1):
                if g >= 1:
                    scan_group(g - 1)
                if g >= 1 and g % GQ == 0:
                    q = g // GQ - 1
                    emit_collective(q)
                    pending.append(c_wave(q))
                if g < NG:
                    qkv_group(g)
            while pending:
                pull(1000)
            s.barrier("gpsimd", ["sync", "vector", "scalar", "tensor"])

        s.emit(block, sems)
    return nc


def stage_inputs(inputs, T):
    Tc = T // NCORE
    x = np.ascontiguousarray(inputs["L0_attn_x"][0, :T].T.astype(np.float32))
    xT_s = np.ascontiguousarray(x.reshape(16, 128, T).transpose(1, 0, 2))

    Wq, Wk, Wv = (inputs["L0_attn_Wq"], inputs["L0_attn_Wk"], inputs["L0_attn_Wv"])
    Wbeta = inputs["L0_attn_Wbeta"]
    Wg1 = inputs["L0_attn_Wg1"]
    Wo_ = inputs["L0_attn_Wo"]
    Wg2_ = inputs["L0_attn_Wg2"]
    lnw, lnb = inputs["L0_attn_ln_w"], inputs["L0_attn_ln_b"]
    nw = inputs["L0_attn_norm_w"]
    gw = inputs["L0_ffn_gate_w"]
    uw = inputs["L0_ffn_up_w"]
    dw = inputs["L0_ffn_down_w"]

    wo_s = np.ascontiguousarray(
        Wo_.reshape(16, 128, 16, 128).transpose(2, 1, 0, 3)).reshape(16, 128, -1)
    gwt_s = np.ascontiguousarray(
        gw.reshape(16, 128, 48, 128).transpose(2, 1, 0, 3)).reshape(48, 128, -1)
    uwt_s = np.ascontiguousarray(
        uw.reshape(16, 128, 48, 128).transpose(2, 1, 0, 3)).reshape(48, 128, -1)
    dwt_s = np.ascontiguousarray(
        dw.reshape(48, 128, 16, 128).transpose(2, 1, 0, 3)).reshape(16, 128, -1)
    gsc_s = np.ascontiguousarray(inputs["L0_ffn_gate_scale"].reshape(48, 128).T)
    usc_s = np.ascontiguousarray(inputs["L0_ffn_up_scale"].reshape(48, 128).T)
    dsc_s = np.ascontiguousarray(inputs["L0_ffn_down_scale"].reshape(16, 128).T)
    sidx = np.arange(128)
    mkK_s = ((sidx[:, None] // CH) < (sidx[None, :] // CH)).astype(np.float32)
    mkG_s = (sidx[:, None] <= sidx[None, :]).astype(np.float32)
    mkKL_s = np.ascontiguousarray(mkK_s[0:64]); mkKH_s = np.ascontiguousarray(mkK_s[64:128])
    mkGL_s = np.ascontiguousarray(mkG_s[0:64]); mkGH_s = np.ascontiguousarray(mkG_s[64:128])

    per_core = []
    for c in range(NCORE):
        h0, h1 = 2 * c, 2 * c + 1
        cols = np.zeros((D, 1024), np.float32)
        cols[:, 0:128] = Wq[:, h0 * 128:(h0 + 1) * 128]
        cols[:, 128:256] = Wq[:, h1 * 128:(h1 + 1) * 128]
        cols[:, 256:384] = Wk[:, h0 * 128:(h0 + 1) * 128]
        cols[:, 384:512] = Wk[:, h1 * 128:(h1 + 1) * 128]
        cols[:, 512:640] = Wv[:, h0 * 128:(h0 + 1) * 128]
        cols[:, 640:768] = Wv[:, h1 * 128:(h1 + 1) * 128]
        cols[:, 768:896] = Wg1
        cols[:, 896] = 1.0
        cols[:, 896 + 32] = Wbeta[:, h0]
        cols[:, 896 + 64] = Wbeta[:, h1]
        wf_s = np.ascontiguousarray(
            cols.reshape(16, 128, 8, 128).transpose(1, 0, 2, 3)).reshape(128, -1)
        wg2_s = np.ascontiguousarray(np.concatenate(
            [Wg2_[:, h0 * 128:(h0 + 1) * 128], Wg2_[:, h1 * 128:(h1 + 1) * 128]],
            axis=1))
        lnw_cs = np.ascontiguousarray(
            np.stack([lnw[h0], lnw[h1]], axis=1).astype(np.float32))
        lnb_cs = np.ascontiguousarray(
            np.stack([lnb[h0], lnb[h1]], axis=1).astype(np.float32))
        nw_cs = np.ascontiguousarray(np.stack(
            [nw[h0 * 128:(h0 + 1) * 128], nw[h1 * 128:(h1 + 1) * 128]],
            axis=1).astype(np.float32))
        NLGs = Tc // 128
        xres_s = np.ascontiguousarray(np.concatenate(
            [xT_s[:, :, (8 * lg + c) * 128:(8 * lg + c) * 128 + 128]
             for lg in range(NLGs)], axis=2))

        per_core.append({
            "xT": xT_s, "wf": wf_s, "wo": wo_s, "wg2": wg2_s,
            "lnw_c": lnw_cs, "lnb_c": lnb_cs, "nw_c": nw_cs,
            "gwt": gwt_s, "uwt": uwt_s, "dwt": dwt_s,
            "gsc": gsc_s, "usc": usc_s, "dsc": dsc_s,
            "idn": np.eye(128, dtype=BF),
            "mkKL": mkKL_s, "mkKH": mkKH_s, "mkGL": mkGL_s, "mkGH": mkGH_s,
            "onesc": np.ones((128, 1), np.float32),
            "onesf": np.ones((128, 128), np.float32),
            "onesrow": np.ones((1, 128), np.float32),
            "xres": xres_s,
        })
    return per_core


def kernel(**inputs):
    T = inputs["L0_attn_x"].shape[1]
    nc = build_program(T)
    in_maps = stage_inputs(inputs, T)
    res = run_bass_kernel_spmd(nc, in_maps, core_ids=list(range(NCORE)))
    outs = [np.asarray(res.results[c]["out"]) for c in range(NCORE)]
    Tc = T // NCORE
    full = np.empty((T, D), np.float32)
    for c in range(NCORE):
        for lg in range(Tc // 128):
            g = 8 * lg + c
            full[g * 128:(g + 1) * 128] = outs[c][lg * 128:(lg + 1) * 128]
    return full[None].astype(np.float32)


# version 08
# speedup vs baseline: 1.0799x; 1.0799x over previous
import os as _os
import numpy as np
import ml_dtypes
import concourse.bass as bass
import concourse.mybir as mybir
from concourse.bass_utils import run_bass_kernel_spmd
from concourse.aot_env import allow_ici_all_to_all

F32 = mybir.dt.float32
BF16 = mybir.dt.bfloat16
F8 = mybir.dt.float8e4
AF = mybir.ActivationFunctionType
ALU = mybir.AluOpType
BF = ml_dtypes.bfloat16

D = 2048
DH = 128
CH = 16
GT = 128
NCORE = 8
EPS_A = 1e-6
EPS_M = 1e-5

ENGS = ("sync", "gpsimd", "vector", "scalar", "tensor")


COMPUTE_ENGS = ("vector", "scalar", "tensor")


class Sched:

    def __init__(self):
        self.ops = {e: [] for e in ENGS}
        self.cnt = {e: 0 for e in ENGS}
        self.cnt["pool"] = 0
        self.step = {"sync": 16, "gpsimd": 16, "vector": 1, "scalar": 1,
                     "tensor": 1, "pool": 1}

    def op(self, eng, fn, inc=False, waits=(), sem=None):
        ch = sem or eng
        self.ops[eng].append((fn, bool(inc), tuple(waits), ch))
        if ch == "pool" or eng in COMPUTE_ENGS or inc:
            self.cnt[ch] += self.step[ch]
        return self.cnt[ch]

    def now(self, eng):
        return self.cnt[eng]

    def barrier(self, eng, others):
        w = tuple((o, self.cnt[o]) for o in others if self.cnt[o] > 0)
        if w:
            self.ops[eng].append((None, False, w, eng))

    def emit(self, block, sems):
        import os as _os
        import bisect
        ssv = _os.environ.get("BASS_SELFSYNC", "vector")
        full = ssv == "full"
        full_engs = set(ssv.split(",")) if ssv not in ("min", "full") else (
            set(COMPUTE_ENGS) if full else set())
        waited = {e: set() for e in ENGS}
        waited["pool"] = set()
        for e in list(self.ops):
            for fn, inc, waits, ch in self.ops[e]:
                for we, wv in waits:
                    if wv > 0 and we in waited:
                        waited[we].add(wv)
        flagged = {e: sorted(waited[e]) for e in COMPUTE_ENGS}

        class _TProxy:
            def __init__(self, h):
                self._h = h

            def __getattr__(self, k):
                return getattr(self._h, k)

            def matmul(self, *a, **kw):
                kw.setdefault("skip_group_check", True)
                return self._h.matmul(*a, **kw)

        def rv(we, wv):
            if we not in COMPUTE_ENGS or we in full_engs:
                return wv
            import bisect as _b
            return _b.bisect_right(flagged[we], wv)

        def mk(eng):
            selfsync = eng in COMPUTE_ENGS
            efull = eng in full_engs
            miles = sorted(waited[eng]) if not selfsync else []
            fset = set(flagged.get(eng, ()))

            def body(h):
                n = 0
                npool = 0
                enforced = 0
                mi = 0
                for fn, inc, waits, ch in self.ops[eng]:
                    for we, wv in waits:
                        wv2 = rv(we, wv)
                        if wv2 > 0:
                            h.wait_ge(sems[we], wv2)
                    if fn is not None:
                        if eng == "tensor":
                            h2 = _TProxy(h)
                        else:
                            h2 = h
                        if ch == "pool":
                            ins = fn(h2)
                            npool += 1
                            ins.then_inc(sems["pool"], 1)
                            continue
                        if selfsync:
                            if efull and n > 0:
                                h.wait_ge(sems[eng], n)
                        elif inc:
                            tgt = 0
                            while mi < len(miles) and miles[mi] <= n:
                                tgt = miles[mi]
                                mi += 1
                            if tgt > enforced:
                                h.wait_ge(sems[eng], tgt)
                                enforced = tgt
                        ins = fn(h2)
                        if selfsync:
                            n += 1
                            if efull:
                                ins.then_inc(sems[eng], 1)
                            elif n in fset:
                                ins.then_inc(sems[eng], 1)
                        elif inc:
                            ins.then_inc(sems[eng], self.step[eng])
                            n += self.step[eng]
            return body
        block.sync(mk("sync"))
        block.gpsimd(mk("gpsimd"))
        block.vector(mk("vector"))
        block.scalar(mk("scalar"))
        block.tensor(mk("tensor"))



def blk2(ap0, c0, w, blk):
    a = ap0[:, c0:c0 + w]
    return bass.AP(tensor=a.tensor, offset=a.offset,
                   ap=[list(a.ap[0]), [blk, 2], [1, w]])

def build_program(T):
    NG = T // GT
    Tc = T // NCORE
    NLG = Tc // GT
    NCH = GT // CH
    LGQ = 2 if NLG >= 2 else 1
    NQ = NLG // LGQ
    GQ = NCORE * LGQ
    TS = LGQ * GT
    assert T % (GT * NCORE) == 0 and NLG % LGQ == 0

    import os as _os
    nc = bass.Bass("TRN2", target_bir_lowering=False,
                   detect_race_conditions=not _os.environ.get("KERN_NORACE"))
    s = Sched()

    def din(name, shape, dt):
        return nc.dram_tensor(name, shape, dt, kind="ExternalInput").ap()

    xT = din("xT", [128, 16, T], F32)
    wf = din("wf", [128, 16 * 8 * 128], F32)
    wo = din("wo", [16, 128, 16 * 128], F32)
    wg2 = din("wg2", [128, 256], F32)
    lnw_c = din("lnw_c", [128, 2], F32)
    lnb_c = din("lnb_c", [128, 2], F32)
    nw_c = din("nw_c", [128, 2], F32)
    gwt = din("gwt", [48, 128, 16 * 128], F32)
    uwt = din("uwt", [48, 128, 16 * 128], F32)
    dwt = din("dwt", [16, 128, 48 * 128], F32)
    gsc = din("gsc", [128, 48], F32)
    usc = din("usc", [128, 48], F32)
    dsc = din("dsc", [128, 16], F32)
    idn = din("idn", [128, 128], BF16)
    mkKL = din("mkKL", [64, 128], F32)
    mkKH = din("mkKH", [64, 128], F32)
    mkGL = din("mkGL", [64, 128], F32)
    mkGH = din("mkGH", [64, 128], F32)
    onesc = din("onesc", [128, 1], F32)
    onesf = din("onesf", [128, 128], F32)
    onesrow = din("onesrow", [1, 128], F32)
    xres = din("xres", [128, 16, Tc], F32)

    out = nc.dram_tensor("out", [Tc, D], F32, kind="ExternalOutput").ap()
    o_stage = [nc.dram_tensor("o_stage%d" % q, [NCORE * LGQ * 2 * 128, GT], BF16)
               for q in range(NQ)]
    o_gath = [nc.dram_tensor("o_gath%d" % q, [NCORE * LGQ * 2 * 128, GT], BF16)
              for q in range(NQ)]
    o_st5 = [t.ap().rearrange("(a l hh p) c -> a l hh p c",
                              a=NCORE, l=LGQ, hh=2, p=128) for t in o_stage]
    o_gt5 = [t.ap().rearrange("(a l hh p) c -> a l hh p c",
                              a=NCORE, l=LGQ, hh=2, p=128) for t in o_gath]
    dbg = nc.dram_tensor("dbg", [1, 8 * 512], F32, kind="ExternalOutput").ap() \
        if _os.environ.get("KERN_DBG") else None
    g8_dram = nc.dram_tensor("g8_dram", [128, 48 * 16 * 128], F8)
    u8_dram = nc.dram_tensor("u8_dram", [128, 48 * 16 * 128], F8)
    d8_dram = nc.dram_tensor("d8_dram", [128, 16 * 48 * 128], F8)
    wobf_dram = nc.dram_tensor("wobf_dram", [16, 128, 16 * 128], BF16)

    import contextlib
    es = contextlib.ExitStack()
    with es:
        def sb(name, shape, dt):
            return es.enter_context(nc.sbuf_tensor(name, shape, dt))

        pbank = [es.enter_context(nc.psum_tensor("pb%d" % i, [128, 512], F32))
                 for i in range(8)]
        pqA, pqB, pA, pB, pS, pC, pQM, pSM = pbank
        pSC = pSM
        pSMb = pSM.ap().bitcast(BF16)

        sems = {e: es.enter_context(nc.semaphore("sem_" + e)) for e in ENGS}
        sems["cc"] = es.enter_context(nc.semaphore("sem_cc"))
        sems["pool"] = es.enter_context(nc.semaphore("sem_pool"))
        block = es.enter_context(nc.Block())

        c_id = sb("c_id", [128, 128], BF16)
        c_idf = sb("c_idf", [128, 128], F32)
        c_mkKL = sb("c_mkKL", [64, 128], F32)
        c_mkKH = sb("c_mkKH", [64, 128], F32)
        c_mkGL = sb("c_mkGL", [64, 128], F32)
        c_mkGH = sb("c_mkGH", [64, 128], F32)
        c_ones = sb("c_ones", [128, 128], F32)
        c_onesc = sb("c_onesc", [128, 1], F32)
        c_onesb = sb("c_onesb", [128, 1], BF16)
        c_onesrow = sb("c_onesrow", [1, 128], F32)
        c_lnw = sb("c_lnw", [128, 2], F32)
        c_lnb = sb("c_lnb", [128, 2], F32)
        c_nw = sb("c_nw", [128, 2], F32)
        c_w2 = sb("c_w2", [128, 2], F32)
        c_wb = sb("c_wb", [128, 2], F32)
        c_negw = sb("c_negw", [128, 2], F32)
        c_w2B = sb("c_w2B", [128, 32], F32)
        c_epsA = sb("c_epsA", [128, 1], F32)
        c_epsM = sb("c_epsM", [128, 1], F32)

        for dst, src in [(c_id, idn), (c_mkKL, mkKL), (c_mkKH, mkKH),
                         (c_mkGL, mkGL), (c_mkGH, mkGH), (c_ones, onesf),
                         (c_onesc, onesc), (c_onesrow, onesrow),
                         (c_lnw, lnw_c), (c_lnb, lnb_c), (c_nw, nw_c)]:
            s.op("sync", lambda h, d=dst, sr=src: h.dma_start(out=d[:], in_=sr),
                 inc=True)
        ld0 = s.now("sync")

        oslab = sb("oslab", [128, 16 * TS], BF16)
        xinb = sb("xinb", [128, 16 * TS], BF16)
        zdn = sb("zdn", [128, 16 * TS], F32)
        x1sl = sb("x1sl", [128, 16 * TS], BF16)
        x2sl = sb("x2sl", [128, 16 * TS], BF16)
        fusl = sb("fusl", [128, 48 * TS], BF16)
        wrng = sb("wrng", [128, 4 * 2048], BF16)
        drng = sb("drng", [128, 6144], BF16)
        zb2 = sb("zb2", [128, 2 * TS], BF16)
        zq2 = sb("zq2", [128, 2 * TS], BF16)
        gth = sb("gth", [128, TS], F32)
        gxh = sb("gxh", [128, TS], F32)
        guv = sb("guv", [128, TS], F32)
        gvv = sb("gvv", [128, TS], F32)
        sac1 = sb("sac1", [1, TS], F32)
        sac2 = sb("sac2", [1, TS], F32)
        stmp = sb("stmp", [1, 4 * TS], F32)
        rB1 = sb("rB1", [128, 3 * TS], F32)
        csc1 = sb("csc1", [128, 3 * TS], F32)
        csc2 = sb("csc2", [128, 2 * TS], F32)
        outp = sb("outp", [128, 2 * 128], F32)
        gscb = sb("gscb", [128, 48], F32)
        uscb = sb("uscb", [128, 48], F32)
        dscb = sb("dscb", [128, 16], F32)
        ngs = sb("ngs", [128, 48], F32)
        dbg16 = sb("dbg16", [1, 64], F32) if _os.environ.get("KERN_DBG") \
            else None

        s.op("sync", lambda h: h.dma_start(out=gscb[:], in_=gsc), inc=True)
        s.op("sync", lambda h: h.dma_start(out=uscb[:], in_=usc), inc=True)
        s.op("sync", lambda h: h.dma_start(out=dscb[:], in_=dsc), inc=True)
        ldsc = s.now("sync")
        s.op("vector", lambda h: h.tensor_scalar(
            out=ngs[:], in0=gscb[:], scalar1=-1.0, scalar2=None, op0=ALU.mult),
            inc=True, waits=[("sync", ldsc)])

        scan_done = {}
        qkv_evac_done = {}
        qkv_pe_done = {}
        for g in range(-4, 0):
            scan_done[g] = 0
            qkv_evac_done[g] = 0
            qkv_pe_done[g] = 0

        pf_jobs = []
        for t in range(16):
            pf_jobs.append(("wo", t, lambda h, t=t: h.dma_start(
                out=wobf_dram.ap()[t], in_=wo[t])))
        for ht in range(48):
            pf_jobs.append(("g", ht, lambda h, ht=ht: h.dma_start(
                out=g8_dram.ap()[:, ht * 2048:ht * 2048 + 2048], in_=gwt[ht])))
            pf_jobs.append(("u", ht, lambda h, ht=ht: h.dma_start(
                out=u8_dram.ap()[:, ht * 2048:ht * 2048 + 2048], in_=uwt[ht])))
        for t in range(16):
            pf_jobs.append(("d", t, lambda h, t=t: h.dma_start(
                out=d8_dram.ap()[:, t * 6144:t * 6144 + 6144], in_=dwt[t])))
        pf_done = {}
        pf_idx = [0]

        def pf_pump(k):
            while pf_idx[0] < len(pf_jobs) and k > 0:
                kind, idx, fn = pf_jobs[pf_idx[0]]
                s.op("gpsimd", fn, inc=True)
                pf_done[(kind, idx)] = s.now("gpsimd")
                pf_idx[0] += 1
                k -= 1

        def pf_wait(kind, idx):
            while (kind, idx) not in pf_done:
                pf_pump(1)
            return ("gpsimd", pf_done[(kind, idx)])

        esA = contextlib.ExitStack()
        with esA:
            def sba(name, shape, dt):
                return esA.enter_context(nc.sbuf_tensor(name, shape, dt))

            wfb = sba("wfb", [128, 16 * 8 * 128], BF16)
            wg2b = sba("wg2b", [128, 256], BF16)
            csWb = sba("csWb", [1, 8 * 128], BF16)
            W = 2
            xb = sba("xb", [128, W * 2048], BF16)
            sqb = sba("sqb", [128, 2048], BF16)
            qs_f = sba("qs_f", [128, W * 256], F32)
            qs_b = sba("qs_b", [128, W * 256], BF16)
            kn_f = sba("kn_f", [128, W * 256], F32)
            kn_b = sba("kn_b", [128, W * 256], BF16)
            v_f = sba("v_f", [128, W * 256], F32)
            g1_b = sba("g1_b", [128, W * 128], BF16)
            eT = sba("eT", [128, W * 4], F32)
            rowbuf = sba("rowbuf", [128, W * 128], F32)
            s_op = sba("s_op", [128, 256], BF16)
            vnstkL = sba("vnstkL", [64, 256], BF16)
            vnstkH = sba("vnstkH", [64, 256], BF16)
            aktL = sba("aktL", [64, 256], BF16)
            aktH = sba("aktH", [64, 256], BF16)
            agtL = sba("agtL", [64, 256], BF16)
            agtH = sba("agtH", [64, 256], BF16)
            vnstg = sba("vnstg", [128, 64], BF16)
            wbm = sba("wbm", [128, 256], F32)
            ketL = sba("ketL", [64, 256], BF16)
            ketH = sba("ketH", [64, 256], BF16)
            kescr = sba("kescr", [128, 128], BF16)
            scrA = sba("scrA", [128, 128], F32)
            scr1 = sba("scr1", [128, 32], F32)
            scr2 = sba("scr2", [128, 32], F32)
            rstdc = sba("rstdc", [128, 32], F32)
            khh = sba("khh", [128, 32], F32)
            strow = sba("strow", [1, 6 * 128], F32)
            strowb = sba("strowb", [1, 128], BF16)
            rstdB = sba("rstdB", [128, 128], F32)
            qscr1 = sba("qscr1", [128, 128], F32)
            qscr2 = sba("qscr2", [128, 128], F32)
            oscr = sba("oscr", [128, 256], F32)
            oth = sba("oth", [128, 128], F32)
            osig = sba("osig", [128, 128], F32)
            oo = sba("oo", [128, 128], F32)
            oo2 = sba("oo2", [128, 128], F32)
            obuf = sba("obuf", [128, 256], BF16)

            s.op("gpsimd", lambda h: h.dma_start(out=wfb[:], in_=wf), inc=True)
            s.op("gpsimd", lambda h: h.dma_start(out=wg2b[:], in_=wg2), inc=True)
            ld1 = s.now("gpsimd")

            s.op("vector", lambda h: h.tensor_tensor(
                out=c_w2[:], in0=c_lnw[:], in1=c_lnw[:], op=ALU.mult),
                waits=[("sync", ld0)])
            s.op("vector", lambda h: h.tensor_tensor(
                out=c_wb[:], in0=c_lnw[:], in1=c_lnb[:], op=ALU.mult))
            s.op("vector", lambda h: h.tensor_scalar(
                out=c_negw[:], in0=c_lnw[:], scalar1=-1.0, scalar2=None,
                op0=ALU.mult))
            s.op("vector", lambda h: h.tensor_copy(out=c_idf[:], in_=c_id[:]))
            s.op("vector", lambda h: h.tensor_copy(out=c_onesb[:], in_=c_onesc[:]))
            for hh in range(2):
                s.op("vector", lambda h, hh=hh: h.tensor_scalar(
                    out=c_w2B[:, hh * 16:hh * 16 + 16], in0=c_ones[:, 0:16],
                    scalar1=c_w2[:, hh:hh + 1], scalar2=None, op0=ALU.mult))
            s.op("vector", lambda h: h.memset(s_op[:], 0.0))
            s.op("vector", lambda h: h.memset(rowbuf[:], 0.0))
            s.op("vector", lambda h: h.memset(c_epsA[:], EPS_A))
            s.op("vector", lambda h: h.memset(c_epsM[:], EPS_M), inc=True)
            pre0 = s.now("vector")

            for t in range(8):
                for d in range(16):
                    wv = [("gpsimd", ld1), ("vector", pre0)] \
                        if (t == 0 and d == 0) else ()
                    s.op("tensor", lambda h, t=t, d=d: h.matmul(
                        pQM[0:1, 0:128], c_onesb[:],
                        wfb[:, (d * 8 + t) * 128:(d * 8 + t) * 128 + 128],
                        start=(d == 0), stop=(d == 15)), waits=wv)
                pe_cs = s.now("tensor")
                s.op("vector", lambda h, t=t: h.tensor_scalar(
                    out=csWb[0:1, t * 128:t * 128 + 128], in0=pQM[0:1, 0:128],
                    scalar1=-1.0, scalar2=None, op0=ALU.mult),
                    inc=True, waits=[("tensor", pe_cs)])
                s.barrier("tensor", ["vector"])

            def qkv_group(g):
                sg = g % W
                xo = sg * 2048
                ao = sg * 256
                wv = [("tensor", qkv_pe_done[g - W])] if g >= W else ()
                s.op("gpsimd", lambda h, g=g, xo=xo: h.dma_start(
                    out=xb[:, xo:xo + 2048].rearrange("p (d c) -> p d c", d=16),
                    in_=xT[:, :, g * GT:(g + 1) * GT]), inc=True, waits=wv)
                ldx = s.now("gpsimd")
                s.op("scalar", lambda h, xo=xo: h.activation(
                    out=sqb[:, 0:2048], in_=xb[:, xo:xo + 2048],
                    func=AF.Square), inc=True,
                    waits=[("gpsimd", ldx), ("tensor", qkv_pe_done[g - 1])])
                sqd = s.now("scalar")

                for t in range(8):
                    pq = pqA if t < 4 else pqB
                    for d in range(16):
                        wv = [("gpsimd", ldx), ("vector", qkv_evac_done[g - 1]),
                              ("vector", scan_done[g - W])] \
                            if (t == 0 and d == 0) else ()
                        s.op("tensor", lambda h, t=t, d=d, pq=pq, xo=xo: h.matmul(
                            pq[:, (t % 4) * 128:(t % 4) * 128 + 128],
                            wfb[:, (d * 8 + t) * 128:(d * 8 + t) * 128 + 128],
                            xb[:, xo + d * 128: xo + d * 128 + 128],
                            start=(t % 4 == 0 and d == 0), stop=(d == 15)),
                            waits=wv)
                for d in range(16):
                    wv = [("scalar", sqd), ("vector", scan_done[g - 1]),
                          ("scalar", s.now("scalar"))] if d == 0 else ()
                    s.op("tensor", lambda h, d=d, xo=xo: h.matmul(
                        pQM[0:1, 0:128], c_onesb[:],
                        sqb[:, d * 128: d * 128 + 128],
                        start=(d == 0), stop=(d == 15)), waits=wv)
                pe_mm = s.now("tensor")

                ext = pqB[:, 384:512]
                s.op("vector", lambda h, ext=ext: h.tensor_copy(
                    out=strow[0:1, 640:768], in_=ext[0:1, 0:128]),
                    waits=[("tensor", pe_mm)])
                s.op("vector", lambda h: h.tensor_tensor(
                    out=strow[0:1, 128:256], in0=strow[0:1, 640:768],
                    in1=strow[0:1, 640:768], op=ALU.mult))
                s.op("vector", lambda h: h.scalar_tensor_tensor(
                    out=strow[0:1, 128:256], in0=strow[0:1, 128:256],
                    scalar=-1.0 / D, in1=pQM[0:1, 0:128], op0=ALU.mult,
                    op1=ALU.add))
                s.op("vector", lambda h, ext=ext: h.tensor_scalar(
                    out=strowb[0:1, :], in0=ext[0:1, 0:128], scalar1=1.0 / D,
                    scalar2=None, op0=ALU.mult), inc=True)
                dve_r = s.now("vector")
                s.op("scalar", lambda h: h.activation(
                    out=strow[0:1, 256:384], in_=strow[0:1, 128:256], func=AF.Ln,
                    scale=1.0 / D, bias=c_epsM[0:1, :]), waits=[("vector", dve_r)])
                s.op("scalar", lambda h: h.activation(
                    out=strow[0:1, 0:128], in_=strow[0:1, 256:384], func=AF.Exp,
                    scale=-0.5), inc=True)
                act_r = s.now("scalar")

                for t in range(8):
                    pq = pqA if t < 4 else pqB
                    s.op("tensor", lambda h, t=t, pq=pq: h.matmul(
                        pq[:, (t % 4) * 128:(t % 4) * 128 + 128],
                        csWb[0:1, t * 128:t * 128 + 128], strowb[0:1, :],
                        start=False, stop=True),
                        waits=[("scalar", act_r)] if t == 0 else ())
                pe_fix = s.op("tensor", lambda h: h.matmul(
                    pQM[:, 128:256], c_onesrow[0:1, :], strow[0:1, 0:128],
                    start=True, stop=True), inc=True)
                qkv_pe_done[g] = pe_fix
                s.op("vector", lambda h: h.tensor_copy(
                    out=rstdB[:], in_=pQM[:, 128:256]), inc=True,
                    waits=[("tensor", pe_fix), ("vector", scan_done[g - W])])
                s.barrier("tensor", ["vector"])

                for hh in range(2):
                    tq = pqA[:, hh * 128:hh * 128 + 128]
                    qf = qs_f[:, ao + hh * 128: ao + hh * 128 + 128]
                    q_ev = s.op("vector", lambda h, tq=tq, qf=qf: h.tensor_tensor(
                        out=qf, in0=tq, in1=rstdB[:], op=ALU.mult), inc=True)
                    s.op("scalar", lambda h, qf=qf: h.activation(
                        out=oth[:], in_=qf, func=AF.Exp, scale=-1.0),
                        waits=[("vector", q_ev)])
                    s.op("scalar", lambda h: h.activation(
                        out=oth[:], in_=oth[:], func=AF.Ln, scale=1.0,
                        bias=c_onesc[:]))
                    a_t = s.op("scalar", lambda h: h.activation(
                        out=oth[:], in_=oth[:], func=AF.Exp, scale=-1.0),
                        inc=True)
                    s.op("vector", lambda h, qf=qf: h.tensor_tensor(
                        out=qf, in0=qf, in1=oth[:], op=ALU.mult),
                        waits=[("scalar", a_t)])
                    s.op("vector", lambda h, qf=qf, hh=hh, ao=ao: h.tensor_copy(
                        out=qs_b[:, ao + hh * 128: ao + hh * 128 + 128], in_=qf),
                        inc=True)
                    s.barrier("scalar", ["vector"])
                for hh in range(2):
                    tk = pqA[:, (2 + hh) * 128:(2 + hh) * 128 + 128]
                    kf = kn_f[:, ao + hh * 128: ao + hh * 128 + 128]
                    k_ev = s.op("vector", lambda h, tk=tk, kf=kf: h.tensor_tensor(
                        out=kf, in0=tk, in1=rstdB[:], op=ALU.mult), inc=True)
                    k_sq = s.op("scalar", lambda h, kf=kf: h.activation(
                        out=qscr1[:], in_=kf, func=AF.Square),
                        inc=True, waits=[("vector", k_ev)])
                    k_mm = s.op("tensor", lambda h: h.matmul(
                        pQM[0:1, 0:128], c_onesc[:], qscr1[:],
                        start=True, stop=True), inc=True, waits=[("scalar", k_sq)])
                    s.op("scalar", lambda h: h.activation(
                        out=strow[0:1, 384:512], in_=pQM[0:1, 0:128], func=AF.Ln,
                        scale=1.0, bias=c_epsA[0:1, :]), waits=[("tensor", k_mm)])
                    k_rn = s.op("scalar", lambda h: h.activation(
                        out=strow[0:1, 384:512], in_=strow[0:1, 384:512],
                        func=AF.Exp, scale=-0.5), inc=True)
                    k_bc = s.op("tensor", lambda h: h.matmul(
                        pQM[:, 128:256], c_onesrow[0:1, :], strow[0:1, 384:512],
                        start=True, stop=True), inc=True, waits=[("scalar", k_rn)])
                    s.op("vector", lambda h, kf=kf: h.tensor_tensor(
                        out=kf, in0=kf, in1=pQM[:, 128:256], op=ALU.mult),
                        waits=[("tensor", k_bc)])
                    s.op("vector", lambda h, kf=kf, hh=hh, ao=ao: h.tensor_copy(
                        out=kn_b[:, ao + hh * 128: ao + hh * 128 + 128], in_=kf),
                        inc=True)
                    s.barrier("tensor", ["vector"])
                    s.barrier("scalar", ["vector"])
                for hh in range(2):
                    tv = pqB[:, hh * 128:hh * 128 + 128]
                    s.op("vector", lambda h, tv=tv, hh=hh, ao=ao: h.tensor_tensor(
                        out=v_f[:, ao + hh * 128: ao + hh * 128 + 128],
                        in0=tv, in1=rstdB[:], op=ALU.mult))
                s.op("vector", lambda h: h.tensor_tensor(
                    out=qscr2[:], in0=pqB[:, 256:384], in1=rstdB[:], op=ALU.mult))
                s.op("vector", lambda h, sg=sg: h.tensor_copy(
                    out=g1_b[:, sg * 128:sg * 128 + 128], in_=qscr2[:]))
                s.op("vector", lambda h, sg=sg, ext=ext: h.tensor_tensor(
                    out=rowbuf[32:33, sg * 128:sg * 128 + 128], in0=ext[32:33, 0:128],
                    in1=rstdB[32:33, 0:128], op=ALU.mult))
                v_done = s.op("vector", lambda h, sg=sg, ext=ext: h.tensor_tensor(
                    out=rowbuf[64:65, sg * 128:sg * 128 + 128],
                    in0=ext[64:65, 0:128], in1=rstdB[64:65, 0:128], op=ALU.mult),
                    inc=True)
                for i, base in enumerate((32, 64)):
                    kw = [("vector", v_done)] if i == 0 else ()
                    rb = lambda h, base=base, sg=sg: rowbuf[
                        base:base + 1, sg * 128:sg * 128 + 128]
                    s.op("scalar", lambda h, base=base, sg=sg: h.activation(
                        out=rowbuf[base:base + 1, sg * 128:sg * 128 + 128],
                        in_=rowbuf[base:base + 1, sg * 128:sg * 128 + 128],
                        func=AF.Exp, scale=-1.0), waits=kw)
                    s.op("scalar", lambda h, base=base, sg=sg: h.activation(
                        out=rowbuf[base:base + 1, sg * 128:sg * 128 + 128],
                        in_=rowbuf[base:base + 1, sg * 128:sg * 128 + 128],
                        func=AF.Ln, scale=1.0, bias=c_onesc[0:1, :]))
                    s.op("scalar", lambda h, base=base, sg=sg: h.activation(
                        out=rowbuf[base:base + 1, sg * 128:sg * 128 + 128],
                        in_=rowbuf[base:base + 1, sg * 128:sg * 128 + 128],
                        func=AF.Exp, scale=-1.0), inc=(i == 1))
                e_th = s.now("scalar")
                for i, base in enumerate((32, 64)):
                    kw = [("scalar", e_th)] if i == 0 else ()
                    s.op("vector", lambda h, base=base, sg=sg: h.tensor_scalar(
                        out=rowbuf[base:base + 1, sg * 128:sg * 128 + 128],
                        in0=rowbuf[base:base + 1, sg * 128:sg * 128 + 128],
                        scalar1=-2.0, scalar2=None, op0=ALU.mult),
                        inc=(i == 1), waits=kw)
                e_neg = s.now("vector")
                i = 0
                for hhb, base in enumerate((32, 64)):
                    for half in range(2):
                        kw = [("vector", e_neg)] if i == 0 else ()
                        s.op("tensor", lambda h, base=base, sg=sg, i=i, half=half:
                             h.matmul(
                                 pQM[0:64, 256 + i:257 + i],
                                 rowbuf[base:base + 1,
                                        sg * 128 + half * 64:
                                        sg * 128 + half * 64 + 64],
                                 c_ones[base:base + 1, 0:1],
                                 start=True, stop=True),
                             inc=(i == 3), waits=kw)
                        i += 1
                e_tr = s.now("tensor")
                s.op("vector", lambda h, sg=sg: h.tensor_copy(
                    out=eT[0:64, sg * 4: sg * 4 + 4], in_=pQM[0:64, 256:260]),
                    inc=True, waits=[("tensor", e_tr)])
                s.barrier("tensor", ["vector"])
                qkv_evac_done[g] = s.now("vector")

            def scan_group(g):
                sg = g % W
                ao = sg * 256
                qev = qkv_evac_done[g]
                guard = qev

                if g >= 1:
                    for hh in range(2):
                        s.op("vector", lambda h, hh=hh: h.tensor_copy(
                            out=s_op[:, hh * 128:hh * 128 + 128],
                            in_=pS[:, hh * 128:hh * 128 + 128]),
                            inc=(hh == 1),
                            waits=[("vector", qev)] if hh == 0 else ())

                for hh in range(2):
                    kb = kn_b[:, ao + hh * 128: ao + hh * 128 + 128]
                    qb = qs_b[:, ao + hh * 128: ao + hh * 128 + 128]
                    for half, (aktX, agtX, mKX, mGX) in enumerate(
                            ((aktL, agtL, c_mkKL, c_mkGL),
                             (aktH, agtH, c_mkKH, c_mkGH))):
                        ec = eT[0:64, sg * 4 + hh * 2 + half:
                                sg * 4 + hh * 2 + half + 1]
                        mm1 = s.op("tensor", lambda h, kb=kb, half=half: h.matmul(
                            pSC[0:64, 256:384], kb[:, half * 64:half * 64 + 64],
                            kb, start=True, stop=True), inc=True,
                            waits=[("vector", qev)] if (hh == 0 and half == 0)
                            else ())
                        dv1 = s.op("vector", lambda h, hh=hh, ec=ec, aktX=aktX,
                                   mKX=mKX: h.scalar_tensor_tensor(
                                       out=aktX[:, hh * 128:hh * 128 + 128],
                                       in0=pSC[0:64, 256:384], scalar=ec,
                                       in1=mKX[:], op0=ALU.mult, op1=ALU.mult),
                                   inc=True, waits=[("tensor", mm1)])
                        mm2 = s.op("tensor", lambda h, kb=kb, qb=qb, half=half:
                                   h.matmul(
                                       pSC[0:64, 256:384],
                                       kb[:, half * 64:half * 64 + 64], qb,
                                       start=True, stop=True), inc=True,
                                   waits=[("vector", dv1)])
                        s.op("vector", lambda h, hh=hh, ec=ec, agtX=agtX,
                             mGX=mGX: h.scalar_tensor_tensor(
                                 out=agtX[:, hh * 128:hh * 128 + 128],
                                 in0=pSC[0:64, 256:384], scalar=ec, in1=mGX[:],
                                 op0=ALU.mult, op1=ALU.mult),
                             inc=True, waits=[("tensor", mm2)])
                        s.barrier("tensor", ["vector"])

                for hh in range(2):
                    s.op("vector", lambda h, hh=hh, ao=ao: h.tensor_tensor(
                        out=oscr[:, 0:128],
                        in0=v_f[:, ao + hh * 128:ao + hh * 128 + 128],
                        in1=kn_f[:, ao + hh * 128:ao + hh * 128 + 128],
                        op=ALU.subtract))
                    s.op("vector", lambda h, hh=hh: h.tensor_scalar(
                        out=wbm[:, hh * 128:hh * 128 + 128], in0=oscr[:, 0:128],
                        scalar1=c_negw[:, hh:hh + 1], scalar2=c_wb[:, hh:hh + 1],
                        op0=ALU.mult, op1=ALU.add), inc=True)
                    base = 32 + hh * 32
                    ebc = s.op("tensor", lambda h, sg=sg, base=base: h.matmul(
                        pQM[:, 128:256], c_ones[base:base + 1, 0:128],
                        rowbuf[base:base + 1, sg * 128:sg * 128 + 128],
                        start=True, stop=True), inc=True,
                        waits=[("vector", guard)] if hh == 0 else ())
                    kes = s.op("vector", lambda h, hh=hh, ao=ao: h.tensor_tensor(
                        out=kescr[:],
                        in0=kn_f[:, ao + hh * 128:ao + hh * 128 + 128],
                        in1=pQM[:, 128:256], op=ALU.mult), inc=True,
                        waits=[("tensor", ebc)])
                    ktr = 0
                    for half in range(2):
                        ktr = s.op("tensor", lambda h, half=half: h.transpose(
                            pSMb[0:64, half * 128:half * 128 + 128],
                            kescr[:, half * 64:half * 64 + 64], c_id[:]),
                            inc=True,
                            waits=[("vector", kes)] if half == 0 else ())
                    for half, ketX in enumerate((ketL, ketH)):
                        s.op("vector", lambda h, hh=hh, half=half, ketX=ketX:
                             h.tensor_copy(
                                 out=ketX[:, hh * 128:hh * 128 + 128],
                                 in_=pSMb[0:64, half * 128:half * 128 + 128]),
                             inc=(half == 1),
                             waits=[("tensor", ktr)] if half == 0 else ())
                    s.barrier("tensor", ["vector"])
                prep_done = s.now("vector")

                for hh in range(2):
                    s.op("tensor", lambda h, hh=hh, ao=ao: h.matmul(
                        pA[:, hh * 128:hh * 128 + 128],
                        s_op[:, hh * 128:hh * 128 + 128],
                        kn_b[:, ao + hh * 128:ao + hh * 128 + 128],
                        start=(hh == 0), stop=True),
                        waits=[("vector", prep_done)] if hh == 0 else ())
                    s.op("tensor", lambda h, hh=hh, ao=ao: h.matmul(
                        pB[:, hh * 128:hh * 128 + 128],
                        s_op[:, hh * 128:hh * 128 + 128],
                        qs_b[:, ao + hh * 128:ao + hh * 128 + 128],
                        start=(hh == 0), stop=True))
                _ = s.now("tensor")

                last_stk = s.now("vector")
                for j in range(NCH):
                    m = j // 2
                    c0 = j * CH
                    if j > 0:
                        KL = min(16 * j, 64)
                        KH = 16 * j - KL
                        for hh in range(2):
                            s.op("tensor", lambda h, hh=hh, KL=KL, c0=c0: h.matmul(
                                pA[:, hh * 128 + c0: hh * 128 + c0 + CH],
                                vnstkL[0:KL, hh * 128:hh * 128 + 128],
                                aktL[0:KL, hh * 128 + c0:hh * 128 + c0 + CH],
                                start=False, stop=True),
                                waits=[("vector", last_stk)] if hh == 0 else ())
                            if KH > 0:
                                s.op("tensor", lambda h, hh=hh, KH=KH, c0=c0:
                                     h.matmul(
                                         pA[:, hh * 128 + c0:
                                            hh * 128 + c0 + CH],
                                         vnstkH[0:KH, hh * 128:hh * 128 + 128],
                                         aktH[0:KH,
                                              hh * 128 + c0:hh * 128 + c0 + CH],
                                         start=False, stop=True))
                    corr_pe = s.now("tensor")

                    s.op("vector", lambda h, c0=c0: h.tensor_copy(
                        out=scrA[:, 0:32],
                        in_=blk2(pA, c0, CH, 128)),
                        waits=[("tensor", corr_pe)])
                    kh_ev = s.op("vector", lambda h: h.tensor_tensor(
                        out=scrA[:, 32:64], in0=scrA[:, 0:32], in1=scrA[:, 0:32],
                        op=ALU.mult), inc=True)
                    st1 = s.op("tensor", lambda h: h.matmul(
                        pSM[:, 128:192], c_ones[:], scrA[:, 0:64],
                        start=True, stop=True), inc=True, waits=[("vector", kh_ev)])
                    s.op("vector", lambda h: h.tensor_copy(
                        out=scr2[:], in_=pSM[:, 128:160]), waits=[("tensor", st1)])
                    s.op("vector", lambda h: h.tensor_tensor(
                        out=scr1[:], in0=scr2[:], in1=scr2[:], op=ALU.mult))
                    vpre = s.op("vector", lambda h: h.scalar_tensor_tensor(
                        out=scr1[:], in0=scr1[:], scalar=-1.0 / DH,
                        in1=pSM[:, 160:192], op0=ALU.mult, op1=ALU.add), inc=True)
                    s.op("scalar", lambda h: h.activation(
                        out=scr2[:], in_=scr1[:], func=AF.Ln, scale=1.0 / DH,
                        bias=c_epsA[:]), waits=[("vector", vpre)])
                    a_rs = s.op("scalar", lambda h: h.activation(
                        out=rstdc[:], in_=scr2[:], func=AF.Exp, scale=-0.5),
                        inc=True)
                    s.op("vector", lambda h: h.scalar_tensor_tensor(
                        out=khh[:], in0=pSM[:, 128:160], scalar=-1.0 / DH,
                        in1=scrA[:, 0:32], op0=ALU.mult, op1=ALU.add),
                        waits=[("scalar", a_rs)])
                    s.op("vector", lambda h: h.tensor_tensor(
                        out=khh[:], in0=khh[:], in1=rstdc[:], op=ALU.mult))
                    s.op("vector", lambda h: h.tensor_tensor(
                        out=scrA[:, 64:96], in0=khh[:], in1=c_w2B[:], op=ALU.mult))
                    s.op("vector", lambda h, c0=c0: h.tensor_tensor(
                        out=scrA[:, 64:96], in0=scrA[:, 64:96],
                        in1=blk2(wbm, c0, CH, 128), op=ALU.add))
                    gw_d = s.op("vector", lambda h: h.tensor_tensor(
                        out=scrA[:, 96:128], in0=scrA[:, 64:96], in1=khh[:],
                        op=ALU.mult), inc=True)
                    st2 = s.op("tensor", lambda h: h.matmul(
                        pSM[:, 192:256], c_ones[:], scrA[:, 64:128],
                        start=True, stop=True), inc=True, waits=[("vector", gw_d)])
                    s.op("vector", lambda h: h.scalar_tensor_tensor(
                        out=scr1[:], in0=khh[:], scalar=1.0 / DH,
                        in1=pSM[:, 224:256], op0=ALU.mult, op1=ALU.mult),
                        waits=[("tensor", st2)])
                    s.op("vector", lambda h: h.tensor_tensor(
                        out=scr1[:], in0=scrA[:, 64:96], in1=scr1[:],
                        op=ALU.subtract))
                    s.op("vector", lambda h: h.scalar_tensor_tensor(
                        out=scr1[:], in0=pSM[:, 192:224], scalar=-1.0 / DH,
                        in1=scr1[:], op0=ALU.mult, op1=ALU.add))
                    s.op("vector", lambda h: h.tensor_tensor(
                        out=scr2[:], in0=scr1[:], in1=rstdc[:], op=ALU.mult))
                    po = (j % 2) * 16
                    s.op("vector", lambda h, po=po: h.tensor_copy(
                        out=vnstg[:, po:po + 16], in_=scr2[:, 0:16]))
                    vn_d = s.op("vector", lambda h, po=po: h.tensor_copy(
                        out=vnstg[:, 32 + po:32 + po + 16], in_=scr2[:, 16:32]),
                        inc=True)

                    vstk = vnstkL if m < 2 else vnstkH
                    r0 = 32 * (m % 2)
                    nr = 16 if j % 2 == 0 else 32
                    src0 = 0 if j % 2 == 0 else 0
                    trs = 0
                    for hh in range(2):
                        trs = s.op("tensor", lambda h, hh=hh, r0=r0, nr=nr:
                                   h.transpose(
                                       pSMb[r0:r0 + nr,
                                            hh * 128:hh * 128 + 128],
                                       vnstg[:, hh * 32:hh * 32 + nr], c_id[:]),
                                   inc=True,
                                   waits=[("vector", vn_d)] if hh == 0 else ())
                    s.op("vector", lambda h, r0=r0, nr=nr, vstk=vstk:
                         h.tensor_copy(
                             out=vstk[r0:r0 + nr, 0:256],
                             in_=pSMb[r0:r0 + nr, 0:256]),
                         inc=True, waits=[("tensor", trs)])
                    last_stk = s.now("vector")
                    s.barrier("tensor", ["vector"])
                    if globals_pull[0] is not None:
                        globals_pull[0](1)

                for hh in range(2):
                    s.op("tensor", lambda h, hh=hh: h.matmul(
                        pB[:, hh * 128:hh * 128 + 128],
                        vnstkL[:, hh * 128:hh * 128 + 128],
                        agtL[:, hh * 128:hh * 128 + 128], start=False, stop=False),
                        waits=[("vector", last_stk)] if hh == 0 else ())
                    s.op("tensor", lambda h, hh=hh: h.matmul(
                        pB[:, hh * 128:hh * 128 + 128],
                        vnstkH[:, hh * 128:hh * 128 + 128],
                        agtH[:, hh * 128:hh * 128 + 128], start=False, stop=True))
                    s.op("tensor", lambda h, hh=hh, g=g: h.matmul(
                        pS[:, hh * 128:hh * 128 + 128],
                        ketL[:, hh * 128:hh * 128 + 128],
                        vnstkL[:, hh * 128:hh * 128 + 128],
                        start=(g == 0 and hh == 0), stop=False))
                    s.op("tensor", lambda h, hh=hh, g=g: h.matmul(
                        pS[:, hh * 128:hh * 128 + 128],
                        ketH[:, hh * 128:hh * 128 + 128],
                        vnstkH[:, hh * 128:hh * 128 + 128],
                        start=False, stop=True))
                qh_pe = s.now("tensor")

                for hh in range(2):
                    qf = qs_f[:, ao + hh * 128: ao + hh * 128 + 128]
                    s.op("vector", lambda h, hh=hh: h.tensor_copy(
                        out=oscr[:, 0:128], in_=pB[:, hh * 128:hh * 128 + 128]),
                        waits=[("tensor", qh_pe)] if hh == 0 else ())
                    o_ev = s.op("vector", lambda h: h.tensor_tensor(
                        out=oscr[:, 128:256], in0=oscr[:, 0:128],
                        in1=oscr[:, 0:128], op=ALU.mult), inc=True)
                    o_mm = s.op("tensor", lambda h: h.matmul(
                        pSM[:, 256:512], c_ones[:], oscr[:, 0:256],
                        start=True, stop=True), inc=True, waits=[("vector", o_ev)])
                    s.op("vector", lambda h: h.tensor_copy(
                        out=oo[:], in_=pSM[:, 256:384]), waits=[("tensor", o_mm)])
                    s.op("vector", lambda h: h.tensor_tensor(
                        out=oo[:], in0=oo[:], in1=oo[:], op=ALU.mult))
                    o_v = s.op("vector", lambda h: h.scalar_tensor_tensor(
                        out=oo[:], in0=oo[:], scalar=-1.0 / DH,
                        in1=pSM[:, 384:512], op0=ALU.mult, op1=ALU.add), inc=True)
                    s.op("scalar", lambda h: h.activation(
                        out=oo2[:], in_=oo[:], func=AF.Ln, scale=1.0 / DH,
                        bias=c_epsA[:]), waits=[("vector", o_v)])
                    o_rs = s.op("scalar", lambda h: h.activation(
                        out=oo2[:], in_=oo2[:], func=AF.Exp, scale=-0.5), inc=True)
                    s.op("vector", lambda h: h.scalar_tensor_tensor(
                        out=oo[:], in0=pSM[:, 256:384], scalar=-1.0 / DH,
                        in1=oscr[:, 0:128], op0=ALU.mult, op1=ALU.add),
                        waits=[("scalar", o_rs)])
                    s.op("vector", lambda h: h.tensor_tensor(
                        out=oo[:], in0=oo[:], in1=oo2[:], op=ALU.mult))
                    s.op("vector", lambda h, hh=hh: h.tensor_scalar(
                        out=oo[:], in0=oo[:], scalar1=c_lnw[:, hh:hh + 1],
                        scalar2=c_lnb[:, hh:hh + 1], op0=ALU.mult, op1=ALU.add))
                    o_d = s.op("vector", lambda h, qf=qf: h.tensor_tensor(
                        out=oo[:], in0=oo[:], in1=qf, op=ALU.add), inc=True)
                    g_mm = s.op("tensor", lambda h, hh=hh, sg=sg: h.matmul(
                        pQM[:, 384:512], wg2b[:, hh * 128:hh * 128 + 128],
                        g1_b[:, sg * 128:sg * 128 + 128], start=True, stop=True),
                        inc=True, waits=[("vector", o_d), ("vector", guard)])
                    s.op("scalar", lambda h: h.activation(
                        out=oth[:], in_=pQM[:, 384:512], func=AF.Exp, scale=-1.0),
                        waits=[("tensor", g_mm)])
                    s.op("scalar", lambda h: h.activation(
                        out=oth[:], in_=oth[:], func=AF.Ln, scale=1.0,
                        bias=c_onesc[:]))
                    g_th = s.op("scalar", lambda h: h.activation(
                        out=oth[:], in_=oth[:], func=AF.Exp, scale=-1.0), inc=True)
                    og_d = s.op("vector", lambda h: h.tensor_tensor(
                        out=oo[:], in0=oo[:], in1=oth[:], op=ALU.mult), inc=True,
                        waits=[("scalar", g_th)])
                    og_sq = s.op("scalar", lambda h: h.activation(
                        out=oo2[:], in_=oo[:], func=AF.Square), inc=True,
                        waits=[("vector", og_d)])
                    r_mm = s.op("tensor", lambda h: h.matmul(
                        pQM[0:1, 128:256], c_onesc[:], oo2[:],
                        start=True, stop=True), inc=True,
                        waits=[("scalar", og_sq)])
                    s.op("scalar", lambda h: h.activation(
                        out=strow[0:1, 512:640], in_=pQM[0:1, 128:256], func=AF.Ln,
                        scale=1.0 / DH, bias=c_epsA[0:1, :]),
                        waits=[("tensor", r_mm)])
                    r_rs = s.op("scalar", lambda h: h.activation(
                        out=strow[0:1, 512:640], in_=strow[0:1, 512:640],
                        func=AF.Exp, scale=-0.5), inc=True)
                    r_bc = s.op("tensor", lambda h: h.matmul(
                        pQM[:, 128:256], c_onesrow[0:1, :], strow[0:1, 512:640],
                        start=True, stop=True), inc=True, waits=[("scalar", r_rs)])
                    s.op("vector", lambda h: h.tensor_tensor(
                        out=oo[:], in0=oo[:], in1=pQM[:, 128:256], op=ALU.mult),
                        waits=[("tensor", r_bc)])
                    ob_d = s.op("vector", lambda h, hh=hh: h.tensor_scalar(
                        out=obuf[:, hh * 128:hh * 128 + 128], in0=oo[:],
                        scalar1=c_nw[:, hh:hh + 1], scalar2=None, op0=ALU.mult),
                        inc=True)
                    s.barrier("tensor", ["vector"])
                    s.barrier("scalar", ["vector"])
                    s.op("sync", lambda h, g=g, hh=hh: h.dma_start(
                        out=o_st5[g // GQ][g % 8, (g % GQ) // 8, hh, :, :],
                        in_=obuf[:, hh * 128:hh * 128 + 128]), inc=True,
                        waits=[("vector", ob_d)])
                scan_done[g] = s.now("vector")
                s.barrier("vector", ["sync"])

            globals_pull = [None]

            def emit_collective(q):
                ns = s.now("sync")
                import os as _os
                if _os.environ.get("KERN_NO_CC"):
                    s.op("gpsimd", lambda h, q=q: h.dma_start(
                        out=o_gath[q].ap(), in_=o_stage[q].ap()).then_inc(
                            sems["cc"], 1), inc=False, waits=[("sync", ns)])
                else:
                    def do_cc(h, q=q):
                        with allow_ici_all_to_all():
                            return h.collective_compute(
                                "AllToAll", ALU.bypass,
                                replica_groups=[list(range(NCORE))],
                                ins=[o_stage[q].ap().opt()],
                                outs=[o_gath[q].ap().opt()]).then_inc(
                                    sems["cc"], 1)
                    s.op("gpsimd", do_cc, inc=False, waits=[("sync", ns)])

            def c_wave(q):
                t0q = q * TS
                for _ in range(12):
                    yield
                for src in range(NCORE):
                    for hh in range(2):
                        dti = src * 2 + hh
                        wv = [("cc", q + 1), ("tensor", s.now("tensor")),
                              ("pool", s.now("pool"))] \
                            if (src == 0 and hh == 0) else ()
                        s.op("sync", lambda h, src=src, hh=hh, dti=dti, q=q:
                             h.dma_start(
                                 out=oslab[:, dti * TS:dti * TS + TS].rearrange(
                                     "p (l c) -> p l c", l=LGQ),
                                 in_=o_gt5[q][src, :, hh].rearrange(
                                     "l p c -> p l c")), inc=True, waits=wv)
                s.op("gpsimd", lambda h, t0q=t0q: h.dma_start(
                    out=xinb[:].rearrange("p (d c) -> p d c", d=16),
                    in_=xres[:, :, t0q:t0q + TS]), inc=True)
                ldxi = s.now("gpsimd")
                lds = s.now("sync")
                yield

                stat_t = {}
                wo_ld = {}
                pe_done = {}
                for t in range(2):
                    s.op("sync", lambda h, t=t: h.dma_start(
                        out=wrng[:, (t % 4) * 2048:(t % 4) * 2048 + 2048],
                        in_=wobf_dram.ap()[t]), inc=True,
                        waits=[pf_wait("wo", t)])
                    wo_ld[t] = s.now("sync")
                for t in range(16):
                    pq0 = 0
                    for d in range(16):
                        wv = [("sync", wo_ld[t]), ("sync", lds)] \
                            if d == 0 else ()
                        if d == 0 and t >= 1:
                            wv = list(wv) + [("scalar", stat_t[t - 1])]
                        s.op("tensor", lambda h, t=t, d=d: h.matmul(
                            pC[:, 0:TS],
                            wrng[:, (t % 4) * 2048 + d * 128:
                                 (t % 4) * 2048 + d * 128 + 128],
                            oslab[:, d * TS:d * TS + TS],
                            start=(d == 0), stop=(d == 15)), waits=wv)
                    pe_t = s.now("tensor")
                    pe_done[t] = pe_t
                    if t + 2 < 16:
                        s.op("sync", lambda h, t=t: h.dma_start(
                            out=wrng[:, ((t + 2) % 4) * 2048:
                                     ((t + 2) % 4) * 2048 + 2048],
                            in_=wobf_dram.ap()[t + 2]), inc=True,
                            waits=[("tensor", pe_t), pf_wait("wo", t + 2)])
                        wo_ld[t + 2] = s.now("sync")
                    s.op("scalar", lambda h, t=t: h.copy(
                        out=zdn[:, t * TS:t * TS + TS], in_=pC[:, 0:TS]),
                        waits=[("tensor", pe_t)])
                    s.op("scalar", lambda h, t=t: h.copy(
                        out=zb2[:, (t % 2) * TS:(t % 2) * TS + TS],
                        in_=pC[:, 0:TS]))
                    zev = s.op("scalar", lambda h, t=t: h.activation(
                        out=zq2[:, (t % 2) * TS:(t % 2) * TS + TS],
                        in_=zb2[:, (t % 2) * TS:(t % 2) * TS + TS],
                        func=AF.Square), inc=True)
                    s.op("tensor", lambda h, t=t: h.matmul(
                        pC[0:1, 256:256 + TS], c_onesb[:],
                        zb2[:, (t % 2) * TS:(t % 2) * TS + TS],
                        start=True, stop=True), waits=[("scalar", zev)])
                    s.op("tensor", lambda h, t=t: h.matmul(
                        pC[32:33, 256:256 + TS], c_onesb[:, 0:1],
                        zq2[:, (t % 2) * TS:(t % 2) * TS + TS],
                        start=True, stop=True))
                    smm = s.now("tensor")
                    s.op("scalar", lambda h: h.copy(
                        out=stmp[0:1, 0:TS], in_=pC[0:1, 256:256 + TS]),
                        waits=[("tensor", smm)])
                    sr = s.op("scalar", lambda h: h.copy(
                        out=stmp[0:1, TS:2 * TS], in_=pC[32:33, 256:256 + TS]),
                        inc=True)
                    if t == 0:
                        s.op("vector", lambda h: h.tensor_copy(
                            out=sac1[0:1, :], in_=stmp[0:1, 0:TS]),
                            inc=True, waits=[("scalar", sr)])
                        s.op("vector", lambda h: h.tensor_copy(
                            out=sac2[0:1, :], in_=stmp[0:1, TS:2 * TS]),
                            inc=True)
                    else:
                        s.op("vector", lambda h: h.tensor_tensor(
                            out=sac1[0:1, :], in0=sac1[0:1, :],
                            in1=stmp[0:1, 0:TS], op=ALU.add),
                            inc=True, waits=[("scalar", sr)])
                        s.op("vector", lambda h: h.tensor_tensor(
                            out=sac2[0:1, :], in0=sac2[0:1, :],
                            in1=stmp[0:1, TS:2 * TS], op=ALU.add), inc=True)
                    stat_t[t] = sr
                    yield

                def ln_rows_bcast(st):
                    pl = s.op("vector", lambda h: h.tensor_tensor(
                        out=stmp[0:1, 2 * TS:3 * TS], in0=sac1[0:1, :],
                        in1=sac1[0:1, :], op=ALU.mult), inc=True)
                    pl = s.op("vector", lambda h: h.scalar_tensor_tensor(
                        out=stmp[0:1, 2 * TS:3 * TS],
                        in0=stmp[0:1, 2 * TS:3 * TS], scalar=-1.0 / D,
                        in1=sac2[0:1, :], op0=ALU.mult, op1=ALU.add), inc=True)
                    s.op("scalar", lambda h: h.activation(
                        out=stmp[0:1, 3 * TS:4 * TS],
                        in_=stmp[0:1, 2 * TS:3 * TS], func=AF.Ln,
                        scale=1.0 / D, bias=c_epsM[0:1, :]),
                        waits=[("vector", pl)])
                    s.op("scalar", lambda h: h.activation(
                        out=stmp[0:1, 3 * TS:4 * TS],
                        in_=stmp[0:1, 3 * TS:4 * TS], func=AF.Exp, scale=-0.5))
                    mr = s.op("scalar", lambda h: h.activation(
                        out=stmp[0:1, 2 * TS:3 * TS], in_=sac1[0:1, :],
                        func=AF.Copy, scale=1.0 / D), inc=True)
                    b1 = s.op("tensor", lambda h: h.matmul(
                        pC[:, 256:256 + TS], c_onesrow[0:1, :],
                        stmp[0:1, 3 * TS:4 * TS], start=True, stop=True),
                        waits=[("scalar", mr)])
                    b1 = s.op("tensor", lambda h: h.matmul(
                        pC[:, 0:TS], c_onesrow[0:1, :],
                        stmp[0:1, 2 * TS:3 * TS], start=True, stop=True))
                    b1 = s.now("tensor")
                    s.op("scalar", lambda h, st=st: h.copy(
                        out=rB1[:, st * TS:st * TS + TS], in_=pC[:, 256:256 + TS]),
                        waits=[("tensor", b1), ("pool", s.now("pool"))])
                    bd = s.op("scalar", lambda h, st=st: h.copy(
                        out=csc1[:, st * TS:st * TS + TS], in_=pC[:, 0:TS]),
                        inc=True)
                    return bd

                bd = ln_rows_bcast(0)
                yield

                for t in range(16):
                    p1 = s.op("gpsimd", lambda h, t=t: h.tensor_tensor(
                        out=csc2[:, 0:TS], in0=zdn[:, t * TS:t * TS + TS],
                        in1=csc1[:, 0:TS], op=ALU.subtract),
                        inc=False, waits=[("scalar", bd)] if t == 0 else (), sem="pool")
                    s.op("gpsimd", lambda h: h.tensor_tensor(
                        out=csc2[:, 0:TS], in0=csc2[:, 0:TS], in1=rB1[:, 0:TS],
                        op=ALU.mult), sem="pool")
                    x1d = s.op("gpsimd", lambda h, t=t: h.tensor_tensor(
                        out=x1sl[:, t * TS:t * TS + TS], in0=csc2[:, 0:TS],
                        in1=xinb[:, t * TS:t * TS + TS], op=ALU.add), inc=True,
                        waits=[("gpsimd", ldxi)] if t == 0 else (), sem="pool")
                    zev = s.op("scalar", lambda h, t=t: h.activation(
                        out=zq2[:, 0:TS], in_=x1sl[:, t * TS:t * TS + TS],
                        func=AF.Square), inc=True, waits=[("pool", x1d)])
                    s.op("tensor", lambda h, t=t: h.matmul(
                        pC[0:1, 256:256 + TS], c_onesb[:],
                        x1sl[:, t * TS:t * TS + TS], start=True, stop=True),
                        waits=[("scalar", zev), ("scalar", s.now("scalar"))])
                    s.op("tensor", lambda h, t=t: h.matmul(
                        pC[32:33, 256:256 + TS], c_onesb[:, 0:1],
                        zq2[:, 0:TS], start=True, stop=True))
                    smm = s.now("tensor")
                    s.op("scalar", lambda h: h.copy(
                        out=stmp[0:1, 0:TS], in_=pC[0:1, 256:256 + TS]),
                        waits=[("tensor", smm)])
                    sr = s.op("scalar", lambda h: h.copy(
                        out=stmp[0:1, TS:2 * TS], in_=pC[32:33, 256:256 + TS]),
                        inc=True)
                    if dbg16 is not None and q == NQ - 1:
                        s.op("vector", lambda h, t=t: h.tensor_copy(
                            out=dbg16[0:1, t:t + 1],
                            in_=stmp[0:1, TS:TS + 1]),
                            inc=True, waits=[("scalar", sr)])
                        s.op("vector", lambda h, t=t: h.tensor_copy(
                            out=dbg16[0:1, 16 + t:16 + t + 1],
                            in_=zq2[0:1, 0:1]), inc=True)
                    op = ALU.add if t > 0 else ALU.bypass
                    if t == 0:
                        s.op("vector", lambda h: h.tensor_copy(
                            out=sac1[0:1, :], in_=stmp[0:1, 0:TS]),
                            inc=True, waits=[("scalar", sr)])
                        s.op("vector", lambda h: h.tensor_copy(
                            out=sac2[0:1, :], in_=stmp[0:1, TS:2 * TS]),
                            inc=True)
                    else:
                        s.op("vector", lambda h: h.tensor_tensor(
                            out=sac1[0:1, :], in0=sac1[0:1, :],
                            in1=stmp[0:1, 0:TS], op=ALU.add),
                            inc=True, waits=[("scalar", sr)])
                        s.op("vector", lambda h: h.tensor_tensor(
                            out=sac2[0:1, :], in0=sac2[0:1, :],
                            in1=stmp[0:1, TS:2 * TS], op=ALU.add), inc=True)
                    if t % 4 == 3:
                        yield

                bd = ln_rows_bcast(1)
                import os as _os2
                if _os2.environ.get("KERN_DBG") and q == NQ - 1:
                    s.op("sync", lambda h: h.dma_start(
                        out=dbg[0:1, 0:TS], in_=sac1[0:1, :]), inc=True,
                        waits=[("scalar", bd)])
                    s.op("sync", lambda h: h.dma_start(
                        out=dbg[0:1, 512:512 + TS], in_=sac2[0:1, :]), inc=True)
                    s.op("sync", lambda h: h.dma_start(
                        out=dbg[0:1, 1024:1024 + TS],
                        in_=rB1[0:1, TS:2 * TS]), inc=True)
                    s.op("sync", lambda h: h.dma_start(
                        out=dbg[0:1, 1536:1536 + TS],
                        in_=csc1[0:1, TS:2 * TS]), inc=True)
                    s.op("gpsimd", lambda h: h.dma_start(
                        out=dbg[0:1, 2048:2048 + TS], in_=zq2[0:1, 0:TS]),
                        inc=True)
                    s.op("gpsimd", lambda h: h.dma_start(
                        out=dbg[0:1, 2560:2560 + TS],
                        in_=x1sl[0:1, 15 * TS:15 * TS + TS]), inc=True)
                yield
                for t in range(16):
                    p1 = s.op("gpsimd", lambda h, t=t: h.tensor_tensor(
                        out=csc2[:, 0:TS], in0=x1sl[:, t * TS:t * TS + TS],
                        in1=csc1[:, TS:2 * TS], op=ALU.subtract),
                        inc=False, waits=[("scalar", bd)] if t == 0 else (), sem="pool")
                    s.op("gpsimd", lambda h, t=t: h.tensor_tensor(
                        out=x2sl[:, t * TS:t * TS + TS], in0=csc2[:, 0:TS],
                        in1=rB1[:, TS:2 * TS], op=ALU.mult), inc=(t == 15),
                        sem="pool")
                    if t % 8 == 7:
                        yield
                x2_d = s.now("pool")

                guv_done = {}
                for ht in range(48):
                    sl4 = ht % 2
                    wv = [("pool", x2_d), pf_wait("g", ht)]
                    if ht >= 2:
                        wv.append(("tensor", guv_done[ht - 2]))
                    s.op("gpsimd", lambda h, ht=ht, sl4=sl4: h.dma_start(
                        out=wrng[:, sl4 * 2048:sl4 * 2048 + 2048],
                        in_=g8_dram.ap()[:, ht * 2048:ht * 2048 + 2048]),
                        inc=True, waits=wv)
                    s.op("gpsimd", lambda h, ht=ht, sl4=sl4: h.dma_start(
                        out=wrng[:, (2 + sl4) * 2048:(2 + sl4) * 2048 + 2048],
                        in_=u8_dram.ap()[:, ht * 2048:ht * 2048 + 2048]),
                        inc=True, waits=[pf_wait("u", ht)])
                    wgu = s.now("gpsimd")
                    yield

                    for d in range(16):
                        wv = [("gpsimd", wgu)] if d == 0 else ()
                        s.op("tensor", lambda h, d=d, sl4=sl4: h.matmul(
                            pC[:, 0:TS],
                            wrng[:, sl4 * 2048 + d * 128:
                                 sl4 * 2048 + d * 128 + 128],
                            x2sl[:, d * TS:d * TS + TS],
                            start=(d == 0), stop=(d == 15)), waits=wv)
                    for d in range(16):
                        s.op("tensor", lambda h, d=d, sl4=sl4: h.matmul(
                            pC[:, 256:256 + TS],
                            wrng[:, (2 + sl4) * 2048 + d * 128:
                                 (2 + sl4) * 2048 + d * 128 + 128],
                            x2sl[:, d * TS:d * TS + TS],
                            start=(d == 0), stop=(d == 15)))
                    gu_pe = s.now("tensor")
                    guv_done[ht] = gu_pe
                    s.op("scalar", lambda h, ht=ht: h.activation(
                        out=gth[:], in_=pC[:, 0:TS], func=AF.Exp,
                        scale=ngs[:, ht:ht + 1]), waits=[("tensor", gu_pe)])
                    s.op("scalar", lambda h: h.activation(
                        out=gth[:], in_=gth[:], func=AF.Ln, scale=1.0,
                        bias=c_onesc[:]))
                    s.op("scalar", lambda h: h.activation(
                        out=gth[:], in_=gth[:], func=AF.Exp, scale=-1.0))
                    s.op("scalar", lambda h, ht=ht: h.activation(
                        out=gxh[:], in_=pC[:, 0:TS], func=AF.Copy,
                        scale=gscb[:, ht:ht + 1]))
                    guc = s.op("scalar", lambda h, ht=ht: h.activation(
                        out=guv[:], in_=pC[:, 256:256 + TS], func=AF.Copy,
                        scale=uscb[:, ht:ht + 1]), inc=True)
                    s.op("gpsimd", lambda h: h.tensor_tensor(
                        out=gvv[:], in0=gth[:], in1=gxh[:], op=ALU.mult),
                        inc=False, waits=[("scalar", guc)] + (
                            [("tensor", s.now("tensor"))] if ht == 0 else []),
                        sem="pool")
                    s.op("gpsimd", lambda h, ht=ht: h.tensor_tensor(
                        out=fusl[:, ht * TS:ht * TS + TS], in0=gvv[:],
                        in1=guv[:], op=ALU.mult), inc=True, sem="pool")
                    s.barrier("tensor", ["scalar"])
                    yield
                fu_d = s.now("pool")

                dn_done = {}
                for t in range(16):
                    sl2 = t % 2
                    wv = [("pool", fu_d), pf_wait("d", t)]
                    if t >= 1:
                        wv.append(("tensor", dn_done[t - 1]))
                    s.op("gpsimd", lambda h, t=t: h.dma_start(
                        out=drng[:, 0:6144],
                        in_=d8_dram.ap()[:, t * 6144:t * 6144 + 6144]),
                        inc=True, waits=wv)
                    dw_d = s.now("gpsimd")
                    yield
                    for half in range(2):
                        for dd in range(24):
                            ddv = half * 24 + dd
                            wv = [("gpsimd", dw_d)] \
                                if (half == 0 and dd == 0) else ()
                            s.op("tensor", lambda h, ddv=ddv: h.matmul(
                                pC[:, 0:TS],
                                drng[:, ddv * 128:ddv * 128 + 128],
                                fusl[:, ddv * TS:ddv * TS + TS],
                                start=(ddv == 0), stop=(ddv == 47)), waits=wv)
                        yield
                    dn_pe = s.now("tensor")
                    dn_done[t] = dn_pe
                    s.op("scalar", lambda h, t=t: h.activation(
                        out=zdn[:, t * TS:t * TS + TS], in_=pC[:, 0:TS],
                        func=AF.Copy, scale=dscb[:, t:t + 1]),
                        waits=[("tensor", dn_pe)])
                    s.op("scalar", lambda h, t=t: h.copy(
                        out=zb2[:, (t % 2) * TS:(t % 2) * TS + TS],
                        in_=zdn[:, t * TS:t * TS + TS]))
                    zev = s.op("scalar", lambda h, t=t: h.activation(
                        out=zq2[:, (t % 2) * TS:(t % 2) * TS + TS],
                        in_=zb2[:, (t % 2) * TS:(t % 2) * TS + TS],
                        func=AF.Square), inc=True)
                    s.op("tensor", lambda h, t=t: h.matmul(
                        pC[0:1, 256:256 + TS], c_onesb[:],
                        zb2[:, (t % 2) * TS:(t % 2) * TS + TS],
                        start=True, stop=True),
                        waits=[("scalar", zev), ("scalar", s.now("scalar"))])
                    s.op("tensor", lambda h, t=t: h.matmul(
                        pC[32:33, 256:256 + TS], c_onesb[:, 0:1],
                        zq2[:, (t % 2) * TS:(t % 2) * TS + TS],
                        start=True, stop=True))
                    smm = s.now("tensor")
                    s.op("scalar", lambda h: h.copy(
                        out=stmp[0:1, 0:TS], in_=pC[0:1, 256:256 + TS]),
                        waits=[("tensor", smm)])
                    sr = s.op("scalar", lambda h: h.copy(
                        out=stmp[0:1, TS:2 * TS], in_=pC[32:33, 256:256 + TS]),
                        inc=True)
                    if t == 0:
                        s.op("vector", lambda h: h.tensor_copy(
                            out=sac1[0:1, :], in_=stmp[0:1, 0:TS]),
                            inc=True, waits=[("scalar", sr)])
                        s.op("vector", lambda h: h.tensor_copy(
                            out=sac2[0:1, :], in_=stmp[0:1, TS:2 * TS]),
                            inc=True)
                    else:
                        s.op("vector", lambda h: h.tensor_tensor(
                            out=sac1[0:1, :], in0=sac1[0:1, :],
                            in1=stmp[0:1, 0:TS], op=ALU.add),
                            inc=True, waits=[("scalar", sr)])
                        s.op("vector", lambda h: h.tensor_tensor(
                            out=sac2[0:1, :], in0=sac2[0:1, :],
                            in1=stmp[0:1, TS:2 * TS], op=ALU.add), inc=True)
                    yield

                bd = ln_rows_bcast(2)
                yield
                trprev = {}
                for t in range(16):
                    cs0 = (t % 2) * TS
                    wv0 = [("scalar", bd)] if t == 0 else []
                    if t >= 2:
                        wv0.append(("tensor", trprev[t - 2]))
                    s.op("gpsimd", lambda h, t=t, cs0=cs0: h.tensor_tensor(
                        out=csc2[:, cs0:cs0 + TS], in0=zdn[:, t * TS:t * TS + TS],
                        in1=csc1[:, 2 * TS:3 * TS], op=ALU.subtract),
                        inc=False, waits=wv0, sem="pool")
                    s.op("gpsimd", lambda h, cs0=cs0: h.tensor_tensor(
                        out=csc2[:, cs0:cs0 + TS], in0=csc2[:, cs0:cs0 + TS],
                        in1=rB1[:, 2 * TS:3 * TS], op=ALU.mult), sem="pool")
                    ov = s.op("gpsimd", lambda h, t=t, cs0=cs0: h.tensor_tensor(
                        out=csc2[:, cs0:cs0 + TS], in0=csc2[:, cs0:cs0 + TS],
                        in1=x1sl[:, t * TS:t * TS + TS], op=ALU.add), inc=True, sem="pool")
                    for iq in range(TS // 128):
                        trq = s.op("tensor", lambda h, iq=iq, cs0=cs0: h.transpose(
                            pC.ap().bitcast(F32)[:, 0:128],
                            csc2[:, cs0 + iq * 128:cs0 + iq * 128 + 128],
                            c_idf[:]), inc=False,
                            waits=[("pool", ov), ("sync", outd.get(iq ^ 1, 0))]
                            if iq == 0 else [("sync", outd.get(iq ^ 1, 0))])
                        trq = s.now("tensor")
                        otd = s.op("scalar", lambda h, iq=iq: h.copy(
                            out=outp[:, iq * 128:iq * 128 + 128],
                            in_=pC.ap().bitcast(F32)[:, 0:128]), inc=True,
                            waits=[("tensor", trq)])
                        s.barrier("tensor", ["scalar"])
                        s.op("sync", lambda h, t=t, iq=iq, t0q=t0q: h.dma_start(
                            out=out[t0q + iq * 128: t0q + iq * 128 + 128,
                                    t * 128:t * 128 + 128],
                            in_=outp[:, iq * 128:iq * 128 + 128]), inc=True,
                            waits=[("scalar", otd)])
                        outd[iq] = s.now("sync")
                    trprev[t] = s.now("tensor")
                    if t % 4 == 3:
                        yield

            outd = {}
            import collections as _c
            pending = _c.deque()

            def pull(k):
                done = 0
                while pending and done < k:
                    try:
                        next(pending[0])
                        done += 1
                    except StopIteration:
                        pending.popleft()

            globals_pull[0] = pull

            for g in range(NG + 1):
                if g >= 1:
                    scan_group(g - 1)
                if g >= 1 and g % GQ == 0:
                    q = g // GQ - 1
                    emit_collective(q)
                    pending.append(c_wave(q))
                if g < NG:
                    qkv_group(g)
                    pf_pump(5)
            pf_pump(10 ** 9)
            while pending:
                pull(1000)
            s.barrier("gpsimd", ["sync", "vector", "scalar", "tensor"])

        s.emit(block, sems)
    return nc


def stage_inputs(inputs, T):
    Tc = T // NCORE
    x = np.ascontiguousarray(inputs["L0_attn_x"][0, :T].T.astype(np.float32))
    xT_s = np.ascontiguousarray(x.reshape(16, 128, T).transpose(1, 0, 2))

    Wq, Wk, Wv = (inputs["L0_attn_Wq"], inputs["L0_attn_Wk"], inputs["L0_attn_Wv"])
    Wbeta = inputs["L0_attn_Wbeta"]
    Wg1 = inputs["L0_attn_Wg1"]
    Wo_ = inputs["L0_attn_Wo"]
    Wg2_ = inputs["L0_attn_Wg2"]
    lnw, lnb = inputs["L0_attn_ln_w"], inputs["L0_attn_ln_b"]
    nw = inputs["L0_attn_norm_w"]
    gw = inputs["L0_ffn_gate_w"]
    uw = inputs["L0_ffn_up_w"]
    dw = inputs["L0_ffn_down_w"]

    wo_s = np.ascontiguousarray(
        Wo_.reshape(16, 128, 16, 128).transpose(2, 1, 0, 3)).reshape(16, 128, -1)
    gwt_s = np.ascontiguousarray(
        gw.reshape(16, 128, 48, 128).transpose(2, 1, 0, 3)).reshape(48, 128, -1)
    uwt_s = np.ascontiguousarray(
        uw.reshape(16, 128, 48, 128).transpose(2, 1, 0, 3)).reshape(48, 128, -1)
    dwt_s = np.ascontiguousarray(
        dw.reshape(48, 128, 16, 128).transpose(2, 1, 0, 3)).reshape(16, 128, -1)
    gsc_s = np.ascontiguousarray(inputs["L0_ffn_gate_scale"].reshape(48, 128).T)
    usc_s = np.ascontiguousarray(inputs["L0_ffn_up_scale"].reshape(48, 128).T)
    dsc_s = np.ascontiguousarray(inputs["L0_ffn_down_scale"].reshape(16, 128).T)
    sidx = np.arange(128)
    mkK_s = ((sidx[:, None] // CH) < (sidx[None, :] // CH)).astype(np.float32)
    mkG_s = (sidx[:, None] <= sidx[None, :]).astype(np.float32)
    mkKL_s = np.ascontiguousarray(mkK_s[0:64]); mkKH_s = np.ascontiguousarray(mkK_s[64:128])
    mkGL_s = np.ascontiguousarray(mkG_s[0:64]); mkGH_s = np.ascontiguousarray(mkG_s[64:128])

    per_core = []
    for c in range(NCORE):
        h0, h1 = 2 * c, 2 * c + 1
        cols = np.zeros((D, 1024), np.float32)
        cols[:, 0:128] = Wq[:, h0 * 128:(h0 + 1) * 128]
        cols[:, 128:256] = Wq[:, h1 * 128:(h1 + 1) * 128]
        cols[:, 256:384] = Wk[:, h0 * 128:(h0 + 1) * 128]
        cols[:, 384:512] = Wk[:, h1 * 128:(h1 + 1) * 128]
        cols[:, 512:640] = Wv[:, h0 * 128:(h0 + 1) * 128]
        cols[:, 640:768] = Wv[:, h1 * 128:(h1 + 1) * 128]
        cols[:, 768:896] = Wg1
        cols[:, 896] = 1.0
        cols[:, 896 + 32] = Wbeta[:, h0]
        cols[:, 896 + 64] = Wbeta[:, h1]
        wf_s = np.ascontiguousarray(
            cols.reshape(16, 128, 8, 128).transpose(1, 0, 2, 3)).reshape(128, -1)
        wg2_s = np.ascontiguousarray(np.concatenate(
            [Wg2_[:, h0 * 128:(h0 + 1) * 128], Wg2_[:, h1 * 128:(h1 + 1) * 128]],
            axis=1))
        lnw_cs = np.ascontiguousarray(
            np.stack([lnw[h0], lnw[h1]], axis=1).astype(np.float32))
        lnb_cs = np.ascontiguousarray(
            np.stack([lnb[h0], lnb[h1]], axis=1).astype(np.float32))
        nw_cs = np.ascontiguousarray(np.stack(
            [nw[h0 * 128:(h0 + 1) * 128], nw[h1 * 128:(h1 + 1) * 128]],
            axis=1).astype(np.float32))
        NLGs = Tc // 128
        xres_s = np.ascontiguousarray(np.concatenate(
            [xT_s[:, :, (8 * lg + c) * 128:(8 * lg + c) * 128 + 128]
             for lg in range(NLGs)], axis=2))

        per_core.append({
            "xT": xT_s, "wf": wf_s, "wo": wo_s, "wg2": wg2_s,
            "lnw_c": lnw_cs, "lnb_c": lnb_cs, "nw_c": nw_cs,
            "gwt": gwt_s, "uwt": uwt_s, "dwt": dwt_s,
            "gsc": gsc_s, "usc": usc_s, "dsc": dsc_s,
            "idn": np.eye(128, dtype=BF),
            "mkKL": mkKL_s, "mkKH": mkKH_s, "mkGL": mkGL_s, "mkGH": mkGH_s,
            "onesc": np.ones((128, 1), np.float32),
            "onesf": np.ones((128, 128), np.float32),
            "onesrow": np.ones((1, 128), np.float32),
            "xres": xres_s,
        })
    return per_core


def kernel(**inputs):
    T = inputs["L0_attn_x"].shape[1]
    nc = build_program(T)
    in_maps = stage_inputs(inputs, T)
    res = run_bass_kernel_spmd(nc, in_maps, core_ids=list(range(NCORE)))
    outs = [np.asarray(res.results[c]["out"]) for c in range(NCORE)]
    Tc = T // NCORE
    full = np.empty((T, D), np.float32)
    for c in range(NCORE):
        for lg in range(Tc // 128):
            g = 8 * lg + c
            full[g * 128:(g + 1) * 128] = outs[c][lg * 128:(lg + 1) * 128]
    return full[None].astype(np.float32)


# version 09
# speedup vs baseline: 1.1454x; 1.0606x over previous
import os as _os
import numpy as np
import ml_dtypes
import concourse.bass as bass
import concourse.mybir as mybir
from concourse.bass_utils import run_bass_kernel_spmd
from concourse.aot_env import allow_ici_all_to_all

F32 = mybir.dt.float32
BF16 = mybir.dt.bfloat16
F8 = mybir.dt.float8e4
AF = mybir.ActivationFunctionType
ALU = mybir.AluOpType
BF = ml_dtypes.bfloat16

D = 2048
DH = 128
CH = 16
GT = 128
NCORE = 8
EPS_A = 1e-6
EPS_M = 1e-5

ENGS = ("sync", "gpsimd", "vector", "scalar", "tensor")


COMPUTE_ENGS = ("vector", "scalar", "tensor")


class Sched:

    def __init__(self):
        self.ops = {e: [] for e in ENGS}
        self.cnt = {e: 0 for e in ENGS}
        self.cnt["pool"] = 0
        self.step = {"sync": 16, "gpsimd": 16, "vector": 1, "scalar": 1,
                     "tensor": 1, "pool": 1}

    def op(self, eng, fn, inc=False, waits=(), sem=None):
        ch = sem or eng
        self.ops[eng].append((fn, bool(inc), tuple(waits), ch))
        if ch == "pool" or eng in COMPUTE_ENGS or inc:
            self.cnt[ch] += self.step[ch]
        return self.cnt[ch]

    def now(self, eng):
        return self.cnt[eng]

    def barrier(self, eng, others):
        w = tuple((o, self.cnt[o]) for o in others if self.cnt[o] > 0)
        if w:
            self.ops[eng].append((None, False, w, eng))

    def emit(self, block, sems):
        import os as _os
        import bisect
        ssv = _os.environ.get("BASS_SELFSYNC", "vector")
        full = ssv == "full"
        full_engs = set(ssv.split(",")) if ssv not in ("min", "full") else (
            set(COMPUTE_ENGS) if full else set())
        waited = {e: set() for e in ENGS}
        waited["pool"] = set()
        for e in list(self.ops):
            for fn, inc, waits, ch in self.ops[e]:
                for we, wv in waits:
                    if wv > 0 and we in waited:
                        waited[we].add(wv)
        flagged = {e: sorted(waited[e]) for e in COMPUTE_ENGS}

        class _TProxy:
            def __init__(self, h):
                self._h = h

            def __getattr__(self, k):
                return getattr(self._h, k)

            def matmul(self, *a, **kw):
                kw.setdefault("skip_group_check", True)
                return self._h.matmul(*a, **kw)

        def rv(we, wv):
            if we not in COMPUTE_ENGS or we in full_engs:
                return wv
            import bisect as _b
            return _b.bisect_right(flagged[we], wv)

        def mk(eng):
            selfsync = eng in COMPUTE_ENGS
            efull = eng in full_engs
            miles = sorted(waited[eng]) if not selfsync else []
            fset = set(flagged.get(eng, ()))

            def body(h):
                n = 0
                npool = 0
                enforced = 0
                mi = 0
                for fn, inc, waits, ch in self.ops[eng]:
                    for we, wv in waits:
                        wv2 = rv(we, wv)
                        if wv2 > 0:
                            h.wait_ge(sems[we], wv2)
                    if fn is not None:
                        if eng == "tensor":
                            h2 = _TProxy(h)
                        else:
                            h2 = h
                        if ch == "pool":
                            ins = fn(h2)
                            npool += 1
                            ins.then_inc(sems["pool"], 1)
                            continue
                        if selfsync:
                            if efull and n > 0:
                                h.wait_ge(sems[eng], n)
                        elif inc:
                            tgt = 0
                            while mi < len(miles) and miles[mi] <= n:
                                tgt = miles[mi]
                                mi += 1
                            if tgt > enforced:
                                h.wait_ge(sems[eng], tgt)
                                enforced = tgt
                        ins = fn(h2)
                        if selfsync:
                            n += 1
                            if efull:
                                ins.then_inc(sems[eng], 1)
                            elif n in fset:
                                ins.then_inc(sems[eng], 1)
                        elif inc:
                            ins.then_inc(sems[eng], self.step[eng])
                            n += self.step[eng]
            return body
        block.sync(mk("sync"))
        block.gpsimd(mk("gpsimd"))
        block.vector(mk("vector"))
        block.scalar(mk("scalar"))
        block.tensor(mk("tensor"))



def blk2(ap0, c0, w, blk):
    a = ap0[:, c0:c0 + w]
    return bass.AP(tensor=a.tensor, offset=a.offset,
                   ap=[list(a.ap[0]), [blk, 2], [1, w]])

def build_program(T):
    NG = T // GT
    Tc = T // NCORE
    NLG = Tc // GT
    NCH = GT // CH
    LGQ = 2 if NLG >= 2 else 1
    NQ = NLG // LGQ
    GQ = NCORE * LGQ
    TS = LGQ * GT
    assert T % (GT * NCORE) == 0 and NLG % LGQ == 0

    import os as _os
    nc = bass.Bass("TRN2", target_bir_lowering=False,
                   detect_race_conditions=not _os.environ.get("KERN_NORACE"))
    s = Sched()

    def din(name, shape, dt):
        return nc.dram_tensor(name, shape, dt, kind="ExternalInput").ap()

    xT = din("xT", [128, 16, T], F32)
    wf = din("wf", [128, 16 * 8 * 128], F32)
    wo = din("wo", [16, 128, 16 * 128], F32)
    wg2 = din("wg2", [128, 256], F32)
    lnw_c = din("lnw_c", [128, 2], F32)
    lnb_c = din("lnb_c", [128, 2], F32)
    nw_c = din("nw_c", [128, 2], F32)
    gwt = din("gwt", [48, 128, 16 * 128], F32)
    uwt = din("uwt", [48, 128, 16 * 128], F32)
    dwt = din("dwt", [16, 128, 48 * 128], F32)
    gsc = din("gsc", [128, 48], F32)
    usc = din("usc", [128, 48], F32)
    dsc = din("dsc", [128, 16], F32)
    idn = din("idn", [128, 128], BF16)
    mkKL = din("mkKL", [64, 128], F32)
    mkKH = din("mkKH", [64, 128], F32)
    mkGL = din("mkGL", [64, 128], F32)
    mkGH = din("mkGH", [64, 128], F32)
    onesc = din("onesc", [128, 1], F32)
    onesf = din("onesf", [128, 128], F32)
    onesrow = din("onesrow", [1, 128], F32)
    xres = din("xres", [128, 16, Tc], F32)

    out = nc.dram_tensor("out", [Tc, D], F32, kind="ExternalOutput").ap()
    o_stage = [nc.dram_tensor("o_stage%d" % q, [NCORE * LGQ * 2 * 128, GT], BF16)
               for q in range(NQ)]
    o_gath = [nc.dram_tensor("o_gath%d" % q, [NCORE * LGQ * 2 * 128, GT], BF16)
              for q in range(NQ)]
    o_st5 = [t.ap().rearrange("(a l hh p) c -> a l hh p c",
                              a=NCORE, l=LGQ, hh=2, p=128) for t in o_stage]
    o_gt5 = [t.ap().rearrange("(a l hh p) c -> a l hh p c",
                              a=NCORE, l=LGQ, hh=2, p=128) for t in o_gath]
    dbg = nc.dram_tensor("dbg", [1, 8 * 512], F32, kind="ExternalOutput").ap() \
        if _os.environ.get("KERN_DBG") else None
    g8_dram = nc.dram_tensor("g8_dram", [128, 48 * 16 * 128], F8)
    u8_dram = nc.dram_tensor("u8_dram", [128, 48 * 16 * 128], F8)
    d8_dram = nc.dram_tensor("d8_dram", [128, 16 * 48 * 128], F8)
    wobf_dram = nc.dram_tensor("wobf_dram", [16, 128, 16 * 128], BF16)

    import contextlib
    es = contextlib.ExitStack()
    with es:
        def sb(name, shape, dt):
            return es.enter_context(nc.sbuf_tensor(name, shape, dt))

        pbank = [es.enter_context(nc.psum_tensor("pb%d" % i, [128, 512], F32))
                 for i in range(8)]
        pqA, pqB, pA, pB, pS, pC, pQM, pSM = pbank
        pSC = pSM
        pSMb = pSM.ap().bitcast(BF16)

        sems = {e: es.enter_context(nc.semaphore("sem_" + e)) for e in ENGS}
        sems["cc"] = es.enter_context(nc.semaphore("sem_cc"))
        sems["pool"] = es.enter_context(nc.semaphore("sem_pool"))
        block = es.enter_context(nc.Block())

        c_id = sb("c_id", [128, 128], BF16)
        c_idf = sb("c_idf", [128, 128], F32)
        c_mkKL = sb("c_mkKL", [64, 128], F32)
        c_mkKH = sb("c_mkKH", [64, 128], F32)
        c_mkGL = sb("c_mkGL", [64, 128], F32)
        c_mkGH = sb("c_mkGH", [64, 128], F32)
        c_ones = sb("c_ones", [128, 128], F32)
        c_onesc = sb("c_onesc", [128, 1], F32)
        c_onesb = sb("c_onesb", [128, 1], BF16)
        c_onesrow = sb("c_onesrow", [1, 128], F32)
        c_lnw = sb("c_lnw", [128, 2], F32)
        c_lnb = sb("c_lnb", [128, 2], F32)
        c_nw = sb("c_nw", [128, 2], F32)
        c_w2 = sb("c_w2", [128, 2], F32)
        c_wb = sb("c_wb", [128, 2], F32)
        c_negw = sb("c_negw", [128, 2], F32)
        c_w2B = sb("c_w2B", [128, 32], F32)
        c_epsA = sb("c_epsA", [128, 1], F32)
        c_epsM = sb("c_epsM", [128, 1], F32)

        for dst, src in [(c_id, idn), (c_mkKL, mkKL), (c_mkKH, mkKH),
                         (c_mkGL, mkGL), (c_mkGH, mkGH), (c_ones, onesf),
                         (c_onesc, onesc), (c_onesrow, onesrow),
                         (c_lnw, lnw_c), (c_lnb, lnb_c), (c_nw, nw_c)]:
            s.op("sync", lambda h, d=dst, sr=src: h.dma_start(out=d[:], in_=sr),
                 inc=True)
        ld0 = s.now("sync")

        oslab = sb("oslab", [128, 16 * TS], BF16)
        xinb = sb("xinb", [128, 16 * TS], BF16)
        zdn = sb("zdn", [128, 16 * TS], F32)
        x1sl = sb("x1sl", [128, 16 * TS], BF16)
        x2sl = sb("x2sl", [128, 16 * TS], BF16)
        fusl = sb("fusl", [128, 48 * TS], BF16)
        wrng = sb("wrng", [128, 4 * 2048], BF16)
        g8s = sb("g8s", [128, 2 * 2048], F8)
        u8s = sb("u8s", [128, 2 * 2048], F8)
        d8s = sb("d8s", [128, 6144], F8)
        zb2 = sb("zb2", [128, 2 * TS], BF16)
        zq2 = sb("zq2", [128, 2 * TS], BF16)
        gth = sb("gth", [128, TS], F32)
        gxh = sb("gxh", [128, TS], F32)
        guv = sb("guv", [128, TS], F32)
        gvv = sb("gvv", [128, TS], F32)
        sac1 = sb("sac1", [1, TS], F32)
        sac2 = sb("sac2", [1, TS], F32)
        stmp = sb("stmp", [1, 4 * TS], F32)
        rB1 = sb("rB1", [128, 3 * TS], F32)
        csc1 = sb("csc1", [128, 3 * TS], F32)
        csc2 = sb("csc2", [128, 2 * TS], F32)
        outp = sb("outp", [128, 2 * 128], F32)
        gscb = sb("gscb", [128, 48], F32)
        uscb = sb("uscb", [128, 48], F32)
        dscb = sb("dscb", [128, 16], F32)
        ngs = sb("ngs", [128, 48], F32)
        dbg16 = sb("dbg16", [1, 64], F32) if _os.environ.get("KERN_DBG") \
            else None

        s.op("sync", lambda h: h.dma_start(out=gscb[:], in_=gsc), inc=True)
        s.op("sync", lambda h: h.dma_start(out=uscb[:], in_=usc), inc=True)
        s.op("sync", lambda h: h.dma_start(out=dscb[:], in_=dsc), inc=True)
        ldsc = s.now("sync")
        s.op("vector", lambda h: h.tensor_scalar(
            out=ngs[:], in0=gscb[:], scalar1=-1.0, scalar2=None, op0=ALU.mult),
            inc=True, waits=[("sync", ldsc)])

        scan_done = {}
        qkv_evac_done = {}
        qkv_pe_done = {}
        for g in range(-4, 0):
            scan_done[g] = 0
            qkv_evac_done[g] = 0
            qkv_pe_done[g] = 0

        pf_jobs = []
        for t in range(16):
            pf_jobs.append(("wo", t, lambda h, t=t: h.dma_start(
                out=wobf_dram.ap()[t], in_=wo[t])))
        for ht in range(48):
            pf_jobs.append(("g", ht, lambda h, ht=ht: h.dma_start(
                out=g8_dram.ap()[:, ht * 2048:ht * 2048 + 2048], in_=gwt[ht])))
            pf_jobs.append(("u", ht, lambda h, ht=ht: h.dma_start(
                out=u8_dram.ap()[:, ht * 2048:ht * 2048 + 2048], in_=uwt[ht])))
        for t in range(16):
            pf_jobs.append(("d", t, lambda h, t=t: h.dma_start(
                out=d8_dram.ap()[:, t * 6144:t * 6144 + 6144], in_=dwt[t])))
        pf_done = {}
        pf_idx = [0]

        def pf_pump(k):
            while pf_idx[0] < len(pf_jobs) and k > 0:
                kind, idx, fn = pf_jobs[pf_idx[0]]
                s.op("gpsimd", fn, inc=True)
                pf_done[(kind, idx)] = s.now("gpsimd")
                pf_idx[0] += 1
                k -= 1

        def pf_wait(kind, idx):
            while (kind, idx) not in pf_done:
                pf_pump(1)
            return ("gpsimd", pf_done[(kind, idx)])

        esA = contextlib.ExitStack()
        with esA:
            def sba(name, shape, dt):
                return esA.enter_context(nc.sbuf_tensor(name, shape, dt))

            wfb = sba("wfb", [128, 16 * 8 * 128], BF16)
            wg2b = sba("wg2b", [128, 256], BF16)
            csWb = sba("csWb", [1, 8 * 128], BF16)
            W = 2
            xb = sba("xb", [128, W * 2048], BF16)
            sqb = sba("sqb", [128, 2048], BF16)
            qs_f = sba("qs_f", [128, W * 256], F32)
            qs_b = sba("qs_b", [128, W * 256], BF16)
            kn_f = sba("kn_f", [128, W * 256], F32)
            kn_b = sba("kn_b", [128, W * 256], BF16)
            v_f = sba("v_f", [128, W * 256], F32)
            g1_b = sba("g1_b", [128, W * 128], BF16)
            eT = sba("eT", [128, W * 4], F32)
            rowbuf = sba("rowbuf", [128, W * 128], F32)
            s_op = sba("s_op", [128, 256], BF16)
            vnstkL = sba("vnstkL", [64, 256], BF16)
            vnstkH = sba("vnstkH", [64, 256], BF16)
            aktL = sba("aktL", [64, 256], BF16)
            aktH = sba("aktH", [64, 256], BF16)
            agtL = sba("agtL", [64, 256], BF16)
            agtH = sba("agtH", [64, 256], BF16)
            vnstg = sba("vnstg", [128, 64], BF16)
            wbm = sba("wbm", [128, 256], F32)
            ketL = sba("ketL", [64, 256], BF16)
            ketH = sba("ketH", [64, 256], BF16)
            kescr = sba("kescr", [128, 128], BF16)
            scrA = sba("scrA", [128, 128], F32)
            scr1 = sba("scr1", [128, 32], F32)
            scr2 = sba("scr2", [128, 32], F32)
            rstdc = sba("rstdc", [128, 32], F32)
            khh = sba("khh", [128, 32], F32)
            strow = sba("strow", [1, 6 * 128], F32)
            strowb = sba("strowb", [1, 128], BF16)
            rstdB = sba("rstdB", [128, 128], F32)
            qscr1 = sba("qscr1", [128, 128], F32)
            qscr2 = sba("qscr2", [128, 128], F32)
            oscr = sba("oscr", [128, 256], F32)
            oth = sba("oth", [128, 128], F32)
            osig = sba("osig", [128, 128], F32)
            oo = sba("oo", [128, 128], F32)
            oo2 = sba("oo2", [128, 128], F32)
            obuf = sba("obuf", [128, 256], BF16)

            s.op("gpsimd", lambda h: h.dma_start(out=wfb[:], in_=wf), inc=True)
            s.op("gpsimd", lambda h: h.dma_start(out=wg2b[:], in_=wg2), inc=True)
            ld1 = s.now("gpsimd")

            s.op("vector", lambda h: h.tensor_tensor(
                out=c_w2[:], in0=c_lnw[:], in1=c_lnw[:], op=ALU.mult),
                waits=[("sync", ld0)])
            s.op("vector", lambda h: h.tensor_tensor(
                out=c_wb[:], in0=c_lnw[:], in1=c_lnb[:], op=ALU.mult))
            s.op("vector", lambda h: h.tensor_scalar(
                out=c_negw[:], in0=c_lnw[:], scalar1=-1.0, scalar2=None,
                op0=ALU.mult))
            s.op("vector", lambda h: h.tensor_copy(out=c_idf[:], in_=c_id[:]))
            s.op("vector", lambda h: h.tensor_copy(out=c_onesb[:], in_=c_onesc[:]))
            for hh in range(2):
                s.op("vector", lambda h, hh=hh: h.tensor_scalar(
                    out=c_w2B[:, hh * 16:hh * 16 + 16], in0=c_ones[:, 0:16],
                    scalar1=c_w2[:, hh:hh + 1], scalar2=None, op0=ALU.mult))
            s.op("vector", lambda h: h.memset(s_op[:], 0.0))
            s.op("vector", lambda h: h.memset(rowbuf[:], 0.0))
            s.op("vector", lambda h: h.memset(c_epsA[:], EPS_A))
            s.op("vector", lambda h: h.memset(c_epsM[:], EPS_M), inc=True)
            pre0 = s.now("vector")

            for t in range(8):
                for d in range(16):
                    wv = [("gpsimd", ld1), ("vector", pre0)] \
                        if (t == 0 and d == 0) else ()
                    s.op("tensor", lambda h, t=t, d=d: h.matmul(
                        pQM[0:1, 0:128], c_onesb[:],
                        wfb[:, (d * 8 + t) * 128:(d * 8 + t) * 128 + 128],
                        start=(d == 0), stop=(d == 15)), waits=wv)
                pe_cs = s.now("tensor")
                s.op("vector", lambda h, t=t: h.tensor_scalar(
                    out=csWb[0:1, t * 128:t * 128 + 128], in0=pQM[0:1, 0:128],
                    scalar1=-1.0, scalar2=None, op0=ALU.mult),
                    inc=True, waits=[("tensor", pe_cs)])
                s.barrier("tensor", ["vector"])

            def qkv_group(g):
                sg = g % W
                xo = sg * 2048
                ao = sg * 256
                wv = [("tensor", qkv_pe_done[g - W])] if g >= W else ()
                s.op("gpsimd", lambda h, g=g, xo=xo: h.dma_start(
                    out=xb[:, xo:xo + 2048].rearrange("p (d c) -> p d c", d=16),
                    in_=xT[:, :, g * GT:(g + 1) * GT]), inc=True, waits=wv)
                ldx = s.now("gpsimd")
                s.op("scalar", lambda h, xo=xo: h.activation(
                    out=sqb[:, 0:2048], in_=xb[:, xo:xo + 2048],
                    func=AF.Square), inc=True,
                    waits=[("gpsimd", ldx), ("tensor", qkv_pe_done[g - 1])])
                sqd = s.now("scalar")

                for t in range(8):
                    pq = pqA if t < 4 else pqB
                    for d in range(16):
                        wv = [("gpsimd", ldx), ("vector", qkv_evac_done[g - 1]),
                              ("vector", scan_done[g - W])] \
                            if (t == 0 and d == 0) else ()
                        s.op("tensor", lambda h, t=t, d=d, pq=pq, xo=xo: h.matmul(
                            pq[:, (t % 4) * 128:(t % 4) * 128 + 128],
                            wfb[:, (d * 8 + t) * 128:(d * 8 + t) * 128 + 128],
                            xb[:, xo + d * 128: xo + d * 128 + 128],
                            start=(t % 4 == 0 and d == 0), stop=(d == 15)),
                            waits=wv)
                for d in range(16):
                    wv = [("scalar", sqd), ("vector", scan_done[g - 1]),
                          ("scalar", s.now("scalar"))] if d == 0 else ()
                    s.op("tensor", lambda h, d=d, xo=xo: h.matmul(
                        pQM[0:1, 0:128], c_onesb[:],
                        sqb[:, d * 128: d * 128 + 128],
                        start=(d == 0), stop=(d == 15)), waits=wv)
                pe_mm = s.now("tensor")

                ext = pqB[:, 384:512]
                s.op("vector", lambda h, ext=ext: h.tensor_copy(
                    out=strow[0:1, 640:768], in_=ext[0:1, 0:128]),
                    waits=[("tensor", pe_mm)])
                s.op("vector", lambda h: h.tensor_tensor(
                    out=strow[0:1, 128:256], in0=strow[0:1, 640:768],
                    in1=strow[0:1, 640:768], op=ALU.mult))
                s.op("vector", lambda h: h.scalar_tensor_tensor(
                    out=strow[0:1, 128:256], in0=strow[0:1, 128:256],
                    scalar=-1.0 / D, in1=pQM[0:1, 0:128], op0=ALU.mult,
                    op1=ALU.add))
                s.op("vector", lambda h, ext=ext: h.tensor_scalar(
                    out=strowb[0:1, :], in0=ext[0:1, 0:128], scalar1=1.0 / D,
                    scalar2=None, op0=ALU.mult), inc=True)
                dve_r = s.now("vector")
                s.op("scalar", lambda h: h.activation(
                    out=strow[0:1, 256:384], in_=strow[0:1, 128:256], func=AF.Ln,
                    scale=1.0 / D, bias=c_epsM[0:1, :]), waits=[("vector", dve_r)])
                s.op("scalar", lambda h: h.activation(
                    out=strow[0:1, 0:128], in_=strow[0:1, 256:384], func=AF.Exp,
                    scale=-0.5), inc=True)
                act_r = s.now("scalar")

                for t in range(8):
                    pq = pqA if t < 4 else pqB
                    s.op("tensor", lambda h, t=t, pq=pq: h.matmul(
                        pq[:, (t % 4) * 128:(t % 4) * 128 + 128],
                        csWb[0:1, t * 128:t * 128 + 128], strowb[0:1, :],
                        start=False, stop=True),
                        waits=[("scalar", act_r)] if t == 0 else ())
                pe_fix = s.op("tensor", lambda h: h.matmul(
                    pQM[:, 128:256], c_onesrow[0:1, :], strow[0:1, 0:128],
                    start=True, stop=True), inc=True)
                qkv_pe_done[g] = pe_fix
                s.op("vector", lambda h: h.tensor_copy(
                    out=rstdB[:], in_=pQM[:, 128:256]), inc=True,
                    waits=[("tensor", pe_fix), ("vector", scan_done[g - W])])
                s.barrier("tensor", ["vector"])

                for hh in range(2):
                    tq = pqA[:, hh * 128:hh * 128 + 128]
                    qf = qs_f[:, ao + hh * 128: ao + hh * 128 + 128]
                    q_ev = s.op("vector", lambda h, tq=tq, qf=qf: h.tensor_tensor(
                        out=qf, in0=tq, in1=rstdB[:], op=ALU.mult), inc=True)
                    s.op("scalar", lambda h, qf=qf: h.activation(
                        out=oth[:], in_=qf, func=AF.Exp, scale=-1.0),
                        waits=[("vector", q_ev)])
                    s.op("scalar", lambda h: h.activation(
                        out=oth[:], in_=oth[:], func=AF.Ln, scale=1.0,
                        bias=c_onesc[:]))
                    a_t = s.op("scalar", lambda h: h.activation(
                        out=oth[:], in_=oth[:], func=AF.Exp, scale=-1.0),
                        inc=True)
                    s.op("vector", lambda h, qf=qf: h.tensor_tensor(
                        out=qf, in0=qf, in1=oth[:], op=ALU.mult),
                        waits=[("scalar", a_t)])
                    s.op("vector", lambda h, qf=qf, hh=hh, ao=ao: h.tensor_copy(
                        out=qs_b[:, ao + hh * 128: ao + hh * 128 + 128], in_=qf),
                        inc=True)
                    s.barrier("scalar", ["vector"])
                for hh in range(2):
                    tk = pqA[:, (2 + hh) * 128:(2 + hh) * 128 + 128]
                    kf = kn_f[:, ao + hh * 128: ao + hh * 128 + 128]
                    k_ev = s.op("vector", lambda h, tk=tk, kf=kf: h.tensor_tensor(
                        out=kf, in0=tk, in1=rstdB[:], op=ALU.mult), inc=True)
                    k_sq = s.op("scalar", lambda h, kf=kf: h.activation(
                        out=qscr1[:], in_=kf, func=AF.Square),
                        inc=True, waits=[("vector", k_ev)])
                    k_mm = s.op("tensor", lambda h: h.matmul(
                        pQM[0:1, 0:128], c_onesc[:], qscr1[:],
                        start=True, stop=True), inc=True, waits=[("scalar", k_sq)])
                    s.op("scalar", lambda h: h.activation(
                        out=strow[0:1, 384:512], in_=pQM[0:1, 0:128], func=AF.Ln,
                        scale=1.0, bias=c_epsA[0:1, :]), waits=[("tensor", k_mm)])
                    k_rn = s.op("scalar", lambda h: h.activation(
                        out=strow[0:1, 384:512], in_=strow[0:1, 384:512],
                        func=AF.Exp, scale=-0.5), inc=True)
                    k_bc = s.op("tensor", lambda h: h.matmul(
                        pQM[:, 128:256], c_onesrow[0:1, :], strow[0:1, 384:512],
                        start=True, stop=True), inc=True, waits=[("scalar", k_rn)])
                    s.op("vector", lambda h, kf=kf: h.tensor_tensor(
                        out=kf, in0=kf, in1=pQM[:, 128:256], op=ALU.mult),
                        waits=[("tensor", k_bc)])
                    s.op("vector", lambda h, kf=kf, hh=hh, ao=ao: h.tensor_copy(
                        out=kn_b[:, ao + hh * 128: ao + hh * 128 + 128], in_=kf),
                        inc=True)
                    s.barrier("tensor", ["vector"])
                    s.barrier("scalar", ["vector"])
                for hh in range(2):
                    tv = pqB[:, hh * 128:hh * 128 + 128]
                    s.op("vector", lambda h, tv=tv, hh=hh, ao=ao: h.tensor_tensor(
                        out=v_f[:, ao + hh * 128: ao + hh * 128 + 128],
                        in0=tv, in1=rstdB[:], op=ALU.mult))
                s.op("vector", lambda h: h.tensor_tensor(
                    out=qscr2[:], in0=pqB[:, 256:384], in1=rstdB[:], op=ALU.mult))
                s.op("vector", lambda h, sg=sg: h.tensor_copy(
                    out=g1_b[:, sg * 128:sg * 128 + 128], in_=qscr2[:]))
                s.op("vector", lambda h, sg=sg, ext=ext: h.tensor_tensor(
                    out=rowbuf[32:33, sg * 128:sg * 128 + 128], in0=ext[32:33, 0:128],
                    in1=rstdB[32:33, 0:128], op=ALU.mult))
                v_done = s.op("vector", lambda h, sg=sg, ext=ext: h.tensor_tensor(
                    out=rowbuf[64:65, sg * 128:sg * 128 + 128],
                    in0=ext[64:65, 0:128], in1=rstdB[64:65, 0:128], op=ALU.mult),
                    inc=True)
                for i, base in enumerate((32, 64)):
                    kw = [("vector", v_done)] if i == 0 else ()
                    rb = lambda h, base=base, sg=sg: rowbuf[
                        base:base + 1, sg * 128:sg * 128 + 128]
                    s.op("scalar", lambda h, base=base, sg=sg: h.activation(
                        out=rowbuf[base:base + 1, sg * 128:sg * 128 + 128],
                        in_=rowbuf[base:base + 1, sg * 128:sg * 128 + 128],
                        func=AF.Exp, scale=-1.0), waits=kw)
                    s.op("scalar", lambda h, base=base, sg=sg: h.activation(
                        out=rowbuf[base:base + 1, sg * 128:sg * 128 + 128],
                        in_=rowbuf[base:base + 1, sg * 128:sg * 128 + 128],
                        func=AF.Ln, scale=1.0, bias=c_onesc[0:1, :]))
                    s.op("scalar", lambda h, base=base, sg=sg: h.activation(
                        out=rowbuf[base:base + 1, sg * 128:sg * 128 + 128],
                        in_=rowbuf[base:base + 1, sg * 128:sg * 128 + 128],
                        func=AF.Exp, scale=-1.0), inc=(i == 1))
                e_th = s.now("scalar")
                for i, base in enumerate((32, 64)):
                    kw = [("scalar", e_th)] if i == 0 else ()
                    s.op("vector", lambda h, base=base, sg=sg: h.tensor_scalar(
                        out=rowbuf[base:base + 1, sg * 128:sg * 128 + 128],
                        in0=rowbuf[base:base + 1, sg * 128:sg * 128 + 128],
                        scalar1=-2.0, scalar2=None, op0=ALU.mult),
                        inc=(i == 1), waits=kw)
                e_neg = s.now("vector")
                i = 0
                for hhb, base in enumerate((32, 64)):
                    for half in range(2):
                        kw = [("vector", e_neg)] if i == 0 else ()
                        s.op("tensor", lambda h, base=base, sg=sg, i=i, half=half:
                             h.matmul(
                                 pQM[0:64, 256 + i:257 + i],
                                 rowbuf[base:base + 1,
                                        sg * 128 + half * 64:
                                        sg * 128 + half * 64 + 64],
                                 c_ones[base:base + 1, 0:1],
                                 start=True, stop=True),
                             inc=(i == 3), waits=kw)
                        i += 1
                e_tr = s.now("tensor")
                s.op("vector", lambda h, sg=sg: h.tensor_copy(
                    out=eT[0:64, sg * 4: sg * 4 + 4], in_=pQM[0:64, 256:260]),
                    inc=True, waits=[("tensor", e_tr)])
                s.barrier("tensor", ["vector"])
                qkv_evac_done[g] = s.now("vector")

            def scan_group(g):
                sg = g % W
                ao = sg * 256
                qev = qkv_evac_done[g]
                guard = qev

                if g >= 1:
                    for hh in range(2):
                        s.op("vector", lambda h, hh=hh: h.tensor_copy(
                            out=s_op[:, hh * 128:hh * 128 + 128],
                            in_=pS[:, hh * 128:hh * 128 + 128]),
                            inc=(hh == 1),
                            waits=[("vector", qev)] if hh == 0 else ())

                for hh in range(2):
                    kb = kn_b[:, ao + hh * 128: ao + hh * 128 + 128]
                    qb = qs_b[:, ao + hh * 128: ao + hh * 128 + 128]
                    for half, (aktX, agtX, mKX, mGX) in enumerate(
                            ((aktL, agtL, c_mkKL, c_mkGL),
                             (aktH, agtH, c_mkKH, c_mkGH))):
                        ec = eT[0:64, sg * 4 + hh * 2 + half:
                                sg * 4 + hh * 2 + half + 1]
                        mm1 = s.op("tensor", lambda h, kb=kb, half=half: h.matmul(
                            pSC[0:64, 256:384], kb[:, half * 64:half * 64 + 64],
                            kb, start=True, stop=True), inc=True,
                            waits=[("vector", qev)] if (hh == 0 and half == 0)
                            else ())
                        dv1 = s.op("vector", lambda h, hh=hh, ec=ec, aktX=aktX,
                                   mKX=mKX: h.scalar_tensor_tensor(
                                       out=aktX[:, hh * 128:hh * 128 + 128],
                                       in0=pSC[0:64, 256:384], scalar=ec,
                                       in1=mKX[:], op0=ALU.mult, op1=ALU.mult),
                                   inc=True, waits=[("tensor", mm1)])
                        mm2 = s.op("tensor", lambda h, kb=kb, qb=qb, half=half:
                                   h.matmul(
                                       pSC[0:64, 256:384],
                                       kb[:, half * 64:half * 64 + 64], qb,
                                       start=True, stop=True), inc=True,
                                   waits=[("vector", dv1)])
                        s.op("vector", lambda h, hh=hh, ec=ec, agtX=agtX,
                             mGX=mGX: h.scalar_tensor_tensor(
                                 out=agtX[:, hh * 128:hh * 128 + 128],
                                 in0=pSC[0:64, 256:384], scalar=ec, in1=mGX[:],
                                 op0=ALU.mult, op1=ALU.mult),
                             inc=True, waits=[("tensor", mm2)])
                        s.barrier("tensor", ["vector"])

                for hh in range(2):
                    s.op("vector", lambda h, hh=hh, ao=ao: h.tensor_tensor(
                        out=oscr[:, 0:128],
                        in0=v_f[:, ao + hh * 128:ao + hh * 128 + 128],
                        in1=kn_f[:, ao + hh * 128:ao + hh * 128 + 128],
                        op=ALU.subtract))
                    s.op("vector", lambda h, hh=hh: h.tensor_scalar(
                        out=wbm[:, hh * 128:hh * 128 + 128], in0=oscr[:, 0:128],
                        scalar1=c_negw[:, hh:hh + 1], scalar2=c_wb[:, hh:hh + 1],
                        op0=ALU.mult, op1=ALU.add), inc=True)
                    base = 32 + hh * 32
                    ebc = s.op("tensor", lambda h, sg=sg, base=base: h.matmul(
                        pQM[:, 128:256], c_ones[base:base + 1, 0:128],
                        rowbuf[base:base + 1, sg * 128:sg * 128 + 128],
                        start=True, stop=True), inc=True,
                        waits=[("vector", guard)] if hh == 0 else ())
                    kes = s.op("vector", lambda h, hh=hh, ao=ao: h.tensor_tensor(
                        out=kescr[:],
                        in0=kn_f[:, ao + hh * 128:ao + hh * 128 + 128],
                        in1=pQM[:, 128:256], op=ALU.mult), inc=True,
                        waits=[("tensor", ebc)])
                    ktr = 0
                    for half in range(2):
                        ktr = s.op("tensor", lambda h, half=half: h.transpose(
                            pSMb[0:64, half * 128:half * 128 + 128],
                            kescr[:, half * 64:half * 64 + 64], c_id[:]),
                            inc=True,
                            waits=[("vector", kes)] if half == 0 else ())
                    for half, ketX in enumerate((ketL, ketH)):
                        s.op("vector", lambda h, hh=hh, half=half, ketX=ketX:
                             h.tensor_copy(
                                 out=ketX[:, hh * 128:hh * 128 + 128],
                                 in_=pSMb[0:64, half * 128:half * 128 + 128]),
                             inc=(half == 1),
                             waits=[("tensor", ktr)] if half == 0 else ())
                    s.barrier("tensor", ["vector"])
                prep_done = s.now("vector")

                for hh in range(2):
                    s.op("tensor", lambda h, hh=hh, ao=ao: h.matmul(
                        pA[:, hh * 128:hh * 128 + 128],
                        s_op[:, hh * 128:hh * 128 + 128],
                        kn_b[:, ao + hh * 128:ao + hh * 128 + 128],
                        start=(hh == 0), stop=True),
                        waits=[("vector", prep_done)] if hh == 0 else ())
                    s.op("tensor", lambda h, hh=hh, ao=ao: h.matmul(
                        pB[:, hh * 128:hh * 128 + 128],
                        s_op[:, hh * 128:hh * 128 + 128],
                        qs_b[:, ao + hh * 128:ao + hh * 128 + 128],
                        start=(hh == 0), stop=True))
                _ = s.now("tensor")

                last_stk = s.now("vector")
                for j in range(NCH):
                    m = j // 2
                    c0 = j * CH
                    if j > 0:
                        KL = min(16 * j, 64)
                        KH = 16 * j - KL
                        for hh in range(2):
                            s.op("tensor", lambda h, hh=hh, KL=KL, c0=c0: h.matmul(
                                pA[:, hh * 128 + c0: hh * 128 + c0 + CH],
                                vnstkL[0:KL, hh * 128:hh * 128 + 128],
                                aktL[0:KL, hh * 128 + c0:hh * 128 + c0 + CH],
                                start=False, stop=True),
                                waits=[("vector", last_stk)] if hh == 0 else ())
                            if KH > 0:
                                s.op("tensor", lambda h, hh=hh, KH=KH, c0=c0:
                                     h.matmul(
                                         pA[:, hh * 128 + c0:
                                            hh * 128 + c0 + CH],
                                         vnstkH[0:KH, hh * 128:hh * 128 + 128],
                                         aktH[0:KH,
                                              hh * 128 + c0:hh * 128 + c0 + CH],
                                         start=False, stop=True))
                    corr_pe = s.now("tensor")

                    s.op("vector", lambda h, c0=c0: h.tensor_copy(
                        out=scrA[:, 0:32],
                        in_=blk2(pA, c0, CH, 128)),
                        waits=[("tensor", corr_pe)])
                    kh_ev = s.op("vector", lambda h: h.tensor_tensor(
                        out=scrA[:, 32:64], in0=scrA[:, 0:32], in1=scrA[:, 0:32],
                        op=ALU.mult), inc=True)
                    st1 = s.op("tensor", lambda h: h.matmul(
                        pSM[:, 128:192], c_ones[:], scrA[:, 0:64],
                        start=True, stop=True), inc=True, waits=[("vector", kh_ev)])
                    s.op("vector", lambda h: h.tensor_copy(
                        out=scr2[:], in_=pSM[:, 128:160]), waits=[("tensor", st1)])
                    s.op("vector", lambda h: h.tensor_tensor(
                        out=scr1[:], in0=scr2[:], in1=scr2[:], op=ALU.mult))
                    vpre = s.op("vector", lambda h: h.scalar_tensor_tensor(
                        out=scr1[:], in0=scr1[:], scalar=-1.0 / DH,
                        in1=pSM[:, 160:192], op0=ALU.mult, op1=ALU.add), inc=True)
                    s.op("scalar", lambda h: h.activation(
                        out=scr2[:], in_=scr1[:], func=AF.Ln, scale=1.0 / DH,
                        bias=c_epsA[:]), waits=[("vector", vpre)])
                    a_rs = s.op("scalar", lambda h: h.activation(
                        out=rstdc[:], in_=scr2[:], func=AF.Exp, scale=-0.5),
                        inc=True)
                    s.op("vector", lambda h: h.scalar_tensor_tensor(
                        out=khh[:], in0=pSM[:, 128:160], scalar=-1.0 / DH,
                        in1=scrA[:, 0:32], op0=ALU.mult, op1=ALU.add),
                        waits=[("scalar", a_rs)])
                    s.op("vector", lambda h: h.tensor_tensor(
                        out=khh[:], in0=khh[:], in1=rstdc[:], op=ALU.mult))
                    s.op("vector", lambda h: h.tensor_tensor(
                        out=scrA[:, 64:96], in0=khh[:], in1=c_w2B[:], op=ALU.mult))
                    s.op("vector", lambda h, c0=c0: h.tensor_tensor(
                        out=scrA[:, 64:96], in0=scrA[:, 64:96],
                        in1=blk2(wbm, c0, CH, 128), op=ALU.add))
                    gw_d = s.op("vector", lambda h: h.tensor_tensor(
                        out=scrA[:, 96:128], in0=scrA[:, 64:96], in1=khh[:],
                        op=ALU.mult), inc=True)
                    st2 = s.op("tensor", lambda h: h.matmul(
                        pSM[:, 192:256], c_ones[:], scrA[:, 64:128],
                        start=True, stop=True), inc=True, waits=[("vector", gw_d)])
                    s.op("vector", lambda h: h.scalar_tensor_tensor(
                        out=scr1[:], in0=khh[:], scalar=1.0 / DH,
                        in1=pSM[:, 224:256], op0=ALU.mult, op1=ALU.mult),
                        waits=[("tensor", st2)])
                    s.op("vector", lambda h: h.tensor_tensor(
                        out=scr1[:], in0=scrA[:, 64:96], in1=scr1[:],
                        op=ALU.subtract))
                    s.op("vector", lambda h: h.scalar_tensor_tensor(
                        out=scr1[:], in0=pSM[:, 192:224], scalar=-1.0 / DH,
                        in1=scr1[:], op0=ALU.mult, op1=ALU.add))
                    s.op("vector", lambda h: h.tensor_tensor(
                        out=scr2[:], in0=scr1[:], in1=rstdc[:], op=ALU.mult))
                    po = (j % 2) * 16
                    s.op("vector", lambda h, po=po: h.tensor_copy(
                        out=vnstg[:, po:po + 16], in_=scr2[:, 0:16]))
                    vn_d = s.op("vector", lambda h, po=po: h.tensor_copy(
                        out=vnstg[:, 32 + po:32 + po + 16], in_=scr2[:, 16:32]),
                        inc=True)

                    vstk = vnstkL if m < 2 else vnstkH
                    r0 = 32 * (m % 2)
                    nr = 16 if j % 2 == 0 else 32
                    src0 = 0 if j % 2 == 0 else 0
                    trs = 0
                    for hh in range(2):
                        trs = s.op("tensor", lambda h, hh=hh, r0=r0, nr=nr:
                                   h.transpose(
                                       pSMb[r0:r0 + nr,
                                            hh * 128:hh * 128 + 128],
                                       vnstg[:, hh * 32:hh * 32 + nr], c_id[:]),
                                   inc=True,
                                   waits=[("vector", vn_d)] if hh == 0 else ())
                    s.op("vector", lambda h, r0=r0, nr=nr, vstk=vstk:
                         h.tensor_copy(
                             out=vstk[r0:r0 + nr, 0:256],
                             in_=pSMb[r0:r0 + nr, 0:256]),
                         inc=True, waits=[("tensor", trs)])
                    last_stk = s.now("vector")
                    s.barrier("tensor", ["vector"])
                    if globals_pull[0] is not None:
                        globals_pull[0](1)

                for hh in range(2):
                    s.op("tensor", lambda h, hh=hh: h.matmul(
                        pB[:, hh * 128:hh * 128 + 128],
                        vnstkL[:, hh * 128:hh * 128 + 128],
                        agtL[:, hh * 128:hh * 128 + 128], start=False, stop=False),
                        waits=[("vector", last_stk)] if hh == 0 else ())
                    s.op("tensor", lambda h, hh=hh: h.matmul(
                        pB[:, hh * 128:hh * 128 + 128],
                        vnstkH[:, hh * 128:hh * 128 + 128],
                        agtH[:, hh * 128:hh * 128 + 128], start=False, stop=True))
                    s.op("tensor", lambda h, hh=hh, g=g: h.matmul(
                        pS[:, hh * 128:hh * 128 + 128],
                        ketL[:, hh * 128:hh * 128 + 128],
                        vnstkL[:, hh * 128:hh * 128 + 128],
                        start=(g == 0 and hh == 0), stop=False))
                    s.op("tensor", lambda h, hh=hh, g=g: h.matmul(
                        pS[:, hh * 128:hh * 128 + 128],
                        ketH[:, hh * 128:hh * 128 + 128],
                        vnstkH[:, hh * 128:hh * 128 + 128],
                        start=False, stop=True))
                qh_pe = s.now("tensor")

                for hh in range(2):
                    qf = qs_f[:, ao + hh * 128: ao + hh * 128 + 128]
                    s.op("vector", lambda h, hh=hh: h.tensor_copy(
                        out=oscr[:, 0:128], in_=pB[:, hh * 128:hh * 128 + 128]),
                        waits=[("tensor", qh_pe)] if hh == 0 else ())
                    o_ev = s.op("vector", lambda h: h.tensor_tensor(
                        out=oscr[:, 128:256], in0=oscr[:, 0:128],
                        in1=oscr[:, 0:128], op=ALU.mult), inc=True)
                    o_mm = s.op("tensor", lambda h: h.matmul(
                        pSM[:, 256:512], c_ones[:], oscr[:, 0:256],
                        start=True, stop=True), inc=True, waits=[("vector", o_ev)])
                    s.op("vector", lambda h: h.tensor_copy(
                        out=oo[:], in_=pSM[:, 256:384]), waits=[("tensor", o_mm)])
                    s.op("vector", lambda h: h.tensor_tensor(
                        out=oo[:], in0=oo[:], in1=oo[:], op=ALU.mult))
                    o_v = s.op("vector", lambda h: h.scalar_tensor_tensor(
                        out=oo[:], in0=oo[:], scalar=-1.0 / DH,
                        in1=pSM[:, 384:512], op0=ALU.mult, op1=ALU.add), inc=True)
                    s.op("scalar", lambda h: h.activation(
                        out=oo2[:], in_=oo[:], func=AF.Ln, scale=1.0 / DH,
                        bias=c_epsA[:]), waits=[("vector", o_v)])
                    o_rs = s.op("scalar", lambda h: h.activation(
                        out=oo2[:], in_=oo2[:], func=AF.Exp, scale=-0.5), inc=True)
                    s.op("vector", lambda h: h.scalar_tensor_tensor(
                        out=oo[:], in0=pSM[:, 256:384], scalar=-1.0 / DH,
                        in1=oscr[:, 0:128], op0=ALU.mult, op1=ALU.add),
                        waits=[("scalar", o_rs)])
                    s.op("vector", lambda h: h.tensor_tensor(
                        out=oo[:], in0=oo[:], in1=oo2[:], op=ALU.mult))
                    s.op("vector", lambda h, hh=hh: h.tensor_scalar(
                        out=oo[:], in0=oo[:], scalar1=c_lnw[:, hh:hh + 1],
                        scalar2=c_lnb[:, hh:hh + 1], op0=ALU.mult, op1=ALU.add))
                    o_d = s.op("vector", lambda h, qf=qf: h.tensor_tensor(
                        out=oo[:], in0=oo[:], in1=qf, op=ALU.add), inc=True)
                    g_mm = s.op("tensor", lambda h, hh=hh, sg=sg: h.matmul(
                        pQM[:, 384:512], wg2b[:, hh * 128:hh * 128 + 128],
                        g1_b[:, sg * 128:sg * 128 + 128], start=True, stop=True),
                        inc=True, waits=[("vector", o_d), ("vector", guard)])
                    s.op("scalar", lambda h: h.activation(
                        out=oth[:], in_=pQM[:, 384:512], func=AF.Exp, scale=-1.0),
                        waits=[("tensor", g_mm)])
                    s.op("scalar", lambda h: h.activation(
                        out=oth[:], in_=oth[:], func=AF.Ln, scale=1.0,
                        bias=c_onesc[:]))
                    g_th = s.op("scalar", lambda h: h.activation(
                        out=oth[:], in_=oth[:], func=AF.Exp, scale=-1.0), inc=True)
                    og_d = s.op("vector", lambda h: h.tensor_tensor(
                        out=oo[:], in0=oo[:], in1=oth[:], op=ALU.mult), inc=True,
                        waits=[("scalar", g_th)])
                    og_sq = s.op("scalar", lambda h: h.activation(
                        out=oo2[:], in_=oo[:], func=AF.Square), inc=True,
                        waits=[("vector", og_d)])
                    r_mm = s.op("tensor", lambda h: h.matmul(
                        pQM[0:1, 128:256], c_onesc[:], oo2[:],
                        start=True, stop=True), inc=True,
                        waits=[("scalar", og_sq)])
                    s.op("scalar", lambda h: h.activation(
                        out=strow[0:1, 512:640], in_=pQM[0:1, 128:256], func=AF.Ln,
                        scale=1.0 / DH, bias=c_epsA[0:1, :]),
                        waits=[("tensor", r_mm)])
                    r_rs = s.op("scalar", lambda h: h.activation(
                        out=strow[0:1, 512:640], in_=strow[0:1, 512:640],
                        func=AF.Exp, scale=-0.5), inc=True)
                    r_bc = s.op("tensor", lambda h: h.matmul(
                        pQM[:, 128:256], c_onesrow[0:1, :], strow[0:1, 512:640],
                        start=True, stop=True), inc=True, waits=[("scalar", r_rs)])
                    s.op("vector", lambda h: h.tensor_tensor(
                        out=oo[:], in0=oo[:], in1=pQM[:, 128:256], op=ALU.mult),
                        waits=[("tensor", r_bc)])
                    ob_d = s.op("vector", lambda h, hh=hh: h.tensor_scalar(
                        out=obuf[:, hh * 128:hh * 128 + 128], in0=oo[:],
                        scalar1=c_nw[:, hh:hh + 1], scalar2=None, op0=ALU.mult),
                        inc=True)
                    s.barrier("tensor", ["vector"])
                    s.barrier("scalar", ["vector"])
                    s.op("sync", lambda h, g=g, hh=hh: h.dma_start(
                        out=o_st5[g // GQ][g % 8, (g % GQ) // 8, hh, :, :],
                        in_=obuf[:, hh * 128:hh * 128 + 128]), inc=True,
                        waits=[("vector", ob_d)])
                scan_done[g] = s.now("vector")
                s.barrier("vector", ["sync"])

            globals_pull = [None]

            def emit_collective(q):
                ns = s.now("sync")
                import os as _os
                if _os.environ.get("KERN_NO_CC"):
                    s.op("gpsimd", lambda h, q=q: h.dma_start(
                        out=o_gath[q].ap(), in_=o_stage[q].ap()).then_inc(
                            sems["cc"], 1), inc=False, waits=[("sync", ns)])
                else:
                    def do_cc(h, q=q):
                        with allow_ici_all_to_all():
                            return h.collective_compute(
                                "AllToAll", ALU.bypass,
                                replica_groups=[list(range(NCORE))],
                                ins=[o_stage[q].ap().opt()],
                                outs=[o_gath[q].ap().opt()]).then_inc(
                                    sems["cc"], 1)
                    s.op("gpsimd", do_cc, inc=False, waits=[("sync", ns)])

            def c_wave(q):
                t0q = q * TS
                for _ in range(12):
                    yield
                for src in range(NCORE):
                    for hh in range(2):
                        dti = src * 2 + hh
                        wv = [("cc", q + 1), ("tensor", s.now("tensor")),
                              ("pool", s.now("pool"))] \
                            if (src == 0 and hh == 0) else ()
                        s.op("sync", lambda h, src=src, hh=hh, dti=dti, q=q:
                             h.dma_start(
                                 out=oslab[:, dti * TS:dti * TS + TS].rearrange(
                                     "p (l c) -> p l c", l=LGQ),
                                 in_=o_gt5[q][src, :, hh].rearrange(
                                     "l p c -> p l c")), inc=True, waits=wv)
                s.op("gpsimd", lambda h, t0q=t0q: h.dma_start(
                    out=xinb[:].rearrange("p (d c) -> p d c", d=16),
                    in_=xres[:, :, t0q:t0q + TS]), inc=True)
                ldxi = s.now("gpsimd")
                lds = s.now("sync")
                yield

                stat_t = {}
                wo_ld = {}
                pe_done = {}
                for t in range(2):
                    s.op("sync", lambda h, t=t: h.dma_start(
                        out=wrng[:, (t % 4) * 2048:(t % 4) * 2048 + 2048],
                        in_=wobf_dram.ap()[t]), inc=True,
                        waits=[pf_wait("wo", t)])
                    wo_ld[t] = s.now("sync")
                for t in range(16):
                    pq0 = 0
                    for d in range(16):
                        wv = [("sync", wo_ld[t]), ("sync", lds)] \
                            if d == 0 else ()
                        if d == 0 and t >= 1:
                            wv = list(wv) + [("scalar", stat_t[t - 1])]
                        s.op("tensor", lambda h, t=t, d=d: h.matmul(
                            pC[:, 0:TS],
                            wrng[:, (t % 4) * 2048 + d * 128:
                                 (t % 4) * 2048 + d * 128 + 128],
                            oslab[:, d * TS:d * TS + TS],
                            start=(d == 0), stop=(d == 15)), waits=wv)
                    pe_t = s.now("tensor")
                    pe_done[t] = pe_t
                    if t + 2 < 16:
                        s.op("sync", lambda h, t=t: h.dma_start(
                            out=wrng[:, ((t + 2) % 4) * 2048:
                                     ((t + 2) % 4) * 2048 + 2048],
                            in_=wobf_dram.ap()[t + 2]), inc=True,
                            waits=[("tensor", pe_t), pf_wait("wo", t + 2)])
                        wo_ld[t + 2] = s.now("sync")
                    s.op("scalar", lambda h, t=t: h.copy(
                        out=zdn[:, t * TS:t * TS + TS], in_=pC[:, 0:TS]),
                        waits=[("tensor", pe_t)])
                    s.op("scalar", lambda h, t=t: h.copy(
                        out=zb2[:, (t % 2) * TS:(t % 2) * TS + TS],
                        in_=pC[:, 0:TS]))
                    zev = s.op("scalar", lambda h, t=t: h.activation(
                        out=zq2[:, (t % 2) * TS:(t % 2) * TS + TS],
                        in_=zb2[:, (t % 2) * TS:(t % 2) * TS + TS],
                        func=AF.Square), inc=True)
                    s.op("tensor", lambda h, t=t: h.matmul(
                        pC[0:1, 256:256 + TS], c_onesb[:],
                        zb2[:, (t % 2) * TS:(t % 2) * TS + TS],
                        start=True, stop=True), waits=[("scalar", zev)])
                    s.op("tensor", lambda h, t=t: h.matmul(
                        pC[32:33, 256:256 + TS], c_onesb[:, 0:1],
                        zq2[:, (t % 2) * TS:(t % 2) * TS + TS],
                        start=True, stop=True))
                    smm = s.now("tensor")
                    s.op("scalar", lambda h: h.copy(
                        out=stmp[0:1, 0:TS], in_=pC[0:1, 256:256 + TS]),
                        waits=[("tensor", smm)])
                    sr = s.op("scalar", lambda h: h.copy(
                        out=stmp[0:1, TS:2 * TS], in_=pC[32:33, 256:256 + TS]),
                        inc=True)
                    if t == 0:
                        s.op("vector", lambda h: h.tensor_copy(
                            out=sac1[0:1, :], in_=stmp[0:1, 0:TS]),
                            inc=True, waits=[("scalar", sr)])
                        s.op("vector", lambda h: h.tensor_copy(
                            out=sac2[0:1, :], in_=stmp[0:1, TS:2 * TS]),
                            inc=True)
                    else:
                        s.op("vector", lambda h: h.tensor_tensor(
                            out=sac1[0:1, :], in0=sac1[0:1, :],
                            in1=stmp[0:1, 0:TS], op=ALU.add),
                            inc=True, waits=[("scalar", sr)])
                        s.op("vector", lambda h: h.tensor_tensor(
                            out=sac2[0:1, :], in0=sac2[0:1, :],
                            in1=stmp[0:1, TS:2 * TS], op=ALU.add), inc=True)
                    stat_t[t] = sr
                    yield

                def ln_rows_bcast(st):
                    pl = s.op("vector", lambda h: h.tensor_tensor(
                        out=stmp[0:1, 2 * TS:3 * TS], in0=sac1[0:1, :],
                        in1=sac1[0:1, :], op=ALU.mult), inc=True)
                    pl = s.op("vector", lambda h: h.scalar_tensor_tensor(
                        out=stmp[0:1, 2 * TS:3 * TS],
                        in0=stmp[0:1, 2 * TS:3 * TS], scalar=-1.0 / D,
                        in1=sac2[0:1, :], op0=ALU.mult, op1=ALU.add), inc=True)
                    s.op("scalar", lambda h: h.activation(
                        out=stmp[0:1, 3 * TS:4 * TS],
                        in_=stmp[0:1, 2 * TS:3 * TS], func=AF.Ln,
                        scale=1.0 / D, bias=c_epsM[0:1, :]),
                        waits=[("vector", pl)])
                    s.op("scalar", lambda h: h.activation(
                        out=stmp[0:1, 3 * TS:4 * TS],
                        in_=stmp[0:1, 3 * TS:4 * TS], func=AF.Exp, scale=-0.5))
                    mr = s.op("scalar", lambda h: h.activation(
                        out=stmp[0:1, 2 * TS:3 * TS], in_=sac1[0:1, :],
                        func=AF.Copy, scale=1.0 / D), inc=True)
                    b1 = s.op("tensor", lambda h: h.matmul(
                        pC[:, 256:256 + TS], c_onesrow[0:1, :],
                        stmp[0:1, 3 * TS:4 * TS], start=True, stop=True),
                        waits=[("scalar", mr)])
                    b1 = s.op("tensor", lambda h: h.matmul(
                        pC[:, 0:TS], c_onesrow[0:1, :],
                        stmp[0:1, 2 * TS:3 * TS], start=True, stop=True))
                    b1 = s.now("tensor")
                    s.op("scalar", lambda h, st=st: h.copy(
                        out=rB1[:, st * TS:st * TS + TS], in_=pC[:, 256:256 + TS]),
                        waits=[("tensor", b1), ("pool", s.now("pool"))])
                    bd = s.op("scalar", lambda h, st=st: h.copy(
                        out=csc1[:, st * TS:st * TS + TS], in_=pC[:, 0:TS]),
                        inc=True)
                    return bd

                bd = ln_rows_bcast(0)
                yield

                for t in range(16):
                    p1 = s.op("gpsimd", lambda h, t=t: h.tensor_tensor(
                        out=csc2[:, 0:TS], in0=zdn[:, t * TS:t * TS + TS],
                        in1=csc1[:, 0:TS], op=ALU.subtract),
                        inc=False, waits=[("scalar", bd)] if t == 0 else (), sem="pool")
                    s.op("gpsimd", lambda h: h.tensor_tensor(
                        out=csc2[:, 0:TS], in0=csc2[:, 0:TS], in1=rB1[:, 0:TS],
                        op=ALU.mult), sem="pool")
                    x1d = s.op("gpsimd", lambda h, t=t: h.tensor_tensor(
                        out=x1sl[:, t * TS:t * TS + TS], in0=csc2[:, 0:TS],
                        in1=xinb[:, t * TS:t * TS + TS], op=ALU.add), inc=True,
                        waits=[("gpsimd", ldxi)] if t == 0 else (), sem="pool")
                    zev = s.op("scalar", lambda h, t=t: h.activation(
                        out=zq2[:, 0:TS], in_=x1sl[:, t * TS:t * TS + TS],
                        func=AF.Square), inc=True, waits=[("pool", x1d)])
                    s.op("tensor", lambda h, t=t: h.matmul(
                        pC[0:1, 256:256 + TS], c_onesb[:],
                        x1sl[:, t * TS:t * TS + TS], start=True, stop=True),
                        waits=[("scalar", zev), ("scalar", s.now("scalar"))])
                    s.op("tensor", lambda h, t=t: h.matmul(
                        pC[32:33, 256:256 + TS], c_onesb[:, 0:1],
                        zq2[:, 0:TS], start=True, stop=True))
                    smm = s.now("tensor")
                    s.op("scalar", lambda h: h.copy(
                        out=stmp[0:1, 0:TS], in_=pC[0:1, 256:256 + TS]),
                        waits=[("tensor", smm)])
                    sr = s.op("scalar", lambda h: h.copy(
                        out=stmp[0:1, TS:2 * TS], in_=pC[32:33, 256:256 + TS]),
                        inc=True)
                    if dbg16 is not None and q == NQ - 1:
                        s.op("vector", lambda h, t=t: h.tensor_copy(
                            out=dbg16[0:1, t:t + 1],
                            in_=stmp[0:1, TS:TS + 1]),
                            inc=True, waits=[("scalar", sr)])
                        s.op("vector", lambda h, t=t: h.tensor_copy(
                            out=dbg16[0:1, 16 + t:16 + t + 1],
                            in_=zq2[0:1, 0:1]), inc=True)
                    op = ALU.add if t > 0 else ALU.bypass
                    if t == 0:
                        s.op("vector", lambda h: h.tensor_copy(
                            out=sac1[0:1, :], in_=stmp[0:1, 0:TS]),
                            inc=True, waits=[("scalar", sr)])
                        s.op("vector", lambda h: h.tensor_copy(
                            out=sac2[0:1, :], in_=stmp[0:1, TS:2 * TS]),
                            inc=True)
                    else:
                        s.op("vector", lambda h: h.tensor_tensor(
                            out=sac1[0:1, :], in0=sac1[0:1, :],
                            in1=stmp[0:1, 0:TS], op=ALU.add),
                            inc=True, waits=[("scalar", sr)])
                        s.op("vector", lambda h: h.tensor_tensor(
                            out=sac2[0:1, :], in0=sac2[0:1, :],
                            in1=stmp[0:1, TS:2 * TS], op=ALU.add), inc=True)
                    if t % 4 == 3:
                        yield

                bd = ln_rows_bcast(1)
                import os as _os2
                if _os2.environ.get("KERN_DBG") and q == NQ - 1:
                    s.op("sync", lambda h: h.dma_start(
                        out=dbg[0:1, 0:TS], in_=sac1[0:1, :]), inc=True,
                        waits=[("scalar", bd)])
                    s.op("sync", lambda h: h.dma_start(
                        out=dbg[0:1, 512:512 + TS], in_=sac2[0:1, :]), inc=True)
                    s.op("sync", lambda h: h.dma_start(
                        out=dbg[0:1, 1024:1024 + TS],
                        in_=rB1[0:1, TS:2 * TS]), inc=True)
                    s.op("sync", lambda h: h.dma_start(
                        out=dbg[0:1, 1536:1536 + TS],
                        in_=csc1[0:1, TS:2 * TS]), inc=True)
                    s.op("gpsimd", lambda h: h.dma_start(
                        out=dbg[0:1, 2048:2048 + TS], in_=zq2[0:1, 0:TS]),
                        inc=True)
                    s.op("gpsimd", lambda h: h.dma_start(
                        out=dbg[0:1, 2560:2560 + TS],
                        in_=x1sl[0:1, 15 * TS:15 * TS + TS]), inc=True)
                yield
                for t in range(16):
                    p1 = s.op("gpsimd", lambda h, t=t: h.tensor_tensor(
                        out=csc2[:, 0:TS], in0=x1sl[:, t * TS:t * TS + TS],
                        in1=csc1[:, TS:2 * TS], op=ALU.subtract),
                        inc=False, waits=[("scalar", bd)] if t == 0 else (), sem="pool")
                    s.op("gpsimd", lambda h, t=t: h.tensor_tensor(
                        out=x2sl[:, t * TS:t * TS + TS], in0=csc2[:, 0:TS],
                        in1=rB1[:, TS:2 * TS], op=ALU.mult), inc=(t == 15),
                        sem="pool")
                    if t % 8 == 7:
                        yield
                x2_d = s.now("pool")

                guv_done = {}
                for ht in range(48):
                    sl4 = ht % 2
                    wv = [("pool", x2_d), pf_wait("g", ht)]
                    if ht >= 2:
                        wv.append(("tensor", guv_done[ht - 2]))
                    s.op("sync", lambda h, ht=ht, sl4=sl4: h.dma_start(
                        out=g8s[:, sl4 * 2048:sl4 * 2048 + 2048],
                        in_=g8_dram.ap()[:, ht * 2048:ht * 2048 + 2048]),
                        inc=True, waits=wv)
                    s.op("sync", lambda h, ht=ht, sl4=sl4: h.dma_start(
                        out=u8s[:, sl4 * 2048:sl4 * 2048 + 2048],
                        in_=u8_dram.ap()[:, ht * 2048:ht * 2048 + 2048]),
                        inc=True, waits=[pf_wait("u", ht)])
                    wgu = s.now("sync")
                    yield

                    for d in range(16):
                        wv = [("sync", wgu)] if d == 0 else ()
                        s.op("tensor", lambda h, d=d, sl4=sl4: h.matmul(
                            pC[:, 0:TS],
                            g8s[:, sl4 * 2048 + d * 128:
                                sl4 * 2048 + d * 128 + 128],
                            x2sl[:, d * TS:d * TS + TS],
                            start=(d == 0), stop=(d == 15)), waits=wv)
                    for d in range(16):
                        s.op("tensor", lambda h, d=d, sl4=sl4: h.matmul(
                            pC[:, 256:256 + TS],
                            u8s[:, sl4 * 2048 + d * 128:
                                sl4 * 2048 + d * 128 + 128],
                            x2sl[:, d * TS:d * TS + TS],
                            start=(d == 0), stop=(d == 15)))
                    gu_pe = s.now("tensor")
                    guv_done[ht] = gu_pe
                    s.op("scalar", lambda h, ht=ht: h.activation(
                        out=gth[:], in_=pC[:, 0:TS], func=AF.Exp,
                        scale=ngs[:, ht:ht + 1]), waits=[("tensor", gu_pe)])
                    s.op("scalar", lambda h: h.activation(
                        out=gth[:], in_=gth[:], func=AF.Ln, scale=1.0,
                        bias=c_onesc[:]))
                    s.op("scalar", lambda h: h.activation(
                        out=gth[:], in_=gth[:], func=AF.Exp, scale=-1.0))
                    s.op("scalar", lambda h, ht=ht: h.activation(
                        out=gxh[:], in_=pC[:, 0:TS], func=AF.Copy,
                        scale=gscb[:, ht:ht + 1]))
                    guc = s.op("scalar", lambda h, ht=ht: h.activation(
                        out=guv[:], in_=pC[:, 256:256 + TS], func=AF.Copy,
                        scale=uscb[:, ht:ht + 1]), inc=True)
                    s.op("gpsimd", lambda h: h.tensor_tensor(
                        out=gvv[:], in0=gth[:], in1=gxh[:], op=ALU.mult),
                        inc=False, waits=[("scalar", guc)] + (
                            [("tensor", s.now("tensor"))] if ht == 0 else []),
                        sem="pool")
                    s.op("gpsimd", lambda h, ht=ht: h.tensor_tensor(
                        out=fusl[:, ht * TS:ht * TS + TS], in0=gvv[:],
                        in1=guv[:], op=ALU.mult), inc=True, sem="pool")
                    s.barrier("tensor", ["scalar"])
                    yield
                fu_d = s.now("pool")

                dn_done = {}
                for t in range(16):
                    sl2 = t % 2
                    wv = [("pool", fu_d), pf_wait("d", t)]
                    if t >= 1:
                        wv.append(("tensor", dn_done[t - 1]))
                    s.op("sync", lambda h, t=t: h.dma_start(
                        out=d8s[:],
                        in_=d8_dram.ap()[:, t * 6144:t * 6144 + 6144]),
                        inc=True, waits=wv)
                    dw_d = s.now("sync")
                    yield
                    for half in range(2):
                        for dd in range(24):
                            ddv = half * 24 + dd
                            wv = [("sync", dw_d)] \
                                if (half == 0 and dd == 0) else ()
                            s.op("tensor", lambda h, ddv=ddv: h.matmul(
                                pC[:, 0:TS],
                                d8s[:, ddv * 128:ddv * 128 + 128],
                                fusl[:, ddv * TS:ddv * TS + TS],
                                start=(ddv == 0), stop=(ddv == 47)), waits=wv)
                        yield
                    dn_pe = s.now("tensor")
                    dn_done[t] = dn_pe
                    s.op("scalar", lambda h, t=t: h.activation(
                        out=zdn[:, t * TS:t * TS + TS], in_=pC[:, 0:TS],
                        func=AF.Copy, scale=dscb[:, t:t + 1]),
                        waits=[("tensor", dn_pe)])
                    s.op("scalar", lambda h, t=t: h.copy(
                        out=zb2[:, (t % 2) * TS:(t % 2) * TS + TS],
                        in_=zdn[:, t * TS:t * TS + TS]))
                    zev = s.op("scalar", lambda h, t=t: h.activation(
                        out=zq2[:, (t % 2) * TS:(t % 2) * TS + TS],
                        in_=zb2[:, (t % 2) * TS:(t % 2) * TS + TS],
                        func=AF.Square), inc=True)
                    s.op("tensor", lambda h, t=t: h.matmul(
                        pC[0:1, 256:256 + TS], c_onesb[:],
                        zb2[:, (t % 2) * TS:(t % 2) * TS + TS],
                        start=True, stop=True),
                        waits=[("scalar", zev), ("scalar", s.now("scalar"))])
                    s.op("tensor", lambda h, t=t: h.matmul(
                        pC[32:33, 256:256 + TS], c_onesb[:, 0:1],
                        zq2[:, (t % 2) * TS:(t % 2) * TS + TS],
                        start=True, stop=True))
                    smm = s.now("tensor")
                    s.op("scalar", lambda h: h.copy(
                        out=stmp[0:1, 0:TS], in_=pC[0:1, 256:256 + TS]),
                        waits=[("tensor", smm)])
                    sr = s.op("scalar", lambda h: h.copy(
                        out=stmp[0:1, TS:2 * TS], in_=pC[32:33, 256:256 + TS]),
                        inc=True)
                    if t == 0:
                        s.op("vector", lambda h: h.tensor_copy(
                            out=sac1[0:1, :], in_=stmp[0:1, 0:TS]),
                            inc=True, waits=[("scalar", sr)])
                        s.op("vector", lambda h: h.tensor_copy(
                            out=sac2[0:1, :], in_=stmp[0:1, TS:2 * TS]),
                            inc=True)
                    else:
                        s.op("vector", lambda h: h.tensor_tensor(
                            out=sac1[0:1, :], in0=sac1[0:1, :],
                            in1=stmp[0:1, 0:TS], op=ALU.add),
                            inc=True, waits=[("scalar", sr)])
                        s.op("vector", lambda h: h.tensor_tensor(
                            out=sac2[0:1, :], in0=sac2[0:1, :],
                            in1=stmp[0:1, TS:2 * TS], op=ALU.add), inc=True)
                    yield

                bd = ln_rows_bcast(2)
                yield
                trprev = {}
                for t in range(16):
                    cs0 = (t % 2) * TS
                    wv0 = [("scalar", bd)] if t == 0 else []
                    if t >= 2:
                        wv0.append(("tensor", trprev[t - 2]))
                    s.op("gpsimd", lambda h, t=t, cs0=cs0: h.tensor_tensor(
                        out=csc2[:, cs0:cs0 + TS], in0=zdn[:, t * TS:t * TS + TS],
                        in1=csc1[:, 2 * TS:3 * TS], op=ALU.subtract),
                        inc=False, waits=wv0, sem="pool")
                    s.op("gpsimd", lambda h, cs0=cs0: h.tensor_tensor(
                        out=csc2[:, cs0:cs0 + TS], in0=csc2[:, cs0:cs0 + TS],
                        in1=rB1[:, 2 * TS:3 * TS], op=ALU.mult), sem="pool")
                    ov = s.op("gpsimd", lambda h, t=t, cs0=cs0: h.tensor_tensor(
                        out=csc2[:, cs0:cs0 + TS], in0=csc2[:, cs0:cs0 + TS],
                        in1=x1sl[:, t * TS:t * TS + TS], op=ALU.add), inc=True, sem="pool")
                    for iq in range(TS // 128):
                        trq = s.op("tensor", lambda h, iq=iq, cs0=cs0: h.transpose(
                            pC.ap().bitcast(F32)[:, 0:128],
                            csc2[:, cs0 + iq * 128:cs0 + iq * 128 + 128],
                            c_idf[:]), inc=False,
                            waits=[("pool", ov), ("sync", outd.get(iq ^ 1, 0))]
                            if iq == 0 else [("sync", outd.get(iq ^ 1, 0))])
                        trq = s.now("tensor")
                        otd = s.op("scalar", lambda h, iq=iq: h.copy(
                            out=outp[:, iq * 128:iq * 128 + 128],
                            in_=pC.ap().bitcast(F32)[:, 0:128]), inc=True,
                            waits=[("tensor", trq)])
                        s.barrier("tensor", ["scalar"])
                        s.op("sync", lambda h, t=t, iq=iq, t0q=t0q: h.dma_start(
                            out=out[t0q + iq * 128: t0q + iq * 128 + 128,
                                    t * 128:t * 128 + 128],
                            in_=outp[:, iq * 128:iq * 128 + 128]), inc=True,
                            waits=[("scalar", otd)])
                        outd[iq] = s.now("sync")
                    trprev[t] = s.now("tensor")
                    if t % 4 == 3:
                        yield

            outd = {}
            import collections as _c
            pending = _c.deque()

            def pull(k):
                done = 0
                while pending and done < k:
                    try:
                        next(pending[0])
                        done += 1
                    except StopIteration:
                        pending.popleft()

            globals_pull[0] = pull

            for g in range(NG + 1):
                if g >= 1:
                    scan_group(g - 1)
                if g >= 1 and g % GQ == 0:
                    q = g // GQ - 1
                    emit_collective(q)
                    pending.append(c_wave(q))
                if g < NG:
                    qkv_group(g)
                    pf_pump(5)
            pf_pump(10 ** 9)
            while pending:
                pull(1000)
            s.barrier("gpsimd", ["sync", "vector", "scalar", "tensor"])

        s.emit(block, sems)
    return nc


def stage_inputs(inputs, T):
    Tc = T // NCORE
    x = np.ascontiguousarray(inputs["L0_attn_x"][0, :T].T.astype(np.float32))
    xT_s = np.ascontiguousarray(x.reshape(16, 128, T).transpose(1, 0, 2))

    Wq, Wk, Wv = (inputs["L0_attn_Wq"], inputs["L0_attn_Wk"], inputs["L0_attn_Wv"])
    Wbeta = inputs["L0_attn_Wbeta"]
    Wg1 = inputs["L0_attn_Wg1"]
    Wo_ = inputs["L0_attn_Wo"]
    Wg2_ = inputs["L0_attn_Wg2"]
    lnw, lnb = inputs["L0_attn_ln_w"], inputs["L0_attn_ln_b"]
    nw = inputs["L0_attn_norm_w"]
    gw = inputs["L0_ffn_gate_w"]
    uw = inputs["L0_ffn_up_w"]
    dw = inputs["L0_ffn_down_w"]

    wo_s = np.ascontiguousarray(
        Wo_.reshape(16, 128, 16, 128).transpose(2, 1, 0, 3)).reshape(16, 128, -1)
    gwt_s = np.ascontiguousarray(
        gw.reshape(16, 128, 48, 128).transpose(2, 1, 0, 3)).reshape(48, 128, -1)
    uwt_s = np.ascontiguousarray(
        uw.reshape(16, 128, 48, 128).transpose(2, 1, 0, 3)).reshape(48, 128, -1)
    dwt_s = np.ascontiguousarray(
        dw.reshape(48, 128, 16, 128).transpose(2, 1, 0, 3)).reshape(16, 128, -1)
    gsc_s = np.ascontiguousarray(inputs["L0_ffn_gate_scale"].reshape(48, 128).T)
    usc_s = np.ascontiguousarray(inputs["L0_ffn_up_scale"].reshape(48, 128).T)
    dsc_s = np.ascontiguousarray(inputs["L0_ffn_down_scale"].reshape(16, 128).T)
    sidx = np.arange(128)
    mkK_s = ((sidx[:, None] // CH) < (sidx[None, :] // CH)).astype(np.float32)
    mkG_s = (sidx[:, None] <= sidx[None, :]).astype(np.float32)
    mkKL_s = np.ascontiguousarray(mkK_s[0:64]); mkKH_s = np.ascontiguousarray(mkK_s[64:128])
    mkGL_s = np.ascontiguousarray(mkG_s[0:64]); mkGH_s = np.ascontiguousarray(mkG_s[64:128])

    per_core = []
    for c in range(NCORE):
        h0, h1 = 2 * c, 2 * c + 1
        cols = np.zeros((D, 1024), np.float32)
        cols[:, 0:128] = Wq[:, h0 * 128:(h0 + 1) * 128]
        cols[:, 128:256] = Wq[:, h1 * 128:(h1 + 1) * 128]
        cols[:, 256:384] = Wk[:, h0 * 128:(h0 + 1) * 128]
        cols[:, 384:512] = Wk[:, h1 * 128:(h1 + 1) * 128]
        cols[:, 512:640] = Wv[:, h0 * 128:(h0 + 1) * 128]
        cols[:, 640:768] = Wv[:, h1 * 128:(h1 + 1) * 128]
        cols[:, 768:896] = Wg1
        cols[:, 896] = 1.0
        cols[:, 896 + 32] = Wbeta[:, h0]
        cols[:, 896 + 64] = Wbeta[:, h1]
        wf_s = np.ascontiguousarray(
            cols.reshape(16, 128, 8, 128).transpose(1, 0, 2, 3)).reshape(128, -1)
        wg2_s = np.ascontiguousarray(np.concatenate(
            [Wg2_[:, h0 * 128:(h0 + 1) * 128], Wg2_[:, h1 * 128:(h1 + 1) * 128]],
            axis=1))
        lnw_cs = np.ascontiguousarray(
            np.stack([lnw[h0], lnw[h1]], axis=1).astype(np.float32))
        lnb_cs = np.ascontiguousarray(
            np.stack([lnb[h0], lnb[h1]], axis=1).astype(np.float32))
        nw_cs = np.ascontiguousarray(np.stack(
            [nw[h0 * 128:(h0 + 1) * 128], nw[h1 * 128:(h1 + 1) * 128]],
            axis=1).astype(np.float32))
        NLGs = Tc // 128
        xres_s = np.ascontiguousarray(np.concatenate(
            [xT_s[:, :, (8 * lg + c) * 128:(8 * lg + c) * 128 + 128]
             for lg in range(NLGs)], axis=2))

        per_core.append({
            "xT": xT_s, "wf": wf_s, "wo": wo_s, "wg2": wg2_s,
            "lnw_c": lnw_cs, "lnb_c": lnb_cs, "nw_c": nw_cs,
            "gwt": gwt_s, "uwt": uwt_s, "dwt": dwt_s,
            "gsc": gsc_s, "usc": usc_s, "dsc": dsc_s,
            "idn": np.eye(128, dtype=BF),
            "mkKL": mkKL_s, "mkKH": mkKH_s, "mkGL": mkGL_s, "mkGH": mkGH_s,
            "onesc": np.ones((128, 1), np.float32),
            "onesf": np.ones((128, 128), np.float32),
            "onesrow": np.ones((1, 128), np.float32),
            "xres": xres_s,
        })
    return per_core


def kernel(**inputs):
    T = inputs["L0_attn_x"].shape[1]
    nc = build_program(T)
    in_maps = stage_inputs(inputs, T)
    res = run_bass_kernel_spmd(nc, in_maps, core_ids=list(range(NCORE)))
    outs = [np.asarray(res.results[c]["out"]) for c in range(NCORE)]
    Tc = T // NCORE
    full = np.empty((T, D), np.float32)
    for c in range(NCORE):
        for lg in range(Tc // 128):
            g = 8 * lg + c
            full[g * 128:(g + 1) * 128] = outs[c][lg * 128:(lg + 1) * 128]
    return full[None].astype(np.float32)


# version 10
# speedup vs baseline: 1.1474x; 1.0017x over previous
import os as _os
import numpy as np
import ml_dtypes
import concourse.bass as bass
import concourse.mybir as mybir
from concourse.bass_utils import run_bass_kernel_spmd
from concourse.aot_env import allow_ici_all_to_all

F32 = mybir.dt.float32
BF16 = mybir.dt.bfloat16
F8 = mybir.dt.float8e4
AF = mybir.ActivationFunctionType
ALU = mybir.AluOpType
BF = ml_dtypes.bfloat16

D = 2048
DH = 128
CH = 16
GT = 128
NCORE = 8
EPS_A = 1e-6
EPS_M = 1e-5

ENGS = ("sync", "gpsimd", "vector", "scalar", "tensor")


COMPUTE_ENGS = ("vector", "scalar", "tensor")


class Sched:

    def __init__(self):
        self.ops = {e: [] for e in ENGS}
        self.cnt = {e: 0 for e in ENGS}
        self.cnt["pool"] = 0
        self.step = {"sync": 16, "gpsimd": 16, "vector": 1, "scalar": 1,
                     "tensor": 1, "pool": 1}

    def op(self, eng, fn, inc=False, waits=(), sem=None):
        ch = sem or eng
        self.ops[eng].append((fn, bool(inc), tuple(waits), ch))
        if ch == "pool" or eng in COMPUTE_ENGS or inc:
            self.cnt[ch] += self.step[ch]
        return self.cnt[ch]

    def now(self, eng):
        return self.cnt[eng]

    def barrier(self, eng, others):
        w = tuple((o, self.cnt[o]) for o in others if self.cnt[o] > 0)
        if w:
            self.ops[eng].append((None, False, w, eng))

    def emit(self, block, sems):
        import os as _os
        import bisect
        ssv = _os.environ.get("BASS_SELFSYNC", "vector")
        full = ssv == "full"
        full_engs = set(ssv.split(",")) if ssv not in ("min", "full") else (
            set(COMPUTE_ENGS) if full else set())
        waited = {e: set() for e in ENGS}
        waited["pool"] = set()
        for e in list(self.ops):
            for fn, inc, waits, ch in self.ops[e]:
                for we, wv in waits:
                    if wv > 0 and we in waited:
                        waited[we].add(wv)
        flagged = {e: sorted(waited[e]) for e in COMPUTE_ENGS}

        class _TProxy:
            def __init__(self, h):
                self._h = h

            def __getattr__(self, k):
                return getattr(self._h, k)

            def matmul(self, *a, **kw):
                kw.setdefault("skip_group_check", True)
                return self._h.matmul(*a, **kw)

        def rv(we, wv):
            if we not in COMPUTE_ENGS or we in full_engs:
                return wv
            import bisect as _b
            return _b.bisect_right(flagged[we], wv)

        def mk(eng):
            selfsync = eng in COMPUTE_ENGS
            efull = eng in full_engs
            miles = sorted(waited[eng]) if not selfsync else []
            fset = set(flagged.get(eng, ()))

            def body(h):
                n = 0
                npool = 0
                enforced = 0
                mi = 0
                for fn, inc, waits, ch in self.ops[eng]:
                    for we, wv in waits:
                        wv2 = rv(we, wv)
                        if wv2 > 0:
                            h.wait_ge(sems[we], wv2)
                    if fn is not None:
                        if eng == "tensor":
                            h2 = _TProxy(h)
                        else:
                            h2 = h
                        if ch == "pool":
                            ins = fn(h2)
                            npool += 1
                            ins.then_inc(sems["pool"], 1)
                            continue
                        if selfsync:
                            if efull and n > 0:
                                h.wait_ge(sems[eng], n)
                        elif inc:
                            tgt = 0
                            while mi < len(miles) and miles[mi] <= n:
                                tgt = miles[mi]
                                mi += 1
                            if tgt > enforced:
                                h.wait_ge(sems[eng], tgt)
                                enforced = tgt
                        ins = fn(h2)
                        if selfsync:
                            n += 1
                            if efull:
                                ins.then_inc(sems[eng], 1)
                            elif n in fset:
                                ins.then_inc(sems[eng], 1)
                        elif inc:
                            ins.then_inc(sems[eng], self.step[eng])
                            n += self.step[eng]
            return body
        block.sync(mk("sync"))
        block.gpsimd(mk("gpsimd"))
        block.vector(mk("vector"))
        block.scalar(mk("scalar"))
        block.tensor(mk("tensor"))



def blk2(ap0, c0, w, blk):
    a = ap0[:, c0:c0 + w]
    return bass.AP(tensor=a.tensor, offset=a.offset,
                   ap=[list(a.ap[0]), [blk, 2], [1, w]])

def build_program(T):
    NG = T // GT
    Tc = T // NCORE
    NLG = Tc // GT
    NCH = GT // CH
    LGQ = 2 if NLG >= 2 else 1
    NQ = NLG // LGQ
    GQ = NCORE * LGQ
    TS = LGQ * GT
    assert T % (GT * NCORE) == 0 and NLG % LGQ == 0

    import os as _os
    nc = bass.Bass("TRN2", target_bir_lowering=False,
                   detect_race_conditions=not _os.environ.get("KERN_NORACE"))
    s = Sched()

    def din(name, shape, dt):
        return nc.dram_tensor(name, shape, dt, kind="ExternalInput").ap()

    xT = din("xT", [128, 16, T], F32)
    wf = din("wf", [128, 16 * 8 * 128], F32)
    wo = din("wo", [16, 128, 16 * 128], F32)
    wg2 = din("wg2", [128, 256], F32)
    lnw_c = din("lnw_c", [128, 2], F32)
    lnb_c = din("lnb_c", [128, 2], F32)
    nw_c = din("nw_c", [128, 2], F32)
    gwt = din("gwt", [48, 128, 16 * 128], F32)
    uwt = din("uwt", [48, 128, 16 * 128], F32)
    dwt = din("dwt", [16, 128, 48 * 128], F32)
    gsc = din("gsc", [128, 48], F32)
    usc = din("usc", [128, 48], F32)
    dsc = din("dsc", [128, 16], F32)
    idn = din("idn", [128, 128], BF16)
    mkKL = din("mkKL", [64, 128], F32)
    mkKH = din("mkKH", [64, 128], F32)
    mkGL = din("mkGL", [64, 128], F32)
    mkGH = din("mkGH", [64, 128], F32)
    onesc = din("onesc", [128, 1], F32)
    onesf = din("onesf", [128, 128], F32)
    onesrow = din("onesrow", [1, 128], F32)
    xres = din("xres", [128, 16, Tc], F32)

    out = nc.dram_tensor("out", [Tc, D], F32, kind="ExternalOutput").ap()
    o_stage = [nc.dram_tensor("o_stage%d" % q, [NCORE * LGQ * 2 * 128, GT], BF16)
               for q in range(NQ)]
    o_gath = [nc.dram_tensor("o_gath%d" % q, [NCORE * LGQ * 2 * 128, GT], BF16)
              for q in range(NQ)]
    o_st5 = [t.ap().rearrange("(a l hh p) c -> a l hh p c",
                              a=NCORE, l=LGQ, hh=2, p=128) for t in o_stage]
    o_gt5 = [t.ap().rearrange("(a l hh p) c -> a l hh p c",
                              a=NCORE, l=LGQ, hh=2, p=128) for t in o_gath]
    dbg = nc.dram_tensor("dbg", [1, 8 * 512], F32, kind="ExternalOutput").ap() \
        if _os.environ.get("KERN_DBG") else None
    g8_dram = nc.dram_tensor("g8_dram", [128, 48 * 16 * 128], F8)
    u8_dram = nc.dram_tensor("u8_dram", [128, 48 * 16 * 128], F8)
    d8_dram = nc.dram_tensor("d8_dram", [128, 16 * 48 * 128], F8)
    wobf_dram = nc.dram_tensor("wobf_dram", [16, 128, 16 * 128], BF16)

    import contextlib
    es = contextlib.ExitStack()
    with es:
        def sb(name, shape, dt):
            return es.enter_context(nc.sbuf_tensor(name, shape, dt))

        pbank = [es.enter_context(nc.psum_tensor("pb%d" % i, [128, 512], F32))
                 for i in range(8)]
        pqA, pqB, pA, pB, pS, pC, pQM, pSM = pbank
        pSC = pSM
        pSMb = pSM.ap().bitcast(BF16)

        sems = {e: es.enter_context(nc.semaphore("sem_" + e)) for e in ENGS}
        sems["cc"] = es.enter_context(nc.semaphore("sem_cc"))
        sems["pool"] = es.enter_context(nc.semaphore("sem_pool"))
        block = es.enter_context(nc.Block())

        c_id = sb("c_id", [128, 128], BF16)
        c_idf = sb("c_idf", [128, 128], F32)
        c_mkKL = sb("c_mkKL", [64, 128], F32)
        c_mkKH = sb("c_mkKH", [64, 128], F32)
        c_mkGL = sb("c_mkGL", [64, 128], F32)
        c_mkGH = sb("c_mkGH", [64, 128], F32)
        c_ones = sb("c_ones", [128, 128], F32)
        c_onesc = sb("c_onesc", [128, 1], F32)
        c_onesb = sb("c_onesb", [128, 1], BF16)
        c_onesrow = sb("c_onesrow", [1, 128], F32)
        c_lnw = sb("c_lnw", [128, 2], F32)
        c_lnb = sb("c_lnb", [128, 2], F32)
        c_nw = sb("c_nw", [128, 2], F32)
        c_w2 = sb("c_w2", [128, 2], F32)
        c_wb = sb("c_wb", [128, 2], F32)
        c_negw = sb("c_negw", [128, 2], F32)
        c_w2B = sb("c_w2B", [128, 32], F32)
        c_epsA = sb("c_epsA", [128, 1], F32)
        c_epsM = sb("c_epsM", [128, 1], F32)

        for dst, src in [(c_id, idn), (c_mkKL, mkKL), (c_mkKH, mkKH),
                         (c_mkGL, mkGL), (c_mkGH, mkGH), (c_ones, onesf),
                         (c_onesc, onesc), (c_onesrow, onesrow),
                         (c_lnw, lnw_c), (c_lnb, lnb_c), (c_nw, nw_c)]:
            s.op("sync", lambda h, d=dst, sr=src: h.dma_start(out=d[:], in_=sr),
                 inc=True)
        ld0 = s.now("sync")

        oslab = sb("oslab", [128, 16 * TS], BF16)
        xinb = sb("xinb", [128, 16 * TS], BF16)
        zdn = sb("zdn", [128, 16 * TS], F32)
        x1sl = sb("x1sl", [128, 16 * TS], BF16)
        x2sl = sb("x2sl", [128, 16 * TS], BF16)
        fusl = sb("fusl", [128, 48 * TS], BF16)
        wrng = sb("wrng", [128, 4 * 2048], BF16)
        g8s = sb("g8s", [128, 2 * 2048], F8)
        u8s = sb("u8s", [128, 2 * 2048], F8)
        d8s = sb("d8s", [128, 6144], F8)
        zb2 = sb("zb2", [128, 2 * TS], BF16)
        zq2 = sb("zq2", [128, 2 * TS], BF16)
        gth = sb("gth", [128, TS], F32)
        gxh = sb("gxh", [128, TS], F32)
        guv = sb("guv", [128, TS], F32)
        gvv = sb("gvv", [128, TS], F32)
        sac1 = sb("sac1", [1, TS], F32)
        sac2 = sb("sac2", [1, TS], F32)
        stmp = sb("stmp", [1, 4 * TS], F32)
        rB1 = sb("rB1", [128, 3 * TS], F32)
        csc1 = sb("csc1", [128, 3 * TS], F32)
        csc2 = sb("csc2", [128, 2 * TS], F32)
        outp = sb("outp", [128, 2 * 128], F32)
        gscb = sb("gscb", [128, 48], F32)
        uscb = sb("uscb", [128, 48], F32)
        dscb = sb("dscb", [128, 16], F32)
        ngs = sb("ngs", [128, 48], F32)
        dbg16 = sb("dbg16", [1, 64], F32) if _os.environ.get("KERN_DBG") \
            else None

        s.op("sync", lambda h: h.dma_start(out=gscb[:], in_=gsc), inc=True)
        s.op("sync", lambda h: h.dma_start(out=uscb[:], in_=usc), inc=True)
        s.op("sync", lambda h: h.dma_start(out=dscb[:], in_=dsc), inc=True)
        ldsc = s.now("sync")
        s.op("vector", lambda h: h.tensor_scalar(
            out=ngs[:], in0=gscb[:], scalar1=-1.0, scalar2=None, op0=ALU.mult),
            inc=True, waits=[("sync", ldsc)])

        scan_done = {}
        qkv_evac_done = {}
        qkv_pe_done = {}
        for g in range(-4, 0):
            scan_done[g] = 0
            qkv_evac_done[g] = 0
            qkv_pe_done[g] = 0

        pf_jobs = []
        for t in range(16):
            pf_jobs.append(("wo", t, lambda h, t=t: h.dma_start(
                out=wobf_dram.ap()[t], in_=wo[t])))
        for ht in range(48):
            pf_jobs.append(("g", ht, lambda h, ht=ht: h.dma_start(
                out=g8_dram.ap()[:, ht * 2048:ht * 2048 + 2048], in_=gwt[ht])))
            pf_jobs.append(("u", ht, lambda h, ht=ht: h.dma_start(
                out=u8_dram.ap()[:, ht * 2048:ht * 2048 + 2048], in_=uwt[ht])))
        for t in range(16):
            pf_jobs.append(("d", t, lambda h, t=t: h.dma_start(
                out=d8_dram.ap()[:, t * 6144:t * 6144 + 6144], in_=dwt[t])))
        pf_done = {}
        pf_idx = [0]

        def pf_pump(k):
            while pf_idx[0] < len(pf_jobs) and k > 0:
                kind, idx, fn = pf_jobs[pf_idx[0]]
                s.op("gpsimd", fn, inc=True)
                pf_done[(kind, idx)] = s.now("gpsimd")
                pf_idx[0] += 1
                k -= 1

        def pf_wait(kind, idx):
            while (kind, idx) not in pf_done:
                pf_pump(1)
            return ("gpsimd", pf_done[(kind, idx)])

        esA = contextlib.ExitStack()
        with esA:
            def sba(name, shape, dt):
                return esA.enter_context(nc.sbuf_tensor(name, shape, dt))

            wfb = sba("wfb", [128, 16 * 8 * 128], BF16)
            wg2b = sba("wg2b", [128, 256], BF16)
            csWb = sba("csWb", [1, 8 * 128], BF16)
            W = 2
            xb = sba("xb", [128, W * 2048], BF16)
            sqb = sba("sqb", [128, 2048], BF16)
            qs_f = sba("qs_f", [128, W * 256], F32)
            qs_b = sba("qs_b", [128, W * 256], BF16)
            kn_f = sba("kn_f", [128, W * 256], F32)
            kn_b = sba("kn_b", [128, W * 256], BF16)
            v_f = sba("v_f", [128, W * 256], F32)
            g1_b = sba("g1_b", [128, W * 128], BF16)
            eT = sba("eT", [128, W * 4], F32)
            rowbuf = sba("rowbuf", [128, W * 128], F32)
            s_op = sba("s_op", [128, 256], BF16)
            vnstkL = sba("vnstkL", [64, 256], BF16)
            vnstkH = sba("vnstkH", [64, 256], BF16)
            aktL = sba("aktL", [64, 256], BF16)
            aktH = sba("aktH", [64, 256], BF16)
            agtL = sba("agtL", [64, 256], BF16)
            agtH = sba("agtH", [64, 256], BF16)
            vnstg = sba("vnstg", [128, 64], BF16)
            wbm = sba("wbm", [128, 256], F32)
            ketL = sba("ketL", [64, 256], BF16)
            ketH = sba("ketH", [64, 256], BF16)
            kescr = sba("kescr", [128, 128], BF16)
            scrA = sba("scrA", [128, 128], F32)
            scr1 = sba("scr1", [128, 32], F32)
            scr2 = sba("scr2", [128, 32], F32)
            rstdc = sba("rstdc", [128, 32], F32)
            khh = sba("khh", [128, 32], F32)
            strow = sba("strow", [1, 6 * 128], F32)
            strowb = sba("strowb", [1, 128], BF16)
            rstdB = sba("rstdB", [128, 128], F32)
            qscr1 = sba("qscr1", [128, 128], F32)
            qscr2 = sba("qscr2", [128, 128], F32)
            oscr = sba("oscr", [128, 256], F32)
            oth = sba("oth", [128, 128], F32)
            osig = sba("osig", [128, 128], F32)
            oo = sba("oo", [128, 128], F32)
            oo2 = sba("oo2", [128, 128], F32)
            obuf = sba("obuf", [128, 256], BF16)

            s.op("gpsimd", lambda h: h.dma_start(out=wfb[:], in_=wf), inc=True)
            s.op("gpsimd", lambda h: h.dma_start(out=wg2b[:], in_=wg2), inc=True)
            ld1 = s.now("gpsimd")

            s.op("vector", lambda h: h.tensor_tensor(
                out=c_w2[:], in0=c_lnw[:], in1=c_lnw[:], op=ALU.mult),
                waits=[("sync", ld0)])
            s.op("vector", lambda h: h.tensor_tensor(
                out=c_wb[:], in0=c_lnw[:], in1=c_lnb[:], op=ALU.mult))
            s.op("vector", lambda h: h.tensor_scalar(
                out=c_negw[:], in0=c_lnw[:], scalar1=-1.0, scalar2=None,
                op0=ALU.mult))
            s.op("vector", lambda h: h.tensor_copy(out=c_idf[:], in_=c_id[:]))
            s.op("vector", lambda h: h.tensor_copy(out=c_onesb[:], in_=c_onesc[:]))
            for hh in range(2):
                s.op("vector", lambda h, hh=hh: h.tensor_scalar(
                    out=c_w2B[:, hh * 16:hh * 16 + 16], in0=c_ones[:, 0:16],
                    scalar1=c_w2[:, hh:hh + 1], scalar2=None, op0=ALU.mult))
            s.op("vector", lambda h: h.memset(s_op[:], 0.0))
            s.op("vector", lambda h: h.memset(rowbuf[:], 0.0))
            s.op("vector", lambda h: h.memset(c_epsA[:], EPS_A))
            s.op("vector", lambda h: h.memset(c_epsM[:], EPS_M), inc=True)
            pre0 = s.now("vector")

            for t in range(8):
                for d in range(16):
                    wv = [("gpsimd", ld1), ("vector", pre0)] \
                        if (t == 0 and d == 0) else ()
                    s.op("tensor", lambda h, t=t, d=d: h.matmul(
                        pQM[0:1, 0:128], c_onesb[:],
                        wfb[:, (d * 8 + t) * 128:(d * 8 + t) * 128 + 128],
                        start=(d == 0), stop=(d == 15)), waits=wv)
                pe_cs = s.now("tensor")
                s.op("vector", lambda h, t=t: h.tensor_scalar(
                    out=csWb[0:1, t * 128:t * 128 + 128], in0=pQM[0:1, 0:128],
                    scalar1=-1.0, scalar2=None, op0=ALU.mult),
                    inc=True, waits=[("tensor", pe_cs)])
                s.barrier("tensor", ["vector"])

            def qkv_group(g):
                sg = g % W
                xo = sg * 2048
                ao = sg * 256
                wv = [("tensor", qkv_pe_done[g - W])] if g >= W else ()
                s.op("gpsimd", lambda h, g=g, xo=xo: h.dma_start(
                    out=xb[:, xo:xo + 2048].rearrange("p (d c) -> p d c", d=16),
                    in_=xT[:, :, g * GT:(g + 1) * GT]), inc=True, waits=wv)
                ldx = s.now("gpsimd")
                s.op("scalar", lambda h, xo=xo: h.activation(
                    out=sqb[:, 0:2048], in_=xb[:, xo:xo + 2048],
                    func=AF.Square), inc=True,
                    waits=[("gpsimd", ldx), ("tensor", qkv_pe_done[g - 1])])
                sqd = s.now("scalar")

                for t in range(8):
                    pq = pqA if t < 4 else pqB
                    for d in range(16):
                        wv = [("gpsimd", ldx), ("vector", qkv_evac_done[g - 1]),
                              ("vector", scan_done[g - W])] \
                            if (t == 0 and d == 0) else ()
                        s.op("tensor", lambda h, t=t, d=d, pq=pq, xo=xo: h.matmul(
                            pq[:, (t % 4) * 128:(t % 4) * 128 + 128],
                            wfb[:, (d * 8 + t) * 128:(d * 8 + t) * 128 + 128],
                            xb[:, xo + d * 128: xo + d * 128 + 128],
                            start=(t % 4 == 0 and d == 0), stop=(d == 15)),
                            waits=wv)
                for d in range(16):
                    wv = [("scalar", sqd), ("vector", scan_done[g - 1]),
                          ("scalar", s.now("scalar"))] if d == 0 else ()
                    s.op("tensor", lambda h, d=d, xo=xo: h.matmul(
                        pQM[0:1, 0:128], c_onesb[:],
                        sqb[:, d * 128: d * 128 + 128],
                        start=(d == 0), stop=(d == 15)), waits=wv)
                pe_mm = s.now("tensor")

                ext = pqB[:, 384:512]
                s.op("vector", lambda h, ext=ext: h.tensor_copy(
                    out=strow[0:1, 640:768], in_=ext[0:1, 0:128]),
                    waits=[("tensor", pe_mm)])
                s.op("vector", lambda h: h.tensor_tensor(
                    out=strow[0:1, 128:256], in0=strow[0:1, 640:768],
                    in1=strow[0:1, 640:768], op=ALU.mult))
                s.op("vector", lambda h: h.scalar_tensor_tensor(
                    out=strow[0:1, 128:256], in0=strow[0:1, 128:256],
                    scalar=-1.0 / D, in1=pQM[0:1, 0:128], op0=ALU.mult,
                    op1=ALU.add))
                s.op("vector", lambda h, ext=ext: h.tensor_scalar(
                    out=strowb[0:1, :], in0=ext[0:1, 0:128], scalar1=1.0 / D,
                    scalar2=None, op0=ALU.mult), inc=True)
                dve_r = s.now("vector")
                s.op("scalar", lambda h: h.activation(
                    out=strow[0:1, 256:384], in_=strow[0:1, 128:256], func=AF.Ln,
                    scale=1.0 / D, bias=c_epsM[0:1, :]), waits=[("vector", dve_r)])
                s.op("scalar", lambda h: h.activation(
                    out=strow[0:1, 0:128], in_=strow[0:1, 256:384], func=AF.Exp,
                    scale=-0.5), inc=True)
                act_r = s.now("scalar")

                for t in range(8):
                    pq = pqA if t < 4 else pqB
                    s.op("tensor", lambda h, t=t, pq=pq: h.matmul(
                        pq[:, (t % 4) * 128:(t % 4) * 128 + 128],
                        csWb[0:1, t * 128:t * 128 + 128], strowb[0:1, :],
                        start=False, stop=True),
                        waits=[("scalar", act_r)] if t == 0 else ())
                pe_fix = s.op("tensor", lambda h: h.matmul(
                    pQM[:, 128:256], c_onesrow[0:1, :], strow[0:1, 0:128],
                    start=True, stop=True), inc=True)
                qkv_pe_done[g] = pe_fix
                s.op("vector", lambda h: h.tensor_copy(
                    out=rstdB[:], in_=pQM[:, 128:256]), inc=True,
                    waits=[("tensor", pe_fix), ("vector", scan_done[g - W])])
                s.barrier("tensor", ["vector"])

                for hh in range(2):
                    tq = pqA[:, hh * 128:hh * 128 + 128]
                    qf = qs_f[:, ao + hh * 128: ao + hh * 128 + 128]
                    q_ev = s.op("vector", lambda h, tq=tq, qf=qf: h.tensor_tensor(
                        out=qf, in0=tq, in1=rstdB[:], op=ALU.mult), inc=True)
                    s.op("scalar", lambda h, qf=qf: h.activation(
                        out=oth[:], in_=qf, func=AF.Exp, scale=-1.0),
                        waits=[("vector", q_ev)])
                    s.op("scalar", lambda h: h.activation(
                        out=oth[:], in_=oth[:], func=AF.Ln, scale=1.0,
                        bias=c_onesc[:]))
                    a_t = s.op("scalar", lambda h: h.activation(
                        out=oth[:], in_=oth[:], func=AF.Exp, scale=-1.0),
                        inc=True)
                    s.op("vector", lambda h, qf=qf: h.tensor_tensor(
                        out=qf, in0=qf, in1=oth[:], op=ALU.mult),
                        waits=[("scalar", a_t)])
                    s.op("vector", lambda h, qf=qf, hh=hh, ao=ao: h.tensor_copy(
                        out=qs_b[:, ao + hh * 128: ao + hh * 128 + 128], in_=qf),
                        inc=True)
                    s.barrier("scalar", ["vector"])
                for hh in range(2):
                    tk = pqA[:, (2 + hh) * 128:(2 + hh) * 128 + 128]
                    kf = kn_f[:, ao + hh * 128: ao + hh * 128 + 128]
                    k_ev = s.op("vector", lambda h, tk=tk, kf=kf: h.tensor_tensor(
                        out=kf, in0=tk, in1=rstdB[:], op=ALU.mult), inc=True)
                    k_sq = s.op("scalar", lambda h, kf=kf: h.activation(
                        out=qscr1[:], in_=kf, func=AF.Square),
                        inc=True, waits=[("vector", k_ev)])
                    k_mm = s.op("tensor", lambda h: h.matmul(
                        pQM[0:1, 0:128], c_onesc[:], qscr1[:],
                        start=True, stop=True), inc=True, waits=[("scalar", k_sq)])
                    s.op("scalar", lambda h: h.activation(
                        out=strow[0:1, 384:512], in_=pQM[0:1, 0:128], func=AF.Ln,
                        scale=1.0, bias=c_epsA[0:1, :]), waits=[("tensor", k_mm)])
                    k_rn = s.op("scalar", lambda h: h.activation(
                        out=strow[0:1, 384:512], in_=strow[0:1, 384:512],
                        func=AF.Exp, scale=-0.5), inc=True)
                    k_bc = s.op("tensor", lambda h: h.matmul(
                        pQM[:, 128:256], c_onesrow[0:1, :], strow[0:1, 384:512],
                        start=True, stop=True), inc=True, waits=[("scalar", k_rn)])
                    s.op("vector", lambda h, kf=kf: h.tensor_tensor(
                        out=kf, in0=kf, in1=pQM[:, 128:256], op=ALU.mult),
                        waits=[("tensor", k_bc)])
                    s.op("vector", lambda h, kf=kf, hh=hh, ao=ao: h.tensor_copy(
                        out=kn_b[:, ao + hh * 128: ao + hh * 128 + 128], in_=kf),
                        inc=True)
                    s.barrier("tensor", ["vector"])
                    s.barrier("scalar", ["vector"])
                for hh in range(2):
                    tv = pqB[:, hh * 128:hh * 128 + 128]
                    s.op("vector", lambda h, tv=tv, hh=hh, ao=ao: h.tensor_tensor(
                        out=v_f[:, ao + hh * 128: ao + hh * 128 + 128],
                        in0=tv, in1=rstdB[:], op=ALU.mult))
                s.op("vector", lambda h: h.tensor_tensor(
                    out=qscr2[:], in0=pqB[:, 256:384], in1=rstdB[:], op=ALU.mult))
                s.op("vector", lambda h, sg=sg: h.tensor_copy(
                    out=g1_b[:, sg * 128:sg * 128 + 128], in_=qscr2[:]))
                s.op("vector", lambda h, sg=sg, ext=ext: h.tensor_tensor(
                    out=rowbuf[32:33, sg * 128:sg * 128 + 128], in0=ext[32:33, 0:128],
                    in1=rstdB[32:33, 0:128], op=ALU.mult))
                v_done = s.op("vector", lambda h, sg=sg, ext=ext: h.tensor_tensor(
                    out=rowbuf[64:65, sg * 128:sg * 128 + 128],
                    in0=ext[64:65, 0:128], in1=rstdB[64:65, 0:128], op=ALU.mult),
                    inc=True)
                for i, base in enumerate((32, 64)):
                    kw = [("vector", v_done)] if i == 0 else ()
                    rb = lambda h, base=base, sg=sg: rowbuf[
                        base:base + 1, sg * 128:sg * 128 + 128]
                    s.op("scalar", lambda h, base=base, sg=sg: h.activation(
                        out=rowbuf[base:base + 1, sg * 128:sg * 128 + 128],
                        in_=rowbuf[base:base + 1, sg * 128:sg * 128 + 128],
                        func=AF.Exp, scale=-1.0), waits=kw)
                    s.op("scalar", lambda h, base=base, sg=sg: h.activation(
                        out=rowbuf[base:base + 1, sg * 128:sg * 128 + 128],
                        in_=rowbuf[base:base + 1, sg * 128:sg * 128 + 128],
                        func=AF.Ln, scale=1.0, bias=c_onesc[0:1, :]))
                    s.op("scalar", lambda h, base=base, sg=sg: h.activation(
                        out=rowbuf[base:base + 1, sg * 128:sg * 128 + 128],
                        in_=rowbuf[base:base + 1, sg * 128:sg * 128 + 128],
                        func=AF.Exp, scale=-1.0), inc=(i == 1))
                e_th = s.now("scalar")
                for i, base in enumerate((32, 64)):
                    kw = [("scalar", e_th)] if i == 0 else ()
                    s.op("vector", lambda h, base=base, sg=sg: h.tensor_scalar(
                        out=rowbuf[base:base + 1, sg * 128:sg * 128 + 128],
                        in0=rowbuf[base:base + 1, sg * 128:sg * 128 + 128],
                        scalar1=-2.0, scalar2=None, op0=ALU.mult),
                        inc=(i == 1), waits=kw)
                e_neg = s.now("vector")
                i = 0
                for hhb, base in enumerate((32, 64)):
                    for half in range(2):
                        kw = [("vector", e_neg)] if i == 0 else ()
                        s.op("tensor", lambda h, base=base, sg=sg, i=i, half=half:
                             h.matmul(
                                 pQM[0:64, 256 + i:257 + i],
                                 rowbuf[base:base + 1,
                                        sg * 128 + half * 64:
                                        sg * 128 + half * 64 + 64],
                                 c_ones[base:base + 1, 0:1],
                                 start=True, stop=True),
                             inc=(i == 3), waits=kw)
                        i += 1
                e_tr = s.now("tensor")
                s.op("vector", lambda h, sg=sg: h.tensor_copy(
                    out=eT[0:64, sg * 4: sg * 4 + 4], in_=pQM[0:64, 256:260]),
                    inc=True, waits=[("tensor", e_tr)])
                s.barrier("tensor", ["vector"])
                qkv_evac_done[g] = s.now("vector")

            def scan_group(g):
                sg = g % W
                ao = sg * 256
                qev = qkv_evac_done[g]
                guard = qev

                if g >= 1:
                    for hh in range(2):
                        s.op("vector", lambda h, hh=hh: h.tensor_copy(
                            out=s_op[:, hh * 128:hh * 128 + 128],
                            in_=pS[:, hh * 128:hh * 128 + 128]),
                            inc=(hh == 1),
                            waits=[("vector", qev)] if hh == 0 else ())

                for hh in range(2):
                    kb = kn_b[:, ao + hh * 128: ao + hh * 128 + 128]
                    qb = qs_b[:, ao + hh * 128: ao + hh * 128 + 128]
                    for half, (aktX, agtX, mKX, mGX) in enumerate(
                            ((aktL, agtL, c_mkKL, c_mkGL),
                             (aktH, agtH, c_mkKH, c_mkGH))):
                        ec = eT[0:64, sg * 4 + hh * 2 + half:
                                sg * 4 + hh * 2 + half + 1]
                        mm1 = s.op("tensor", lambda h, kb=kb, half=half: h.matmul(
                            pSC[0:64, 256:384], kb[:, half * 64:half * 64 + 64],
                            kb, start=True, stop=True), inc=True,
                            waits=[("vector", qev)] if (hh == 0 and half == 0)
                            else ())
                        dv1 = s.op("vector", lambda h, hh=hh, ec=ec, aktX=aktX,
                                   mKX=mKX: h.scalar_tensor_tensor(
                                       out=aktX[:, hh * 128:hh * 128 + 128],
                                       in0=pSC[0:64, 256:384], scalar=ec,
                                       in1=mKX[:], op0=ALU.mult, op1=ALU.mult),
                                   inc=True, waits=[("tensor", mm1)])
                        mm2 = s.op("tensor", lambda h, kb=kb, qb=qb, half=half:
                                   h.matmul(
                                       pSC[0:64, 256:384],
                                       kb[:, half * 64:half * 64 + 64], qb,
                                       start=True, stop=True), inc=True,
                                   waits=[("vector", dv1)])
                        s.op("vector", lambda h, hh=hh, ec=ec, agtX=agtX,
                             mGX=mGX: h.scalar_tensor_tensor(
                                 out=agtX[:, hh * 128:hh * 128 + 128],
                                 in0=pSC[0:64, 256:384], scalar=ec, in1=mGX[:],
                                 op0=ALU.mult, op1=ALU.mult),
                             inc=True, waits=[("tensor", mm2)])
                        s.barrier("tensor", ["vector"])

                for hh in range(2):
                    s.op("vector", lambda h, hh=hh, ao=ao: h.tensor_tensor(
                        out=oscr[:, 0:128],
                        in0=v_f[:, ao + hh * 128:ao + hh * 128 + 128],
                        in1=kn_f[:, ao + hh * 128:ao + hh * 128 + 128],
                        op=ALU.subtract))
                    s.op("vector", lambda h, hh=hh: h.tensor_scalar(
                        out=wbm[:, hh * 128:hh * 128 + 128], in0=oscr[:, 0:128],
                        scalar1=c_negw[:, hh:hh + 1], scalar2=c_wb[:, hh:hh + 1],
                        op0=ALU.mult, op1=ALU.add), inc=True)
                    base = 32 + hh * 32
                    ebc = s.op("tensor", lambda h, sg=sg, base=base: h.matmul(
                        pQM[:, 128:256], c_ones[base:base + 1, 0:128],
                        rowbuf[base:base + 1, sg * 128:sg * 128 + 128],
                        start=True, stop=True), inc=True,
                        waits=[("vector", guard)] if hh == 0 else ())
                    kes = s.op("vector", lambda h, hh=hh, ao=ao: h.tensor_tensor(
                        out=kescr[:],
                        in0=kn_f[:, ao + hh * 128:ao + hh * 128 + 128],
                        in1=pQM[:, 128:256], op=ALU.mult), inc=True,
                        waits=[("tensor", ebc)])
                    ktr = 0
                    for half in range(2):
                        ktr = s.op("tensor", lambda h, half=half: h.transpose(
                            pSMb[0:64, half * 128:half * 128 + 128],
                            kescr[:, half * 64:half * 64 + 64], c_id[:]),
                            inc=True,
                            waits=[("vector", kes)] if half == 0 else ())
                    for half, ketX in enumerate((ketL, ketH)):
                        s.op("vector", lambda h, hh=hh, half=half, ketX=ketX:
                             h.tensor_copy(
                                 out=ketX[:, hh * 128:hh * 128 + 128],
                                 in_=pSMb[0:64, half * 128:half * 128 + 128]),
                             inc=(half == 1),
                             waits=[("tensor", ktr)] if half == 0 else ())
                    s.barrier("tensor", ["vector"])
                prep_done = s.now("vector")

                for hh in range(2):
                    s.op("tensor", lambda h, hh=hh, ao=ao: h.matmul(
                        pA[:, hh * 128:hh * 128 + 128],
                        s_op[:, hh * 128:hh * 128 + 128],
                        kn_b[:, ao + hh * 128:ao + hh * 128 + 128],
                        start=(hh == 0), stop=True),
                        waits=[("vector", prep_done)] if hh == 0 else ())
                    s.op("tensor", lambda h, hh=hh, ao=ao: h.matmul(
                        pB[:, hh * 128:hh * 128 + 128],
                        s_op[:, hh * 128:hh * 128 + 128],
                        qs_b[:, ao + hh * 128:ao + hh * 128 + 128],
                        start=(hh == 0), stop=True))
                _ = s.now("tensor")

                last_stk = s.now("vector")
                for j in range(NCH):
                    m = j // 2
                    c0 = j * CH
                    if j > 0:
                        KL = min(16 * j, 64)
                        KH = 16 * j - KL
                        for hh in range(2):
                            s.op("tensor", lambda h, hh=hh, KL=KL, c0=c0: h.matmul(
                                pA[:, hh * 128 + c0: hh * 128 + c0 + CH],
                                vnstkL[0:KL, hh * 128:hh * 128 + 128],
                                aktL[0:KL, hh * 128 + c0:hh * 128 + c0 + CH],
                                start=False, stop=True),
                                waits=[("vector", last_stk)] if hh == 0 else ())
                            if KH > 0:
                                s.op("tensor", lambda h, hh=hh, KH=KH, c0=c0:
                                     h.matmul(
                                         pA[:, hh * 128 + c0:
                                            hh * 128 + c0 + CH],
                                         vnstkH[0:KH, hh * 128:hh * 128 + 128],
                                         aktH[0:KH,
                                              hh * 128 + c0:hh * 128 + c0 + CH],
                                         start=False, stop=True))
                    corr_pe = s.now("tensor")

                    s.op("vector", lambda h, c0=c0: h.tensor_copy(
                        out=scrA[:, 0:32],
                        in_=blk2(pA, c0, CH, 128)),
                        waits=[("tensor", corr_pe)])
                    kh_ev = s.op("vector", lambda h: h.tensor_tensor(
                        out=scrA[:, 32:64], in0=scrA[:, 0:32], in1=scrA[:, 0:32],
                        op=ALU.mult), inc=True)
                    st1 = s.op("tensor", lambda h: h.matmul(
                        pSM[:, 128:192], c_ones[:], scrA[:, 0:64],
                        start=True, stop=True), inc=True, waits=[("vector", kh_ev)])
                    s.op("vector", lambda h: h.tensor_copy(
                        out=scr2[:], in_=pSM[:, 128:160]), waits=[("tensor", st1)])
                    s.op("vector", lambda h: h.tensor_tensor(
                        out=scr1[:], in0=scr2[:], in1=scr2[:], op=ALU.mult))
                    vpre = s.op("vector", lambda h: h.scalar_tensor_tensor(
                        out=scr1[:], in0=scr1[:], scalar=-1.0 / DH,
                        in1=pSM[:, 160:192], op0=ALU.mult, op1=ALU.add), inc=True)
                    s.op("scalar", lambda h: h.activation(
                        out=scr2[:], in_=scr1[:], func=AF.Ln, scale=1.0 / DH,
                        bias=c_epsA[:]), waits=[("vector", vpre)])
                    a_rs = s.op("scalar", lambda h: h.activation(
                        out=rstdc[:], in_=scr2[:], func=AF.Exp, scale=-0.5),
                        inc=True)
                    s.op("vector", lambda h: h.scalar_tensor_tensor(
                        out=khh[:], in0=pSM[:, 128:160], scalar=-1.0 / DH,
                        in1=scrA[:, 0:32], op0=ALU.mult, op1=ALU.add),
                        waits=[("scalar", a_rs)])
                    s.op("vector", lambda h: h.tensor_tensor(
                        out=khh[:], in0=khh[:], in1=rstdc[:], op=ALU.mult))
                    s.op("vector", lambda h: h.tensor_tensor(
                        out=scrA[:, 64:96], in0=khh[:], in1=c_w2B[:], op=ALU.mult))
                    s.op("vector", lambda h, c0=c0: h.tensor_tensor(
                        out=scrA[:, 64:96], in0=scrA[:, 64:96],
                        in1=blk2(wbm, c0, CH, 128), op=ALU.add))
                    gw_d = s.op("vector", lambda h: h.tensor_tensor(
                        out=scrA[:, 96:128], in0=scrA[:, 64:96], in1=khh[:],
                        op=ALU.mult), inc=True)
                    st2 = s.op("tensor", lambda h: h.matmul(
                        pSM[:, 192:256], c_ones[:], scrA[:, 64:128],
                        start=True, stop=True), inc=True, waits=[("vector", gw_d)])
                    s.op("vector", lambda h: h.scalar_tensor_tensor(
                        out=scr1[:], in0=khh[:], scalar=1.0 / DH,
                        in1=pSM[:, 224:256], op0=ALU.mult, op1=ALU.mult),
                        waits=[("tensor", st2)])
                    s.op("vector", lambda h: h.tensor_tensor(
                        out=scr1[:], in0=scrA[:, 64:96], in1=scr1[:],
                        op=ALU.subtract))
                    s.op("vector", lambda h: h.scalar_tensor_tensor(
                        out=scr1[:], in0=pSM[:, 192:224], scalar=-1.0 / DH,
                        in1=scr1[:], op0=ALU.mult, op1=ALU.add))
                    s.op("vector", lambda h: h.tensor_tensor(
                        out=scr2[:], in0=scr1[:], in1=rstdc[:], op=ALU.mult))
                    po = (j % 2) * 16
                    s.op("vector", lambda h, po=po: h.tensor_copy(
                        out=vnstg[:, po:po + 16], in_=scr2[:, 0:16]))
                    vn_d = s.op("vector", lambda h, po=po: h.tensor_copy(
                        out=vnstg[:, 32 + po:32 + po + 16], in_=scr2[:, 16:32]),
                        inc=True)

                    vstk = vnstkL if m < 2 else vnstkH
                    r0 = 32 * (m % 2)
                    nr = 16 if j % 2 == 0 else 32
                    src0 = 0 if j % 2 == 0 else 0
                    trs = 0
                    for hh in range(2):
                        trs = s.op("tensor", lambda h, hh=hh, r0=r0, nr=nr:
                                   h.transpose(
                                       pSMb[r0:r0 + nr,
                                            hh * 128:hh * 128 + 128],
                                       vnstg[:, hh * 32:hh * 32 + nr], c_id[:]),
                                   inc=True,
                                   waits=[("vector", vn_d)] if hh == 0 else ())
                    s.op("vector", lambda h, r0=r0, nr=nr, vstk=vstk:
                         h.tensor_copy(
                             out=vstk[r0:r0 + nr, 0:256],
                             in_=pSMb[r0:r0 + nr, 0:256]),
                         inc=True, waits=[("tensor", trs)])
                    last_stk = s.now("vector")
                    s.barrier("tensor", ["vector"])
                    if globals_pull[0] is not None:
                        globals_pull[0](1)

                for hh in range(2):
                    s.op("tensor", lambda h, hh=hh: h.matmul(
                        pB[:, hh * 128:hh * 128 + 128],
                        vnstkL[:, hh * 128:hh * 128 + 128],
                        agtL[:, hh * 128:hh * 128 + 128], start=False, stop=False),
                        waits=[("vector", last_stk)] if hh == 0 else ())
                    s.op("tensor", lambda h, hh=hh: h.matmul(
                        pB[:, hh * 128:hh * 128 + 128],
                        vnstkH[:, hh * 128:hh * 128 + 128],
                        agtH[:, hh * 128:hh * 128 + 128], start=False, stop=True))
                    s.op("tensor", lambda h, hh=hh, g=g: h.matmul(
                        pS[:, hh * 128:hh * 128 + 128],
                        ketL[:, hh * 128:hh * 128 + 128],
                        vnstkL[:, hh * 128:hh * 128 + 128],
                        start=(g == 0 and hh == 0), stop=False))
                    s.op("tensor", lambda h, hh=hh, g=g: h.matmul(
                        pS[:, hh * 128:hh * 128 + 128],
                        ketH[:, hh * 128:hh * 128 + 128],
                        vnstkH[:, hh * 128:hh * 128 + 128],
                        start=False, stop=True))
                qh_pe = s.now("tensor")

                for hh in range(2):
                    qf = qs_f[:, ao + hh * 128: ao + hh * 128 + 128]
                    s.op("vector", lambda h, hh=hh: h.tensor_copy(
                        out=oscr[:, 0:128], in_=pB[:, hh * 128:hh * 128 + 128]),
                        waits=[("tensor", qh_pe)] if hh == 0 else ())
                    o_ev = s.op("vector", lambda h: h.tensor_tensor(
                        out=oscr[:, 128:256], in0=oscr[:, 0:128],
                        in1=oscr[:, 0:128], op=ALU.mult), inc=True)
                    o_mm = s.op("tensor", lambda h: h.matmul(
                        pSM[:, 256:512], c_ones[:], oscr[:, 0:256],
                        start=True, stop=True), inc=True, waits=[("vector", o_ev)])
                    s.op("vector", lambda h: h.tensor_copy(
                        out=oo[:], in_=pSM[:, 256:384]), waits=[("tensor", o_mm)])
                    s.op("vector", lambda h: h.tensor_tensor(
                        out=oo[:], in0=oo[:], in1=oo[:], op=ALU.mult))
                    o_v = s.op("vector", lambda h: h.scalar_tensor_tensor(
                        out=oo[:], in0=oo[:], scalar=-1.0 / DH,
                        in1=pSM[:, 384:512], op0=ALU.mult, op1=ALU.add), inc=True)
                    s.op("scalar", lambda h: h.activation(
                        out=oo2[:], in_=oo[:], func=AF.Ln, scale=1.0 / DH,
                        bias=c_epsA[:]), waits=[("vector", o_v)])
                    o_rs = s.op("scalar", lambda h: h.activation(
                        out=oo2[:], in_=oo2[:], func=AF.Exp, scale=-0.5), inc=True)
                    s.op("vector", lambda h: h.scalar_tensor_tensor(
                        out=oo[:], in0=pSM[:, 256:384], scalar=-1.0 / DH,
                        in1=oscr[:, 0:128], op0=ALU.mult, op1=ALU.add),
                        waits=[("scalar", o_rs)])
                    s.op("vector", lambda h: h.tensor_tensor(
                        out=oo[:], in0=oo[:], in1=oo2[:], op=ALU.mult))
                    s.op("vector", lambda h, hh=hh: h.tensor_scalar(
                        out=oo[:], in0=oo[:], scalar1=c_lnw[:, hh:hh + 1],
                        scalar2=c_lnb[:, hh:hh + 1], op0=ALU.mult, op1=ALU.add))
                    o_d = s.op("vector", lambda h, qf=qf: h.tensor_tensor(
                        out=oo[:], in0=oo[:], in1=qf, op=ALU.add), inc=True)
                    g_mm = s.op("tensor", lambda h, hh=hh, sg=sg: h.matmul(
                        pQM[:, 384:512], wg2b[:, hh * 128:hh * 128 + 128],
                        g1_b[:, sg * 128:sg * 128 + 128], start=True, stop=True),
                        inc=True, waits=[("vector", o_d), ("vector", guard)])
                    s.op("scalar", lambda h: h.activation(
                        out=oth[:], in_=pQM[:, 384:512], func=AF.Exp, scale=-1.0),
                        waits=[("tensor", g_mm)])
                    s.op("scalar", lambda h: h.activation(
                        out=oth[:], in_=oth[:], func=AF.Ln, scale=1.0,
                        bias=c_onesc[:]))
                    g_th = s.op("scalar", lambda h: h.activation(
                        out=oth[:], in_=oth[:], func=AF.Exp, scale=-1.0), inc=True)
                    og_d = s.op("vector", lambda h: h.tensor_tensor(
                        out=oo[:], in0=oo[:], in1=oth[:], op=ALU.mult), inc=True,
                        waits=[("scalar", g_th)])
                    og_sq = s.op("scalar", lambda h: h.activation(
                        out=oo2[:], in_=oo[:], func=AF.Square), inc=True,
                        waits=[("vector", og_d)])
                    r_mm = s.op("tensor", lambda h: h.matmul(
                        pQM[0:1, 128:256], c_onesc[:], oo2[:],
                        start=True, stop=True), inc=True,
                        waits=[("scalar", og_sq)])
                    s.op("scalar", lambda h: h.activation(
                        out=strow[0:1, 512:640], in_=pQM[0:1, 128:256], func=AF.Ln,
                        scale=1.0 / DH, bias=c_epsA[0:1, :]),
                        waits=[("tensor", r_mm)])
                    r_rs = s.op("scalar", lambda h: h.activation(
                        out=strow[0:1, 512:640], in_=strow[0:1, 512:640],
                        func=AF.Exp, scale=-0.5), inc=True)
                    r_bc = s.op("tensor", lambda h: h.matmul(
                        pQM[:, 128:256], c_onesrow[0:1, :], strow[0:1, 512:640],
                        start=True, stop=True), inc=True, waits=[("scalar", r_rs)])
                    s.op("vector", lambda h: h.tensor_tensor(
                        out=oo[:], in0=oo[:], in1=pQM[:, 128:256], op=ALU.mult),
                        waits=[("tensor", r_bc)])
                    ob_d = s.op("vector", lambda h, hh=hh: h.tensor_scalar(
                        out=obuf[:, hh * 128:hh * 128 + 128], in0=oo[:],
                        scalar1=c_nw[:, hh:hh + 1], scalar2=None, op0=ALU.mult),
                        inc=True)
                    s.barrier("tensor", ["vector"])
                    s.barrier("scalar", ["vector"])
                    s.op("sync", lambda h, g=g, hh=hh: h.dma_start(
                        out=o_st5[g // GQ][g % 8, (g % GQ) // 8, hh, :, :],
                        in_=obuf[:, hh * 128:hh * 128 + 128]), inc=True,
                        waits=[("vector", ob_d)])
                scan_done[g] = s.now("vector")
                s.barrier("vector", ["sync"])

            globals_pull = [None]

            def emit_collective(q):
                ns = s.now("sync")
                import os as _os
                if _os.environ.get("KERN_NO_CC"):
                    s.op("gpsimd", lambda h, q=q: h.dma_start(
                        out=o_gath[q].ap(), in_=o_stage[q].ap()).then_inc(
                            sems["cc"], 1), inc=False, waits=[("sync", ns)])
                else:
                    def do_cc(h, q=q):
                        with allow_ici_all_to_all():
                            return h.collective_compute(
                                "AllToAll", ALU.bypass,
                                replica_groups=[list(range(NCORE))],
                                ins=[o_stage[q].ap().opt()],
                                outs=[o_gath[q].ap().opt()]).then_inc(
                                    sems["cc"], 1)
                    s.op("gpsimd", do_cc, inc=False, waits=[("sync", ns)])

            def c_wave(q):
                t0q = q * TS
                for _ in range(12):
                    yield
                for src in range(NCORE):
                    for hh in range(2):
                        dti = src * 2 + hh
                        wv = [("cc", q + 1), ("tensor", s.now("tensor")),
                              ("pool", s.now("pool"))] \
                            if (src == 0 and hh == 0) else ()
                        s.op("sync", lambda h, src=src, hh=hh, dti=dti, q=q:
                             h.dma_start(
                                 out=oslab[:, dti * TS:dti * TS + TS].rearrange(
                                     "p (l c) -> p l c", l=LGQ),
                                 in_=o_gt5[q][src, :, hh].rearrange(
                                     "l p c -> p l c")), inc=True, waits=wv)
                s.op("gpsimd", lambda h, t0q=t0q: h.dma_start(
                    out=xinb[:].rearrange("p (d c) -> p d c", d=16),
                    in_=xres[:, :, t0q:t0q + TS]), inc=True)
                ldxi = s.now("gpsimd")
                lds = s.now("sync")
                yield

                stat_t = {}
                wo_ld = {}
                pe_done = {}
                for t in range(2):
                    s.op("sync", lambda h, t=t: h.dma_start(
                        out=wrng[:, (t % 4) * 2048:(t % 4) * 2048 + 2048],
                        in_=wobf_dram.ap()[t]), inc=True,
                        waits=[pf_wait("wo", t)])
                    wo_ld[t] = s.now("sync")
                for t in range(16):
                    pq0 = 0
                    for d in range(16):
                        wv = [("sync", wo_ld[t]), ("sync", lds)] \
                            if d == 0 else ()
                        if d == 0 and t >= 1:
                            wv = list(wv) + [("scalar", stat_t[t - 1])]
                        s.op("tensor", lambda h, t=t, d=d: h.matmul(
                            pC[:, 0:TS],
                            wrng[:, (t % 4) * 2048 + d * 128:
                                 (t % 4) * 2048 + d * 128 + 128],
                            oslab[:, d * TS:d * TS + TS],
                            start=(d == 0), stop=(d == 15)), waits=wv)
                    pe_t = s.now("tensor")
                    pe_done[t] = pe_t
                    if t + 2 < 16:
                        s.op("sync", lambda h, t=t: h.dma_start(
                            out=wrng[:, ((t + 2) % 4) * 2048:
                                     ((t + 2) % 4) * 2048 + 2048],
                            in_=wobf_dram.ap()[t + 2]), inc=True,
                            waits=[("tensor", pe_t), pf_wait("wo", t + 2)])
                        wo_ld[t + 2] = s.now("sync")
                    s.op("scalar", lambda h, t=t: h.copy(
                        out=zdn[:, t * TS:t * TS + TS], in_=pC[:, 0:TS]),
                        waits=[("tensor", pe_t)])
                    s.op("scalar", lambda h, t=t: h.copy(
                        out=zb2[:, (t % 2) * TS:(t % 2) * TS + TS],
                        in_=pC[:, 0:TS]))
                    zev = s.op("scalar", lambda h, t=t: h.activation(
                        out=zq2[:, (t % 2) * TS:(t % 2) * TS + TS],
                        in_=zb2[:, (t % 2) * TS:(t % 2) * TS + TS],
                        func=AF.Square), inc=True)
                    s.op("tensor", lambda h, t=t: h.matmul(
                        pC[0:1, 256:256 + TS], c_onesb[:],
                        zb2[:, (t % 2) * TS:(t % 2) * TS + TS],
                        start=True, stop=True), waits=[("scalar", zev)])
                    s.op("tensor", lambda h, t=t: h.matmul(
                        pC[32:33, 256:256 + TS], c_onesb[:, 0:1],
                        zq2[:, (t % 2) * TS:(t % 2) * TS + TS],
                        start=True, stop=True))
                    smm = s.now("tensor")
                    s.op("scalar", lambda h: h.copy(
                        out=stmp[0:1, 0:TS], in_=pC[0:1, 256:256 + TS]),
                        waits=[("tensor", smm)])
                    sr = s.op("scalar", lambda h: h.copy(
                        out=stmp[0:1, TS:2 * TS], in_=pC[32:33, 256:256 + TS]),
                        inc=True)
                    if t == 0:
                        s.op("vector", lambda h: h.tensor_copy(
                            out=sac1[0:1, :], in_=stmp[0:1, 0:TS]),
                            inc=True, waits=[("scalar", sr)])
                        s.op("vector", lambda h: h.tensor_copy(
                            out=sac2[0:1, :], in_=stmp[0:1, TS:2 * TS]),
                            inc=True)
                    else:
                        s.op("vector", lambda h: h.tensor_tensor(
                            out=sac1[0:1, :], in0=sac1[0:1, :],
                            in1=stmp[0:1, 0:TS], op=ALU.add),
                            inc=True, waits=[("scalar", sr)])
                        s.op("vector", lambda h: h.tensor_tensor(
                            out=sac2[0:1, :], in0=sac2[0:1, :],
                            in1=stmp[0:1, TS:2 * TS], op=ALU.add), inc=True)
                    stat_t[t] = sr
                    yield

                def ln_rows_bcast(st):
                    pl = s.op("vector", lambda h: h.tensor_tensor(
                        out=stmp[0:1, 2 * TS:3 * TS], in0=sac1[0:1, :],
                        in1=sac1[0:1, :], op=ALU.mult), inc=True)
                    pl = s.op("vector", lambda h: h.scalar_tensor_tensor(
                        out=stmp[0:1, 2 * TS:3 * TS],
                        in0=stmp[0:1, 2 * TS:3 * TS], scalar=-1.0 / D,
                        in1=sac2[0:1, :], op0=ALU.mult, op1=ALU.add), inc=True)
                    s.op("scalar", lambda h: h.activation(
                        out=stmp[0:1, 3 * TS:4 * TS],
                        in_=stmp[0:1, 2 * TS:3 * TS], func=AF.Ln,
                        scale=1.0 / D, bias=c_epsM[0:1, :]),
                        waits=[("vector", pl)])
                    s.op("scalar", lambda h: h.activation(
                        out=stmp[0:1, 3 * TS:4 * TS],
                        in_=stmp[0:1, 3 * TS:4 * TS], func=AF.Exp, scale=-0.5))
                    mr = s.op("scalar", lambda h: h.activation(
                        out=stmp[0:1, 2 * TS:3 * TS], in_=sac1[0:1, :],
                        func=AF.Copy, scale=1.0 / D), inc=True)
                    b1 = s.op("tensor", lambda h: h.matmul(
                        pC[:, 256:256 + TS], c_onesrow[0:1, :],
                        stmp[0:1, 3 * TS:4 * TS], start=True, stop=True),
                        waits=[("scalar", mr)])
                    b1 = s.op("tensor", lambda h: h.matmul(
                        pC[:, 0:TS], c_onesrow[0:1, :],
                        stmp[0:1, 2 * TS:3 * TS], start=True, stop=True))
                    b1 = s.now("tensor")
                    s.op("scalar", lambda h, st=st: h.copy(
                        out=rB1[:, st * TS:st * TS + TS], in_=pC[:, 256:256 + TS]),
                        waits=[("tensor", b1), ("pool", s.now("pool"))])
                    bd = s.op("scalar", lambda h, st=st: h.copy(
                        out=csc1[:, st * TS:st * TS + TS], in_=pC[:, 0:TS]),
                        inc=True)
                    return bd

                bd = ln_rows_bcast(0)
                yield

                smm_t = {}
                for t in range(16):
                    p1 = s.op("gpsimd", lambda h, t=t: h.tensor_tensor(
                        out=csc2[:, 0:TS], in0=zdn[:, t * TS:t * TS + TS],
                        in1=csc1[:, 0:TS], op=ALU.subtract),
                        inc=False, waits=[("scalar", bd)] if t == 0 else (), sem="pool")
                    s.op("gpsimd", lambda h: h.tensor_tensor(
                        out=csc2[:, 0:TS], in0=csc2[:, 0:TS], in1=rB1[:, 0:TS],
                        op=ALU.mult), sem="pool")
                    x1d = s.op("gpsimd", lambda h, t=t: h.tensor_tensor(
                        out=x1sl[:, t * TS:t * TS + TS], in0=csc2[:, 0:TS],
                        in1=xinb[:, t * TS:t * TS + TS], op=ALU.add), inc=True,
                        waits=[("gpsimd", ldxi)] if t == 0 else (), sem="pool")
                    zwv = [("pool", x1d)]
                    if t >= 2:
                        zwv.append(("tensor", smm_t[t - 2]))
                    zev = s.op("scalar", lambda h, t=t: h.activation(
                        out=zq2[:, (t % 2) * TS:(t % 2) * TS + TS],
                        in_=x1sl[:, t * TS:t * TS + TS],
                        func=AF.Square), inc=True, waits=zwv)
                    s.op("tensor", lambda h, t=t: h.matmul(
                        pC[0:1, 256:256 + TS], c_onesb[:],
                        x1sl[:, t * TS:t * TS + TS],
                        start=(t == 0), stop=(t == 15)),
                        waits=[("scalar", zev)])
                    s.op("tensor", lambda h, t=t: h.matmul(
                        pC[32:33, 256:256 + TS], c_onesb[:, 0:1],
                        zq2[:, (t % 2) * TS:(t % 2) * TS + TS],
                        start=(t == 0), stop=(t == 15)))
                    smm_t[t] = s.now("tensor")
                    if t % 4 == 3:
                        yield
                smm = smm_t[15]
                s.op("scalar", lambda h: h.copy(
                    out=stmp[0:1, 0:TS], in_=pC[0:1, 256:256 + TS]),
                    waits=[("tensor", smm)])
                sr = s.op("scalar", lambda h: h.copy(
                    out=stmp[0:1, TS:2 * TS], in_=pC[32:33, 256:256 + TS]),
                    inc=True)
                s.op("vector", lambda h: h.tensor_copy(
                    out=sac1[0:1, :], in_=stmp[0:1, 0:TS]),
                    inc=True, waits=[("scalar", sr)])
                s.op("vector", lambda h: h.tensor_copy(
                    out=sac2[0:1, :], in_=stmp[0:1, TS:2 * TS]), inc=True)

                bd = ln_rows_bcast(1)
                import os as _os2
                if _os2.environ.get("KERN_DBG") and q == NQ - 1:
                    s.op("sync", lambda h: h.dma_start(
                        out=dbg[0:1, 0:TS], in_=sac1[0:1, :]), inc=True,
                        waits=[("scalar", bd)])
                    s.op("sync", lambda h: h.dma_start(
                        out=dbg[0:1, 512:512 + TS], in_=sac2[0:1, :]), inc=True)
                    s.op("sync", lambda h: h.dma_start(
                        out=dbg[0:1, 1024:1024 + TS],
                        in_=rB1[0:1, TS:2 * TS]), inc=True)
                    s.op("sync", lambda h: h.dma_start(
                        out=dbg[0:1, 1536:1536 + TS],
                        in_=csc1[0:1, TS:2 * TS]), inc=True)
                    s.op("gpsimd", lambda h: h.dma_start(
                        out=dbg[0:1, 2048:2048 + TS], in_=zq2[0:1, 0:TS]),
                        inc=True)
                    s.op("gpsimd", lambda h: h.dma_start(
                        out=dbg[0:1, 2560:2560 + TS],
                        in_=x1sl[0:1, 15 * TS:15 * TS + TS]), inc=True)
                yield
                for t in range(16):
                    p1 = s.op("gpsimd", lambda h, t=t: h.tensor_tensor(
                        out=csc2[:, 0:TS], in0=x1sl[:, t * TS:t * TS + TS],
                        in1=csc1[:, TS:2 * TS], op=ALU.subtract),
                        inc=False, waits=[("scalar", bd)] if t == 0 else (), sem="pool")
                    s.op("gpsimd", lambda h, t=t: h.tensor_tensor(
                        out=x2sl[:, t * TS:t * TS + TS], in0=csc2[:, 0:TS],
                        in1=rB1[:, TS:2 * TS], op=ALU.mult), inc=(t == 15),
                        sem="pool")
                    if t % 8 == 7:
                        yield
                x2_d = s.now("pool")

                guv_done = {}
                for ht in range(48):
                    sl4 = ht % 2
                    wv = [("pool", x2_d), pf_wait("g", ht)]
                    if ht >= 2:
                        wv.append(("tensor", guv_done[ht - 2]))
                    s.op("sync", lambda h, ht=ht, sl4=sl4: h.dma_start(
                        out=g8s[:, sl4 * 2048:sl4 * 2048 + 2048],
                        in_=g8_dram.ap()[:, ht * 2048:ht * 2048 + 2048]),
                        inc=True, waits=wv)
                    s.op("sync", lambda h, ht=ht, sl4=sl4: h.dma_start(
                        out=u8s[:, sl4 * 2048:sl4 * 2048 + 2048],
                        in_=u8_dram.ap()[:, ht * 2048:ht * 2048 + 2048]),
                        inc=True, waits=[pf_wait("u", ht)])
                    wgu = s.now("sync")
                    yield

                    for d in range(16):
                        wv = [("sync", wgu)] if d == 0 else ()
                        s.op("tensor", lambda h, d=d, sl4=sl4: h.matmul(
                            pC[:, 0:TS],
                            g8s[:, sl4 * 2048 + d * 128:
                                sl4 * 2048 + d * 128 + 128],
                            x2sl[:, d * TS:d * TS + TS],
                            start=(d == 0), stop=(d == 15)), waits=wv)
                    for d in range(16):
                        s.op("tensor", lambda h, d=d, sl4=sl4: h.matmul(
                            pC[:, 256:256 + TS],
                            u8s[:, sl4 * 2048 + d * 128:
                                sl4 * 2048 + d * 128 + 128],
                            x2sl[:, d * TS:d * TS + TS],
                            start=(d == 0), stop=(d == 15)))
                    gu_pe = s.now("tensor")
                    guv_done[ht] = gu_pe
                    s.op("scalar", lambda h, ht=ht: h.activation(
                        out=gth[:], in_=pC[:, 0:TS], func=AF.Exp,
                        scale=ngs[:, ht:ht + 1]), waits=[("tensor", gu_pe)])
                    s.op("scalar", lambda h: h.activation(
                        out=gth[:], in_=gth[:], func=AF.Ln, scale=1.0,
                        bias=c_onesc[:]))
                    s.op("scalar", lambda h: h.activation(
                        out=gth[:], in_=gth[:], func=AF.Exp, scale=-1.0))
                    s.op("scalar", lambda h, ht=ht: h.activation(
                        out=gxh[:], in_=pC[:, 0:TS], func=AF.Copy,
                        scale=gscb[:, ht:ht + 1]))
                    guc = s.op("scalar", lambda h, ht=ht: h.activation(
                        out=guv[:], in_=pC[:, 256:256 + TS], func=AF.Copy,
                        scale=uscb[:, ht:ht + 1]), inc=True)
                    s.op("gpsimd", lambda h: h.tensor_tensor(
                        out=gvv[:], in0=gth[:], in1=gxh[:], op=ALU.mult),
                        inc=False, waits=[("scalar", guc)] + (
                            [("tensor", s.now("tensor"))] if ht == 0 else []),
                        sem="pool")
                    s.op("gpsimd", lambda h, ht=ht: h.tensor_tensor(
                        out=fusl[:, ht * TS:ht * TS + TS], in0=gvv[:],
                        in1=guv[:], op=ALU.mult), inc=True, sem="pool")
                    s.barrier("tensor", ["scalar"])
                    yield
                fu_d = s.now("pool")

                dn_done = {}
                for t in range(16):
                    sl2 = t % 2
                    wv = [("pool", fu_d), pf_wait("d", t)]
                    if t >= 1:
                        wv.append(("tensor", dn_done[t - 1]))
                    s.op("sync", lambda h, t=t: h.dma_start(
                        out=d8s[:],
                        in_=d8_dram.ap()[:, t * 6144:t * 6144 + 6144]),
                        inc=True, waits=wv)
                    dw_d = s.now("sync")
                    yield
                    for half in range(2):
                        for dd in range(24):
                            ddv = half * 24 + dd
                            wv = [("sync", dw_d)] \
                                if (half == 0 and dd == 0) else ()
                            s.op("tensor", lambda h, ddv=ddv: h.matmul(
                                pC[:, 0:TS],
                                d8s[:, ddv * 128:ddv * 128 + 128],
                                fusl[:, ddv * TS:ddv * TS + TS],
                                start=(ddv == 0), stop=(ddv == 47)), waits=wv)
                        yield
                    dn_pe = s.now("tensor")
                    dn_done[t] = dn_pe
                    s.op("scalar", lambda h, t=t: h.activation(
                        out=zdn[:, t * TS:t * TS + TS], in_=pC[:, 0:TS],
                        func=AF.Copy, scale=dscb[:, t:t + 1]),
                        waits=[("tensor", dn_pe)])
                    s.op("scalar", lambda h, t=t: h.copy(
                        out=zb2[:, (t % 2) * TS:(t % 2) * TS + TS],
                        in_=zdn[:, t * TS:t * TS + TS]))
                    zev = s.op("scalar", lambda h, t=t: h.activation(
                        out=zq2[:, (t % 2) * TS:(t % 2) * TS + TS],
                        in_=zb2[:, (t % 2) * TS:(t % 2) * TS + TS],
                        func=AF.Square), inc=True)
                    s.op("tensor", lambda h, t=t: h.matmul(
                        pC[0:1, 256:256 + TS], c_onesb[:],
                        zb2[:, (t % 2) * TS:(t % 2) * TS + TS],
                        start=True, stop=True),
                        waits=[("scalar", zev), ("scalar", s.now("scalar"))])
                    s.op("tensor", lambda h, t=t: h.matmul(
                        pC[32:33, 256:256 + TS], c_onesb[:, 0:1],
                        zq2[:, (t % 2) * TS:(t % 2) * TS + TS],
                        start=True, stop=True))
                    smm = s.now("tensor")
                    s.op("scalar", lambda h: h.copy(
                        out=stmp[0:1, 0:TS], in_=pC[0:1, 256:256 + TS]),
                        waits=[("tensor", smm)])
                    sr = s.op("scalar", lambda h: h.copy(
                        out=stmp[0:1, TS:2 * TS], in_=pC[32:33, 256:256 + TS]),
                        inc=True)
                    if t == 0:
                        s.op("vector", lambda h: h.tensor_copy(
                            out=sac1[0:1, :], in_=stmp[0:1, 0:TS]),
                            inc=True, waits=[("scalar", sr)])
                        s.op("vector", lambda h: h.tensor_copy(
                            out=sac2[0:1, :], in_=stmp[0:1, TS:2 * TS]),
                            inc=True)
                    else:
                        s.op("vector", lambda h: h.tensor_tensor(
                            out=sac1[0:1, :], in0=sac1[0:1, :],
                            in1=stmp[0:1, 0:TS], op=ALU.add),
                            inc=True, waits=[("scalar", sr)])
                        s.op("vector", lambda h: h.tensor_tensor(
                            out=sac2[0:1, :], in0=sac2[0:1, :],
                            in1=stmp[0:1, TS:2 * TS], op=ALU.add), inc=True)
                    yield

                bd = ln_rows_bcast(2)
                yield
                trprev = {}
                for t in range(16):
                    cs0 = (t % 2) * TS
                    wv0 = [("scalar", bd)] if t == 0 else []
                    if t >= 2:
                        wv0.append(("tensor", trprev[t - 2]))
                    s.op("gpsimd", lambda h, t=t, cs0=cs0: h.tensor_tensor(
                        out=csc2[:, cs0:cs0 + TS], in0=zdn[:, t * TS:t * TS + TS],
                        in1=csc1[:, 2 * TS:3 * TS], op=ALU.subtract),
                        inc=False, waits=wv0, sem="pool")
                    s.op("gpsimd", lambda h, cs0=cs0: h.tensor_tensor(
                        out=csc2[:, cs0:cs0 + TS], in0=csc2[:, cs0:cs0 + TS],
                        in1=rB1[:, 2 * TS:3 * TS], op=ALU.mult), sem="pool")
                    ov = s.op("gpsimd", lambda h, t=t, cs0=cs0: h.tensor_tensor(
                        out=csc2[:, cs0:cs0 + TS], in0=csc2[:, cs0:cs0 + TS],
                        in1=x1sl[:, t * TS:t * TS + TS], op=ALU.add), inc=True, sem="pool")
                    for iq in range(TS // 128):
                        trq = s.op("tensor", lambda h, iq=iq, cs0=cs0: h.transpose(
                            pC.ap().bitcast(F32)[:, 0:128],
                            csc2[:, cs0 + iq * 128:cs0 + iq * 128 + 128],
                            c_idf[:]), inc=False,
                            waits=[("pool", ov), ("sync", outd.get(iq ^ 1, 0))]
                            if iq == 0 else [("sync", outd.get(iq ^ 1, 0))])
                        trq = s.now("tensor")
                        otd = s.op("scalar", lambda h, iq=iq: h.copy(
                            out=outp[:, iq * 128:iq * 128 + 128],
                            in_=pC.ap().bitcast(F32)[:, 0:128]), inc=True,
                            waits=[("tensor", trq)])
                        s.barrier("tensor", ["scalar"])
                        s.op("sync", lambda h, t=t, iq=iq, t0q=t0q: h.dma_start(
                            out=out[t0q + iq * 128: t0q + iq * 128 + 128,
                                    t * 128:t * 128 + 128],
                            in_=outp[:, iq * 128:iq * 128 + 128]), inc=True,
                            waits=[("scalar", otd)])
                        outd[iq] = s.now("sync")
                    trprev[t] = s.now("tensor")
                    if t % 4 == 3:
                        yield

            outd = {}
            import collections as _c
            pending = _c.deque()

            def pull(k):
                done = 0
                while pending and done < k:
                    try:
                        next(pending[0])
                        done += 1
                    except StopIteration:
                        pending.popleft()

            globals_pull[0] = pull

            for g in range(NG + 1):
                if g >= 1:
                    scan_group(g - 1)
                if g >= 1 and g % GQ == 0:
                    q = g // GQ - 1
                    emit_collective(q)
                    pending.append(c_wave(q))
                if g < NG:
                    qkv_group(g)
                    pf_pump(5)
            pf_pump(10 ** 9)
            while pending:
                pull(1000)
            s.barrier("gpsimd", ["sync", "vector", "scalar", "tensor"])

        s.emit(block, sems)
    return nc


def stage_inputs(inputs, T):
    Tc = T // NCORE
    x = np.ascontiguousarray(inputs["L0_attn_x"][0, :T].T.astype(np.float32))
    xT_s = np.ascontiguousarray(x.reshape(16, 128, T).transpose(1, 0, 2))

    Wq, Wk, Wv = (inputs["L0_attn_Wq"], inputs["L0_attn_Wk"], inputs["L0_attn_Wv"])
    Wbeta = inputs["L0_attn_Wbeta"]
    Wg1 = inputs["L0_attn_Wg1"]
    Wo_ = inputs["L0_attn_Wo"]
    Wg2_ = inputs["L0_attn_Wg2"]
    lnw, lnb = inputs["L0_attn_ln_w"], inputs["L0_attn_ln_b"]
    nw = inputs["L0_attn_norm_w"]
    gw = inputs["L0_ffn_gate_w"]
    uw = inputs["L0_ffn_up_w"]
    dw = inputs["L0_ffn_down_w"]

    wo_s = np.ascontiguousarray(
        Wo_.reshape(16, 128, 16, 128).transpose(2, 1, 0, 3)).reshape(16, 128, -1)
    gwt_s = np.ascontiguousarray(
        gw.reshape(16, 128, 48, 128).transpose(2, 1, 0, 3)).reshape(48, 128, -1)
    uwt_s = np.ascontiguousarray(
        uw.reshape(16, 128, 48, 128).transpose(2, 1, 0, 3)).reshape(48, 128, -1)
    dwt_s = np.ascontiguousarray(
        dw.reshape(48, 128, 16, 128).transpose(2, 1, 0, 3)).reshape(16, 128, -1)
    gsc_s = np.ascontiguousarray(inputs["L0_ffn_gate_scale"].reshape(48, 128).T)
    usc_s = np.ascontiguousarray(inputs["L0_ffn_up_scale"].reshape(48, 128).T)
    dsc_s = np.ascontiguousarray(inputs["L0_ffn_down_scale"].reshape(16, 128).T)
    sidx = np.arange(128)
    mkK_s = ((sidx[:, None] // CH) < (sidx[None, :] // CH)).astype(np.float32)
    mkG_s = (sidx[:, None] <= sidx[None, :]).astype(np.float32)
    mkKL_s = np.ascontiguousarray(mkK_s[0:64]); mkKH_s = np.ascontiguousarray(mkK_s[64:128])
    mkGL_s = np.ascontiguousarray(mkG_s[0:64]); mkGH_s = np.ascontiguousarray(mkG_s[64:128])

    per_core = []
    for c in range(NCORE):
        h0, h1 = 2 * c, 2 * c + 1
        cols = np.zeros((D, 1024), np.float32)
        cols[:, 0:128] = Wq[:, h0 * 128:(h0 + 1) * 128]
        cols[:, 128:256] = Wq[:, h1 * 128:(h1 + 1) * 128]
        cols[:, 256:384] = Wk[:, h0 * 128:(h0 + 1) * 128]
        cols[:, 384:512] = Wk[:, h1 * 128:(h1 + 1) * 128]
        cols[:, 512:640] = Wv[:, h0 * 128:(h0 + 1) * 128]
        cols[:, 640:768] = Wv[:, h1 * 128:(h1 + 1) * 128]
        cols[:, 768:896] = Wg1
        cols[:, 896] = 1.0
        cols[:, 896 + 32] = Wbeta[:, h0]
        cols[:, 896 + 64] = Wbeta[:, h1]
        wf_s = np.ascontiguousarray(
            cols.reshape(16, 128, 8, 128).transpose(1, 0, 2, 3)).reshape(128, -1)
        wg2_s = np.ascontiguousarray(np.concatenate(
            [Wg2_[:, h0 * 128:(h0 + 1) * 128], Wg2_[:, h1 * 128:(h1 + 1) * 128]],
            axis=1))
        lnw_cs = np.ascontiguousarray(
            np.stack([lnw[h0], lnw[h1]], axis=1).astype(np.float32))
        lnb_cs = np.ascontiguousarray(
            np.stack([lnb[h0], lnb[h1]], axis=1).astype(np.float32))
        nw_cs = np.ascontiguousarray(np.stack(
            [nw[h0 * 128:(h0 + 1) * 128], nw[h1 * 128:(h1 + 1) * 128]],
            axis=1).astype(np.float32))
        NLGs = Tc // 128
        xres_s = np.ascontiguousarray(np.concatenate(
            [xT_s[:, :, (8 * lg + c) * 128:(8 * lg + c) * 128 + 128]
             for lg in range(NLGs)], axis=2))

        per_core.append({
            "xT": xT_s, "wf": wf_s, "wo": wo_s, "wg2": wg2_s,
            "lnw_c": lnw_cs, "lnb_c": lnb_cs, "nw_c": nw_cs,
            "gwt": gwt_s, "uwt": uwt_s, "dwt": dwt_s,
            "gsc": gsc_s, "usc": usc_s, "dsc": dsc_s,
            "idn": np.eye(128, dtype=BF),
            "mkKL": mkKL_s, "mkKH": mkKH_s, "mkGL": mkGL_s, "mkGH": mkGH_s,
            "onesc": np.ones((128, 1), np.float32),
            "onesf": np.ones((128, 128), np.float32),
            "onesrow": np.ones((1, 128), np.float32),
            "xres": xres_s,
        })
    return per_core


def kernel(**inputs):
    T = inputs["L0_attn_x"].shape[1]
    nc = build_program(T)
    in_maps = stage_inputs(inputs, T)
    res = run_bass_kernel_spmd(nc, in_maps, core_ids=list(range(NCORE)))
    outs = [np.asarray(res.results[c]["out"]) for c in range(NCORE)]
    Tc = T // NCORE
    full = np.empty((T, D), np.float32)
    for c in range(NCORE):
        for lg in range(Tc // 128):
            g = 8 * lg + c
            full[g * 128:(g + 1) * 128] = outs[c][lg * 128:(lg + 1) * 128]
    return full[None].astype(np.float32)


# version 12
# speedup vs baseline: 1.3213x; 1.1516x over previous
import os as _os
import numpy as np
import ml_dtypes
import concourse.bass as bass
import concourse.mybir as mybir
from concourse.bass_utils import run_bass_kernel_spmd
from concourse.aot_env import allow_ici_all_to_all

F32 = mybir.dt.float32
BF16 = mybir.dt.bfloat16
F8 = mybir.dt.float8e4
AF = mybir.ActivationFunctionType
ALU = mybir.AluOpType
BF = ml_dtypes.bfloat16

D = 2048
DH = 128
CH = 16
GT = 128
NCORE = 8
EPS_A = 1e-6
EPS_M = 1e-5

ENGS = ("sync", "gpsimd", "vector", "scalar", "tensor")


COMPUTE_ENGS = ("vector", "scalar", "tensor")


class Sched:

    def __init__(self):
        self.ops = {e: [] for e in ENGS}
        self.cnt = {e: 0 for e in ENGS}
        self.cnt["pool"] = 0
        self.step = {"sync": 16, "gpsimd": 16, "vector": 1, "scalar": 1,
                     "tensor": 1, "pool": 1}

    def op(self, eng, fn, inc=False, waits=(), sem=None):
        ch = sem or eng
        self.ops[eng].append((fn, bool(inc), tuple(waits), ch))
        if ch == "pool" or eng in COMPUTE_ENGS or inc:
            self.cnt[ch] += self.step[ch]
        return self.cnt[ch]

    def now(self, eng):
        return self.cnt[eng]

    def barrier(self, eng, others):
        w = tuple((o, self.cnt[o]) for o in others if self.cnt[o] > 0)
        if w:
            self.ops[eng].append((None, False, w, eng))

    def emit(self, block, sems):
        import os as _os
        import bisect
        ssv = _os.environ.get("BASS_SELFSYNC", "vector")
        full = ssv == "full"
        depth = int(_os.environ.get("BASS_SELFSYNC_DEPTH", "4"))
        full_engs = set(ssv.split(",")) if ssv not in ("min", "full") else (
            set(COMPUTE_ENGS) if full else set())
        waited = {e: set() for e in ENGS}
        waited["pool"] = set()
        for e in list(self.ops):
            for fn, inc, waits, ch in self.ops[e]:
                for we, wv in waits:
                    if wv > 0 and we in waited:
                        waited[we].add(wv)
        flagged = {e: sorted(waited[e]) for e in COMPUTE_ENGS}

        class _TProxy:
            def __init__(self, h):
                self._h = h

            def __getattr__(self, k):
                return getattr(self._h, k)

            def matmul(self, *a, **kw):
                kw.setdefault("skip_group_check", True)
                return self._h.matmul(*a, **kw)

        def rv(we, wv):
            if we not in COMPUTE_ENGS or we in full_engs:
                return wv
            import bisect as _b
            return _b.bisect_right(flagged[we], wv)

        def mk(eng):
            selfsync = eng in COMPUTE_ENGS
            efull = eng in full_engs
            miles = sorted(waited[eng]) if not selfsync else []
            fset = set(flagged.get(eng, ()))

            def body(h):
                n = 0
                npool = 0
                enforced = 0
                mi = 0
                for fn, inc, waits, ch in self.ops[eng]:
                    for we, wv in waits:
                        wv2 = rv(we, wv)
                        if wv2 > 0:
                            h.wait_ge(sems[we], wv2)
                    if fn is not None:
                        if eng == "tensor":
                            h2 = _TProxy(h)
                        else:
                            h2 = h
                        if ch == "pool":
                            ins = fn(h2)
                            npool += 1
                            ins.then_inc(sems["pool"], 1)
                            continue
                        if selfsync:
                            if efull and n >= depth:
                                h.wait_ge(sems[eng], n - depth + 1)
                        elif inc:
                            tgt = 0
                            while mi < len(miles) and miles[mi] <= n:
                                tgt = miles[mi]
                                mi += 1
                            if tgt > enforced:
                                h.wait_ge(sems[eng], tgt)
                                enforced = tgt
                        ins = fn(h2)
                        if selfsync:
                            n += 1
                            if efull:
                                ins.then_inc(sems[eng], 1)
                            elif n in fset:
                                ins.then_inc(sems[eng], 1)
                        elif inc:
                            ins.then_inc(sems[eng], self.step[eng])
                            n += self.step[eng]
            return body
        block.sync(mk("sync"))
        block.gpsimd(mk("gpsimd"))
        block.vector(mk("vector"))
        block.scalar(mk("scalar"))
        block.tensor(mk("tensor"))



def blk2(ap0, c0, w, blk):
    a = ap0[:, c0:c0 + w]
    return bass.AP(tensor=a.tensor, offset=a.offset,
                   ap=[list(a.ap[0]), [blk, 2], [1, w]])

def build_program(T):
    NG = T // GT
    Tc = T // NCORE
    NLG = Tc // GT
    NCH = GT // CH
    LGQ = 2 if NLG >= 2 else 1
    NQ = NLG // LGQ
    GQ = NCORE * LGQ
    TS = LGQ * GT
    assert T % (GT * NCORE) == 0 and NLG % LGQ == 0

    import os as _os
    nc = bass.Bass("TRN2", target_bir_lowering=False,
                   detect_race_conditions=not _os.environ.get("KERN_NORACE"))
    s = Sched()

    def din(name, shape, dt):
        return nc.dram_tensor(name, shape, dt, kind="ExternalInput").ap()

    xT = din("xT", [128, 16, T], F32)
    wf = din("wf", [128, 16 * 8 * 128], F32)
    wo = din("wo", [16, 128, 16 * 128], F32)
    wg2 = din("wg2", [128, 256], F32)
    lnw_c = din("lnw_c", [128, 2], F32)
    lnb_c = din("lnb_c", [128, 2], F32)
    nw_c = din("nw_c", [128, 2], F32)
    gwt = din("gwt", [48, 128, 16 * 128], F32)
    uwt = din("uwt", [48, 128, 16 * 128], F32)
    dwt = din("dwt", [16, 128, 48 * 128], F32)
    gsc = din("gsc", [128, 48], F32)
    usc = din("usc", [128, 48], F32)
    dsc = din("dsc", [128, 16], F32)
    idn = din("idn", [128, 128], BF16)
    mkKL = din("mkKL", [64, 128], F32)
    mkKH = din("mkKH", [64, 128], F32)
    mkGL = din("mkGL", [64, 128], F32)
    mkGH = din("mkGH", [64, 128], F32)
    onesc = din("onesc", [128, 1], F32)
    onesf = din("onesf", [128, 128], F32)
    onesrow = din("onesrow", [1, 128], F32)
    xres = din("xres", [128, 16, Tc], F32)

    out = nc.dram_tensor("out", [Tc, D], F32, kind="ExternalOutput").ap()
    o_stage = [nc.dram_tensor("o_stage%d" % q, [NCORE * LGQ * 2 * 128, GT], BF16)
               for q in range(NQ)]
    o_gath = [nc.dram_tensor("o_gath%d" % q, [NCORE * LGQ * 2 * 128, GT], BF16)
              for q in range(NQ)]
    o_st5 = [t.ap().rearrange("(a l hh p) c -> a l hh p c",
                              a=NCORE, l=LGQ, hh=2, p=128) for t in o_stage]
    o_gt5 = [t.ap().rearrange("(a l hh p) c -> a l hh p c",
                              a=NCORE, l=LGQ, hh=2, p=128) for t in o_gath]
    dbg = nc.dram_tensor("dbg", [1, 8 * 512], F32, kind="ExternalOutput").ap() \
        if _os.environ.get("KERN_DBG") else None
    g8_dram = nc.dram_tensor("g8_dram", [128, 48 * 16 * 128], F8)
    u8_dram = nc.dram_tensor("u8_dram", [128, 48 * 16 * 128], F8)
    d8_dram = nc.dram_tensor("d8_dram", [128, 16 * 48 * 128], F8)
    wobf_dram = nc.dram_tensor("wobf_dram", [16, 128, 16 * 128], BF16)

    import contextlib
    es = contextlib.ExitStack()
    with es:
        def sb(name, shape, dt):
            return es.enter_context(nc.sbuf_tensor(name, shape, dt))

        pbank = [es.enter_context(nc.psum_tensor("pb%d" % i, [128, 512], F32))
                 for i in range(8)]
        pqA, pqB, pA, pB, pS, pC, pQM, pSM = pbank
        pSC = pSM
        pSMb = pSM.ap().bitcast(BF16)

        sems = {e: es.enter_context(nc.semaphore("sem_" + e)) for e in ENGS}
        sems["cc"] = es.enter_context(nc.semaphore("sem_cc"))
        sems["pool"] = es.enter_context(nc.semaphore("sem_pool"))
        block = es.enter_context(nc.Block())

        c_id = sb("c_id", [128, 128], BF16)
        c_idf = sb("c_idf", [128, 128], F32)
        c_mkKL = sb("c_mkKL", [64, 128], F32)
        c_mkKH = sb("c_mkKH", [64, 128], F32)
        c_mkGL = sb("c_mkGL", [64, 128], F32)
        c_mkGH = sb("c_mkGH", [64, 128], F32)
        c_ones = sb("c_ones", [128, 128], F32)
        c_onesc = sb("c_onesc", [128, 1], F32)
        c_onesb = sb("c_onesb", [128, 1], BF16)
        c_onesrow = sb("c_onesrow", [1, 128], F32)
        c_lnw = sb("c_lnw", [128, 2], F32)
        c_lnb = sb("c_lnb", [128, 2], F32)
        c_nw = sb("c_nw", [128, 2], F32)
        c_w2 = sb("c_w2", [128, 2], F32)
        c_wb = sb("c_wb", [128, 2], F32)
        c_negw = sb("c_negw", [128, 2], F32)
        c_w2B = sb("c_w2B", [128, 32], F32)
        c_epsA = sb("c_epsA", [128, 1], F32)
        c_epsM = sb("c_epsM", [128, 1], F32)

        for dst, src in [(c_id, idn), (c_mkKL, mkKL), (c_mkKH, mkKH),
                         (c_mkGL, mkGL), (c_mkGH, mkGH), (c_ones, onesf),
                         (c_onesc, onesc), (c_onesrow, onesrow),
                         (c_lnw, lnw_c), (c_lnb, lnb_c), (c_nw, nw_c)]:
            s.op("sync", lambda h, d=dst, sr=src: h.dma_start(out=d[:], in_=sr),
                 inc=True)
        ld0 = s.now("sync")

        oslab = sb("oslab", [128, 16 * TS], BF16)
        xinb = sb("xinb", [128, 16 * TS], BF16)
        zdn = sb("zdn", [128, 16 * TS], F32)
        x1sl = sb("x1sl", [128, 16 * TS], BF16)
        x2sl = sb("x2sl", [128, 16 * TS], BF16)
        fusl = sb("fusl", [128, 48 * TS], BF16)
        wrng = sb("wrng", [128, 4 * 2048], BF16)
        g8s = sb("g8s", [128, 2 * 2048], F8)
        u8s = sb("u8s", [128, 2 * 2048], F8)
        d8s = sb("d8s", [128, 6144], F8)
        zb2 = sb("zb2", [128, 2 * TS], BF16)
        zq2 = sb("zq2", [128, 2 * TS], BF16)
        gth = sb("gth", [128, TS], F32)
        gxh = sb("gxh", [128, TS], F32)
        guv = sb("guv", [128, TS], F32)
        gvv = sb("gvv", [128, TS], F32)
        sac1 = sb("sac1", [1, TS], F32)
        sac2 = sb("sac2", [1, TS], F32)
        stmp = sb("stmp", [1, 4 * TS], F32)
        rB1 = sb("rB1", [128, 3 * TS], F32)
        csc1 = sb("csc1", [128, 3 * TS], F32)
        csc2 = sb("csc2", [128, 2 * TS], F32)
        outp = sb("outp", [128, 2 * 128], F32)
        gscb = sb("gscb", [128, 48], F32)
        uscb = sb("uscb", [128, 48], F32)
        dscb = sb("dscb", [128, 16], F32)
        ngs = sb("ngs", [128, 48], F32)
        dbg16 = sb("dbg16", [1, 64], F32) if _os.environ.get("KERN_DBG") \
            else None

        s.op("sync", lambda h: h.dma_start(out=gscb[:], in_=gsc), inc=True)
        s.op("sync", lambda h: h.dma_start(out=uscb[:], in_=usc), inc=True)
        s.op("sync", lambda h: h.dma_start(out=dscb[:], in_=dsc), inc=True)
        ldsc = s.now("sync")
        s.op("vector", lambda h: h.tensor_scalar(
            out=ngs[:], in0=gscb[:], scalar1=-1.0, scalar2=None, op0=ALU.mult),
            inc=True, waits=[("sync", ldsc)])

        scan_done = {}
        qkv_evac_done = {}
        qkv_pe_done = {}
        for g in range(-4, 0):
            scan_done[g] = 0
            qkv_evac_done[g] = 0
            qkv_pe_done[g] = 0

        pf_jobs = []
        for t in range(16):
            pf_jobs.append(("wo", t, lambda h, t=t: h.dma_start(
                out=wobf_dram.ap()[t], in_=wo[t])))
        for ht in range(48):
            pf_jobs.append(("g", ht, lambda h, ht=ht: h.dma_start(
                out=g8_dram.ap()[:, ht * 2048:ht * 2048 + 2048], in_=gwt[ht])))
            pf_jobs.append(("u", ht, lambda h, ht=ht: h.dma_start(
                out=u8_dram.ap()[:, ht * 2048:ht * 2048 + 2048], in_=uwt[ht])))
        for t in range(16):
            pf_jobs.append(("d", t, lambda h, t=t: h.dma_start(
                out=d8_dram.ap()[:, t * 6144:t * 6144 + 6144], in_=dwt[t])))
        pf_done = {}
        pf_idx = [0]

        def pf_pump(k):
            while pf_idx[0] < len(pf_jobs) and k > 0:
                kind, idx, fn = pf_jobs[pf_idx[0]]
                s.op("gpsimd", fn, inc=True)
                pf_done[(kind, idx)] = s.now("gpsimd")
                pf_idx[0] += 1
                k -= 1

        def pf_wait(kind, idx):
            while (kind, idx) not in pf_done:
                pf_pump(1)
            return ("gpsimd", pf_done[(kind, idx)])

        esA = contextlib.ExitStack()
        with esA:
            def sba(name, shape, dt):
                return esA.enter_context(nc.sbuf_tensor(name, shape, dt))

            wfb = sba("wfb", [128, 16 * 8 * 128], BF16)
            wg2b = sba("wg2b", [128, 256], BF16)
            csWb = sba("csWb", [1, 8 * 128], BF16)
            W = 2
            xb = sba("xb", [128, W * 2048], BF16)
            sqb = sba("sqb", [128, 2048], BF16)
            qs_f = sba("qs_f", [128, W * 256], F32)
            qs_b = sba("qs_b", [128, W * 256], BF16)
            kn_f = sba("kn_f", [128, W * 256], F32)
            kn_b = sba("kn_b", [128, W * 256], BF16)
            v_f = sba("v_f", [128, W * 256], F32)
            g1_b = sba("g1_b", [128, W * 128], BF16)
            eT = sba("eT", [128, W * 4], F32)
            rowbuf = sba("rowbuf", [128, W * 128], F32)
            s_op = sba("s_op", [128, 256], BF16)
            vnstkL = sba("vnstkL", [64, 256], BF16)
            vnstkH = sba("vnstkH", [64, 256], BF16)
            aktL = sba("aktL", [64, 256], BF16)
            aktH = sba("aktH", [64, 256], BF16)
            agtL = sba("agtL", [64, 256], BF16)
            agtH = sba("agtH", [64, 256], BF16)
            vnstg = sba("vnstg", [128, 64], BF16)
            wbm = sba("wbm", [128, 256], F32)
            ketL = sba("ketL", [64, 256], BF16)
            ketH = sba("ketH", [64, 256], BF16)
            kescr = sba("kescr", [128, 128], BF16)
            scrA = sba("scrA", [128, 128], F32)
            scr1 = sba("scr1", [128, 32], F32)
            scr2 = sba("scr2", [128, 32], F32)
            rstdc = sba("rstdc", [128, 32], F32)
            khh = sba("khh", [128, 32], F32)
            strow = sba("strow", [1, 6 * 128], F32)
            strowb = sba("strowb", [1, 128], BF16)
            rstdB = sba("rstdB", [128, 128], F32)
            qscr1 = sba("qscr1", [128, 128], F32)
            qscr2 = sba("qscr2", [128, 128], F32)
            oscr = sba("oscr", [128, 256], F32)
            oth = sba("oth", [128, 128], F32)
            osig = sba("osig", [128, 128], F32)
            oo = sba("oo", [128, 128], F32)
            oo2 = sba("oo2", [128, 128], F32)
            obuf = sba("obuf", [128, 256], BF16)

            s.op("gpsimd", lambda h: h.dma_start(out=wfb[:], in_=wf), inc=True)
            s.op("gpsimd", lambda h: h.dma_start(out=wg2b[:], in_=wg2), inc=True)
            ld1 = s.now("gpsimd")

            s.op("vector", lambda h: h.tensor_tensor(
                out=c_w2[:], in0=c_lnw[:], in1=c_lnw[:], op=ALU.mult),
                waits=[("sync", ld0)])
            s.op("vector", lambda h: h.tensor_tensor(
                out=c_wb[:], in0=c_lnw[:], in1=c_lnb[:], op=ALU.mult))
            s.op("vector", lambda h: h.tensor_scalar(
                out=c_negw[:], in0=c_lnw[:], scalar1=-1.0, scalar2=None,
                op0=ALU.mult))
            s.op("vector", lambda h: h.tensor_copy(out=c_idf[:], in_=c_id[:]))
            s.op("vector", lambda h: h.tensor_copy(out=c_onesb[:], in_=c_onesc[:]))
            for hh in range(2):
                s.op("vector", lambda h, hh=hh: h.tensor_scalar(
                    out=c_w2B[:, hh * 16:hh * 16 + 16], in0=c_ones[:, 0:16],
                    scalar1=c_w2[:, hh:hh + 1], scalar2=None, op0=ALU.mult))
            s.op("vector", lambda h: h.memset(s_op[:], 0.0))
            s.op("vector", lambda h: h.memset(rowbuf[:], 0.0))
            s.op("vector", lambda h: h.memset(c_epsA[:], EPS_A))
            s.op("vector", lambda h: h.memset(c_epsM[:], EPS_M), inc=True)
            pre0 = s.now("vector")

            for t in range(8):
                for d in range(16):
                    wv = [("gpsimd", ld1), ("vector", pre0)] \
                        if (t == 0 and d == 0) else ()
                    s.op("tensor", lambda h, t=t, d=d: h.matmul(
                        pQM[0:1, 0:128], c_onesb[:],
                        wfb[:, (d * 8 + t) * 128:(d * 8 + t) * 128 + 128],
                        start=(d == 0), stop=(d == 15)), waits=wv)
                pe_cs = s.now("tensor")
                s.op("vector", lambda h, t=t: h.tensor_scalar(
                    out=csWb[0:1, t * 128:t * 128 + 128], in0=pQM[0:1, 0:128],
                    scalar1=-1.0, scalar2=None, op0=ALU.mult),
                    inc=True, waits=[("tensor", pe_cs)])
                s.barrier("tensor", ["vector"])

            def qkv_group(g):
                sg = g % W
                xo = sg * 2048
                ao = sg * 256
                wv = [("tensor", qkv_pe_done[g - W])] if g >= W else ()
                s.op("gpsimd", lambda h, g=g, xo=xo: h.dma_start(
                    out=xb[:, xo:xo + 2048].rearrange("p (d c) -> p d c", d=16),
                    in_=xT[:, :, g * GT:(g + 1) * GT]), inc=True, waits=wv)
                ldx = s.now("gpsimd")
                s.op("scalar", lambda h, xo=xo: h.activation(
                    out=sqb[:, 0:2048], in_=xb[:, xo:xo + 2048],
                    func=AF.Square), inc=True,
                    waits=[("gpsimd", ldx), ("tensor", qkv_pe_done[g - 1])])
                sqd = s.now("scalar")

                for t in range(8):
                    pq = pqA if t < 4 else pqB
                    for d in range(16):
                        wv = [("gpsimd", ldx), ("vector", qkv_evac_done[g - 1]),
                              ("vector", scan_done[g - W])] \
                            if (t == 0 and d == 0) else ()
                        s.op("tensor", lambda h, t=t, d=d, pq=pq, xo=xo: h.matmul(
                            pq[:, (t % 4) * 128:(t % 4) * 128 + 128],
                            wfb[:, (d * 8 + t) * 128:(d * 8 + t) * 128 + 128],
                            xb[:, xo + d * 128: xo + d * 128 + 128],
                            start=(t % 4 == 0 and d == 0), stop=(d == 15)),
                            waits=wv)
                for d in range(16):
                    wv = [("scalar", sqd), ("vector", scan_done[g - 1]),
                          ("scalar", s.now("scalar"))] if d == 0 else ()
                    s.op("tensor", lambda h, d=d, xo=xo: h.matmul(
                        pQM[0:1, 0:128], c_onesb[:],
                        sqb[:, d * 128: d * 128 + 128],
                        start=(d == 0), stop=(d == 15)), waits=wv)
                pe_mm = s.now("tensor")

                ext = pqB[:, 384:512]
                s.op("vector", lambda h, ext=ext: h.tensor_copy(
                    out=strow[0:1, 640:768], in_=ext[0:1, 0:128]),
                    waits=[("tensor", pe_mm)])
                s.op("vector", lambda h: h.tensor_tensor(
                    out=strow[0:1, 128:256], in0=strow[0:1, 640:768],
                    in1=strow[0:1, 640:768], op=ALU.mult))
                s.op("vector", lambda h: h.scalar_tensor_tensor(
                    out=strow[0:1, 128:256], in0=strow[0:1, 128:256],
                    scalar=-1.0 / D, in1=pQM[0:1, 0:128], op0=ALU.mult,
                    op1=ALU.add))
                s.op("vector", lambda h, ext=ext: h.tensor_scalar(
                    out=strowb[0:1, :], in0=ext[0:1, 0:128], scalar1=1.0 / D,
                    scalar2=None, op0=ALU.mult), inc=True)
                dve_r = s.now("vector")
                s.op("scalar", lambda h: h.activation(
                    out=strow[0:1, 256:384], in_=strow[0:1, 128:256], func=AF.Ln,
                    scale=1.0 / D, bias=c_epsM[0:1, :]), waits=[("vector", dve_r)])
                s.op("scalar", lambda h: h.activation(
                    out=strow[0:1, 0:128], in_=strow[0:1, 256:384], func=AF.Exp,
                    scale=-0.5), inc=True)
                act_r = s.now("scalar")

                for t in range(8):
                    pq = pqA if t < 4 else pqB
                    s.op("tensor", lambda h, t=t, pq=pq: h.matmul(
                        pq[:, (t % 4) * 128:(t % 4) * 128 + 128],
                        csWb[0:1, t * 128:t * 128 + 128], strowb[0:1, :],
                        start=False, stop=True),
                        waits=[("scalar", act_r)] if t == 0 else ())
                pe_fix = s.op("tensor", lambda h: h.matmul(
                    pQM[:, 128:256], c_onesrow[0:1, :], strow[0:1, 0:128],
                    start=True, stop=True), inc=True)
                qkv_pe_done[g] = pe_fix
                s.op("vector", lambda h: h.tensor_copy(
                    out=rstdB[:], in_=pQM[:, 128:256]), inc=True,
                    waits=[("tensor", pe_fix), ("vector", scan_done[g - W])])
                s.barrier("tensor", ["vector"])

                for hh in range(2):
                    tq = pqA[:, hh * 128:hh * 128 + 128]
                    qf = qs_f[:, ao + hh * 128: ao + hh * 128 + 128]
                    q_ev = s.op("vector", lambda h, tq=tq, qf=qf: h.tensor_tensor(
                        out=qf, in0=tq, in1=rstdB[:], op=ALU.mult), inc=True)
                    s.op("scalar", lambda h, qf=qf: h.activation(
                        out=oth[:], in_=qf, func=AF.Exp, scale=-1.0),
                        waits=[("vector", q_ev)])
                    s.op("scalar", lambda h: h.activation(
                        out=oth[:], in_=oth[:], func=AF.Ln, scale=1.0,
                        bias=c_onesc[:]))
                    a_t = s.op("scalar", lambda h: h.activation(
                        out=oth[:], in_=oth[:], func=AF.Exp, scale=-1.0),
                        inc=True)
                    s.op("vector", lambda h, qf=qf: h.tensor_tensor(
                        out=qf, in0=qf, in1=oth[:], op=ALU.mult),
                        waits=[("scalar", a_t)])
                    s.op("vector", lambda h, qf=qf, hh=hh, ao=ao: h.tensor_copy(
                        out=qs_b[:, ao + hh * 128: ao + hh * 128 + 128], in_=qf),
                        inc=True)
                    s.barrier("scalar", ["vector"])
                for hh in range(2):
                    tk = pqA[:, (2 + hh) * 128:(2 + hh) * 128 + 128]
                    kf = kn_f[:, ao + hh * 128: ao + hh * 128 + 128]
                    k_ev = s.op("vector", lambda h, tk=tk, kf=kf: h.tensor_tensor(
                        out=kf, in0=tk, in1=rstdB[:], op=ALU.mult), inc=True)
                    k_sq = s.op("scalar", lambda h, kf=kf: h.activation(
                        out=qscr1[:], in_=kf, func=AF.Square),
                        inc=True, waits=[("vector", k_ev)])
                    k_mm = s.op("tensor", lambda h: h.matmul(
                        pQM[0:1, 0:128], c_onesc[:], qscr1[:],
                        start=True, stop=True), inc=True, waits=[("scalar", k_sq)])
                    s.op("scalar", lambda h: h.activation(
                        out=strow[0:1, 384:512], in_=pQM[0:1, 0:128], func=AF.Ln,
                        scale=1.0, bias=c_epsA[0:1, :]), waits=[("tensor", k_mm)])
                    k_rn = s.op("scalar", lambda h: h.activation(
                        out=strow[0:1, 384:512], in_=strow[0:1, 384:512],
                        func=AF.Exp, scale=-0.5), inc=True)
                    k_bc = s.op("tensor", lambda h: h.matmul(
                        pQM[:, 128:256], c_onesrow[0:1, :], strow[0:1, 384:512],
                        start=True, stop=True), inc=True, waits=[("scalar", k_rn)])
                    s.op("vector", lambda h, kf=kf: h.tensor_tensor(
                        out=kf, in0=kf, in1=pQM[:, 128:256], op=ALU.mult),
                        waits=[("tensor", k_bc)])
                    s.op("vector", lambda h, kf=kf, hh=hh, ao=ao: h.tensor_copy(
                        out=kn_b[:, ao + hh * 128: ao + hh * 128 + 128], in_=kf),
                        inc=True)
                    s.barrier("tensor", ["vector"])
                    s.barrier("scalar", ["vector"])
                for hh in range(2):
                    tv = pqB[:, hh * 128:hh * 128 + 128]
                    s.op("vector", lambda h, tv=tv, hh=hh, ao=ao: h.tensor_tensor(
                        out=v_f[:, ao + hh * 128: ao + hh * 128 + 128],
                        in0=tv, in1=rstdB[:], op=ALU.mult))
                s.op("vector", lambda h: h.tensor_tensor(
                    out=qscr2[:], in0=pqB[:, 256:384], in1=rstdB[:], op=ALU.mult))
                s.op("vector", lambda h, sg=sg: h.tensor_copy(
                    out=g1_b[:, sg * 128:sg * 128 + 128], in_=qscr2[:]))
                s.op("vector", lambda h, sg=sg, ext=ext: h.tensor_tensor(
                    out=rowbuf[32:33, sg * 128:sg * 128 + 128], in0=ext[32:33, 0:128],
                    in1=rstdB[32:33, 0:128], op=ALU.mult))
                v_done = s.op("vector", lambda h, sg=sg, ext=ext: h.tensor_tensor(
                    out=rowbuf[64:65, sg * 128:sg * 128 + 128],
                    in0=ext[64:65, 0:128], in1=rstdB[64:65, 0:128], op=ALU.mult),
                    inc=True)
                for i, base in enumerate((32, 64)):
                    kw = [("vector", v_done)] if i == 0 else ()
                    rb = lambda h, base=base, sg=sg: rowbuf[
                        base:base + 1, sg * 128:sg * 128 + 128]
                    s.op("scalar", lambda h, base=base, sg=sg: h.activation(
                        out=rowbuf[base:base + 1, sg * 128:sg * 128 + 128],
                        in_=rowbuf[base:base + 1, sg * 128:sg * 128 + 128],
                        func=AF.Exp, scale=-1.0), waits=kw)
                    s.op("scalar", lambda h, base=base, sg=sg: h.activation(
                        out=rowbuf[base:base + 1, sg * 128:sg * 128 + 128],
                        in_=rowbuf[base:base + 1, sg * 128:sg * 128 + 128],
                        func=AF.Ln, scale=1.0, bias=c_onesc[0:1, :]))
                    s.op("scalar", lambda h, base=base, sg=sg: h.activation(
                        out=rowbuf[base:base + 1, sg * 128:sg * 128 + 128],
                        in_=rowbuf[base:base + 1, sg * 128:sg * 128 + 128],
                        func=AF.Exp, scale=-1.0), inc=(i == 1))
                e_th = s.now("scalar")
                for i, base in enumerate((32, 64)):
                    kw = [("scalar", e_th)] if i == 0 else ()
                    s.op("vector", lambda h, base=base, sg=sg: h.tensor_scalar(
                        out=rowbuf[base:base + 1, sg * 128:sg * 128 + 128],
                        in0=rowbuf[base:base + 1, sg * 128:sg * 128 + 128],
                        scalar1=-2.0, scalar2=None, op0=ALU.mult),
                        inc=(i == 1), waits=kw)
                e_neg = s.now("vector")
                i = 0
                for hhb, base in enumerate((32, 64)):
                    for half in range(2):
                        kw = [("vector", e_neg)] if i == 0 else ()
                        s.op("tensor", lambda h, base=base, sg=sg, i=i, half=half:
                             h.matmul(
                                 pQM[0:64, 256 + i:257 + i],
                                 rowbuf[base:base + 1,
                                        sg * 128 + half * 64:
                                        sg * 128 + half * 64 + 64],
                                 c_ones[base:base + 1, 0:1],
                                 start=True, stop=True),
                             inc=(i == 3), waits=kw)
                        i += 1
                e_tr = s.now("tensor")
                s.op("vector", lambda h, sg=sg: h.tensor_copy(
                    out=eT[0:64, sg * 4: sg * 4 + 4], in_=pQM[0:64, 256:260]),
                    inc=True, waits=[("tensor", e_tr)])
                s.barrier("tensor", ["vector"])
                qkv_evac_done[g] = s.now("vector")

            def scan_group(g):
                sg = g % W
                ao = sg * 256
                qev = qkv_evac_done[g]
                guard = qev

                if g >= 1:
                    for hh in range(2):
                        s.op("vector", lambda h, hh=hh: h.tensor_copy(
                            out=s_op[:, hh * 128:hh * 128 + 128],
                            in_=pS[:, hh * 128:hh * 128 + 128]),
                            inc=(hh == 1),
                            waits=[("vector", qev)] if hh == 0 else ())

                for hh in range(2):
                    kb = kn_b[:, ao + hh * 128: ao + hh * 128 + 128]
                    qb = qs_b[:, ao + hh * 128: ao + hh * 128 + 128]
                    for half, (aktX, agtX, mKX, mGX) in enumerate(
                            ((aktL, agtL, c_mkKL, c_mkGL),
                             (aktH, agtH, c_mkKH, c_mkGH))):
                        ec = eT[0:64, sg * 4 + hh * 2 + half:
                                sg * 4 + hh * 2 + half + 1]
                        mm1 = s.op("tensor", lambda h, kb=kb, half=half: h.matmul(
                            pSC[0:64, 256:384], kb[:, half * 64:half * 64 + 64],
                            kb, start=True, stop=True), inc=True,
                            waits=[("vector", qev)] if (hh == 0 and half == 0)
                            else ())
                        dv1 = s.op("vector", lambda h, hh=hh, ec=ec, aktX=aktX,
                                   mKX=mKX: h.scalar_tensor_tensor(
                                       out=aktX[:, hh * 128:hh * 128 + 128],
                                       in0=pSC[0:64, 256:384], scalar=ec,
                                       in1=mKX[:], op0=ALU.mult, op1=ALU.mult),
                                   inc=True, waits=[("tensor", mm1)])
                        mm2 = s.op("tensor", lambda h, kb=kb, qb=qb, half=half:
                                   h.matmul(
                                       pSC[0:64, 256:384],
                                       kb[:, half * 64:half * 64 + 64], qb,
                                       start=True, stop=True), inc=True,
                                   waits=[("vector", dv1)])
                        s.op("vector", lambda h, hh=hh, ec=ec, agtX=agtX,
                             mGX=mGX: h.scalar_tensor_tensor(
                                 out=agtX[:, hh * 128:hh * 128 + 128],
                                 in0=pSC[0:64, 256:384], scalar=ec, in1=mGX[:],
                                 op0=ALU.mult, op1=ALU.mult),
                             inc=True, waits=[("tensor", mm2)])
                        s.barrier("tensor", ["vector"])

                for hh in range(2):
                    s.op("vector", lambda h, hh=hh, ao=ao: h.tensor_tensor(
                        out=oscr[:, 0:128],
                        in0=v_f[:, ao + hh * 128:ao + hh * 128 + 128],
                        in1=kn_f[:, ao + hh * 128:ao + hh * 128 + 128],
                        op=ALU.subtract))
                    s.op("vector", lambda h, hh=hh: h.tensor_scalar(
                        out=wbm[:, hh * 128:hh * 128 + 128], in0=oscr[:, 0:128],
                        scalar1=c_negw[:, hh:hh + 1], scalar2=c_wb[:, hh:hh + 1],
                        op0=ALU.mult, op1=ALU.add), inc=True)
                    base = 32 + hh * 32
                    ebc = s.op("tensor", lambda h, sg=sg, base=base: h.matmul(
                        pQM[:, 128:256], c_ones[base:base + 1, 0:128],
                        rowbuf[base:base + 1, sg * 128:sg * 128 + 128],
                        start=True, stop=True), inc=True,
                        waits=[("vector", guard)] if hh == 0 else ())
                    kes = s.op("vector", lambda h, hh=hh, ao=ao: h.tensor_tensor(
                        out=kescr[:],
                        in0=kn_f[:, ao + hh * 128:ao + hh * 128 + 128],
                        in1=pQM[:, 128:256], op=ALU.mult), inc=True,
                        waits=[("tensor", ebc)])
                    ktr = 0
                    for half in range(2):
                        ktr = s.op("tensor", lambda h, half=half: h.transpose(
                            pSMb[0:64, half * 128:half * 128 + 128],
                            kescr[:, half * 64:half * 64 + 64], c_id[:]),
                            inc=True,
                            waits=[("vector", kes)] if half == 0 else ())
                    for half, ketX in enumerate((ketL, ketH)):
                        s.op("vector", lambda h, hh=hh, half=half, ketX=ketX:
                             h.tensor_copy(
                                 out=ketX[:, hh * 128:hh * 128 + 128],
                                 in_=pSMb[0:64, half * 128:half * 128 + 128]),
                             inc=(half == 1),
                             waits=[("tensor", ktr)] if half == 0 else ())
                    s.barrier("tensor", ["vector"])
                prep_done = s.now("vector")

                for hh in range(2):
                    s.op("tensor", lambda h, hh=hh, ao=ao: h.matmul(
                        pA[:, hh * 128:hh * 128 + 128],
                        s_op[:, hh * 128:hh * 128 + 128],
                        kn_b[:, ao + hh * 128:ao + hh * 128 + 128],
                        start=(hh == 0), stop=True),
                        waits=[("vector", prep_done)] if hh == 0 else ())
                    s.op("tensor", lambda h, hh=hh, ao=ao: h.matmul(
                        pB[:, hh * 128:hh * 128 + 128],
                        s_op[:, hh * 128:hh * 128 + 128],
                        qs_b[:, ao + hh * 128:ao + hh * 128 + 128],
                        start=(hh == 0), stop=True))
                _ = s.now("tensor")

                last_stk = s.now("vector")
                for j in range(NCH):
                    m = j // 2
                    c0 = j * CH
                    if j > 0:
                        KL = min(16 * j, 64)
                        KH = 16 * j - KL
                        for hh in range(2):
                            s.op("tensor", lambda h, hh=hh, KL=KL, c0=c0: h.matmul(
                                pA[:, hh * 128 + c0: hh * 128 + c0 + CH],
                                vnstkL[0:KL, hh * 128:hh * 128 + 128],
                                aktL[0:KL, hh * 128 + c0:hh * 128 + c0 + CH],
                                start=False, stop=True),
                                waits=[("vector", last_stk)] if hh == 0 else ())
                            if KH > 0:
                                s.op("tensor", lambda h, hh=hh, KH=KH, c0=c0:
                                     h.matmul(
                                         pA[:, hh * 128 + c0:
                                            hh * 128 + c0 + CH],
                                         vnstkH[0:KH, hh * 128:hh * 128 + 128],
                                         aktH[0:KH,
                                              hh * 128 + c0:hh * 128 + c0 + CH],
                                         start=False, stop=True))
                    corr_pe = s.now("tensor")

                    s.op("vector", lambda h, c0=c0: h.tensor_copy(
                        out=scrA[:, 0:32],
                        in_=blk2(pA, c0, CH, 128)),
                        waits=[("tensor", corr_pe)])
                    kh_ev = s.op("vector", lambda h: h.tensor_tensor(
                        out=scrA[:, 32:64], in0=scrA[:, 0:32], in1=scrA[:, 0:32],
                        op=ALU.mult), inc=True)
                    st1 = s.op("tensor", lambda h: h.matmul(
                        pSM[:, 128:192], c_ones[:], scrA[:, 0:64],
                        start=True, stop=True), inc=True, waits=[("vector", kh_ev)])
                    s.op("vector", lambda h: h.tensor_copy(
                        out=scr2[:], in_=pSM[:, 128:160]), waits=[("tensor", st1)])
                    s.op("vector", lambda h: h.tensor_tensor(
                        out=scr1[:], in0=scr2[:], in1=scr2[:], op=ALU.mult))
                    vpre = s.op("vector", lambda h: h.scalar_tensor_tensor(
                        out=scr1[:], in0=scr1[:], scalar=-1.0 / DH,
                        in1=pSM[:, 160:192], op0=ALU.mult, op1=ALU.add), inc=True)
                    s.op("scalar", lambda h: h.activation(
                        out=scr2[:], in_=scr1[:], func=AF.Ln, scale=1.0 / DH,
                        bias=c_epsA[:]), waits=[("vector", vpre)])
                    a_rs = s.op("scalar", lambda h: h.activation(
                        out=rstdc[:], in_=scr2[:], func=AF.Exp, scale=-0.5),
                        inc=True)
                    s.op("vector", lambda h: h.scalar_tensor_tensor(
                        out=khh[:], in0=pSM[:, 128:160], scalar=-1.0 / DH,
                        in1=scrA[:, 0:32], op0=ALU.mult, op1=ALU.add),
                        waits=[("scalar", a_rs)])
                    s.op("vector", lambda h: h.tensor_tensor(
                        out=khh[:], in0=khh[:], in1=rstdc[:], op=ALU.mult))
                    s.op("vector", lambda h: h.tensor_tensor(
                        out=scrA[:, 64:96], in0=khh[:], in1=c_w2B[:], op=ALU.mult))
                    s.op("vector", lambda h, c0=c0: h.tensor_tensor(
                        out=scrA[:, 64:96], in0=scrA[:, 64:96],
                        in1=blk2(wbm, c0, CH, 128), op=ALU.add))
                    gw_d = s.op("vector", lambda h: h.tensor_tensor(
                        out=scrA[:, 96:128], in0=scrA[:, 64:96], in1=khh[:],
                        op=ALU.mult), inc=True)
                    st2 = s.op("tensor", lambda h: h.matmul(
                        pSM[:, 192:256], c_ones[:], scrA[:, 64:128],
                        start=True, stop=True), inc=True, waits=[("vector", gw_d)])
                    s.op("vector", lambda h: h.scalar_tensor_tensor(
                        out=scr1[:], in0=khh[:], scalar=1.0 / DH,
                        in1=pSM[:, 224:256], op0=ALU.mult, op1=ALU.mult),
                        waits=[("tensor", st2)])
                    s.op("vector", lambda h: h.tensor_tensor(
                        out=scr1[:], in0=scrA[:, 64:96], in1=scr1[:],
                        op=ALU.subtract))
                    s.op("vector", lambda h: h.scalar_tensor_tensor(
                        out=scr1[:], in0=pSM[:, 192:224], scalar=-1.0 / DH,
                        in1=scr1[:], op0=ALU.mult, op1=ALU.add))
                    s.op("vector", lambda h: h.tensor_tensor(
                        out=scr2[:], in0=scr1[:], in1=rstdc[:], op=ALU.mult))
                    po = (j % 2) * 16
                    s.op("vector", lambda h, po=po: h.tensor_copy(
                        out=vnstg[:, po:po + 16], in_=scr2[:, 0:16]))
                    vn_d = s.op("vector", lambda h, po=po: h.tensor_copy(
                        out=vnstg[:, 32 + po:32 + po + 16], in_=scr2[:, 16:32]),
                        inc=True)

                    vstk = vnstkL if m < 2 else vnstkH
                    r0 = 32 * (m % 2)
                    nr = 16 if j % 2 == 0 else 32
                    src0 = 0 if j % 2 == 0 else 0
                    trs = 0
                    for hh in range(2):
                        trs = s.op("tensor", lambda h, hh=hh, r0=r0, nr=nr:
                                   h.transpose(
                                       pSMb[r0:r0 + nr,
                                            hh * 128:hh * 128 + 128],
                                       vnstg[:, hh * 32:hh * 32 + nr], c_id[:]),
                                   inc=True,
                                   waits=[("vector", vn_d)] if hh == 0 else ())
                    s.op("vector", lambda h, r0=r0, nr=nr, vstk=vstk:
                         h.tensor_copy(
                             out=vstk[r0:r0 + nr, 0:256],
                             in_=pSMb[r0:r0 + nr, 0:256]),
                         inc=True, waits=[("tensor", trs)])
                    last_stk = s.now("vector")
                    s.barrier("tensor", ["vector"])
                    if globals_pull[0] is not None:
                        globals_pull[0](1)

                for hh in range(2):
                    s.op("tensor", lambda h, hh=hh: h.matmul(
                        pB[:, hh * 128:hh * 128 + 128],
                        vnstkL[:, hh * 128:hh * 128 + 128],
                        agtL[:, hh * 128:hh * 128 + 128], start=False, stop=False),
                        waits=[("vector", last_stk)] if hh == 0 else ())
                    s.op("tensor", lambda h, hh=hh: h.matmul(
                        pB[:, hh * 128:hh * 128 + 128],
                        vnstkH[:, hh * 128:hh * 128 + 128],
                        agtH[:, hh * 128:hh * 128 + 128], start=False, stop=True))
                    s.op("tensor", lambda h, hh=hh, g=g: h.matmul(
                        pS[:, hh * 128:hh * 128 + 128],
                        ketL[:, hh * 128:hh * 128 + 128],
                        vnstkL[:, hh * 128:hh * 128 + 128],
                        start=(g == 0 and hh == 0), stop=False))
                    s.op("tensor", lambda h, hh=hh, g=g: h.matmul(
                        pS[:, hh * 128:hh * 128 + 128],
                        ketH[:, hh * 128:hh * 128 + 128],
                        vnstkH[:, hh * 128:hh * 128 + 128],
                        start=False, stop=True))
                qh_pe = s.now("tensor")

                for hh in range(2):
                    qf = qs_f[:, ao + hh * 128: ao + hh * 128 + 128]
                    s.op("vector", lambda h, hh=hh: h.tensor_copy(
                        out=oscr[:, 0:128], in_=pB[:, hh * 128:hh * 128 + 128]),
                        waits=[("tensor", qh_pe)] if hh == 0 else ())
                    o_ev = s.op("vector", lambda h: h.tensor_tensor(
                        out=oscr[:, 128:256], in0=oscr[:, 0:128],
                        in1=oscr[:, 0:128], op=ALU.mult), inc=True)
                    o_mm = s.op("tensor", lambda h: h.matmul(
                        pSM[:, 256:512], c_ones[:], oscr[:, 0:256],
                        start=True, stop=True), inc=True, waits=[("vector", o_ev)])
                    s.op("vector", lambda h: h.tensor_copy(
                        out=oo[:], in_=pSM[:, 256:384]), waits=[("tensor", o_mm)])
                    s.op("vector", lambda h: h.tensor_tensor(
                        out=oo[:], in0=oo[:], in1=oo[:], op=ALU.mult))
                    o_v = s.op("vector", lambda h: h.scalar_tensor_tensor(
                        out=oo[:], in0=oo[:], scalar=-1.0 / DH,
                        in1=pSM[:, 384:512], op0=ALU.mult, op1=ALU.add), inc=True)
                    s.op("scalar", lambda h: h.activation(
                        out=oo2[:], in_=oo[:], func=AF.Ln, scale=1.0 / DH,
                        bias=c_epsA[:]), waits=[("vector", o_v)])
                    o_rs = s.op("scalar", lambda h: h.activation(
                        out=oo2[:], in_=oo2[:], func=AF.Exp, scale=-0.5), inc=True)
                    s.op("vector", lambda h: h.scalar_tensor_tensor(
                        out=oo[:], in0=pSM[:, 256:384], scalar=-1.0 / DH,
                        in1=oscr[:, 0:128], op0=ALU.mult, op1=ALU.add),
                        waits=[("scalar", o_rs)])
                    s.op("vector", lambda h: h.tensor_tensor(
                        out=oo[:], in0=oo[:], in1=oo2[:], op=ALU.mult))
                    s.op("vector", lambda h, hh=hh: h.tensor_scalar(
                        out=oo[:], in0=oo[:], scalar1=c_lnw[:, hh:hh + 1],
                        scalar2=c_lnb[:, hh:hh + 1], op0=ALU.mult, op1=ALU.add))
                    o_d = s.op("vector", lambda h, qf=qf: h.tensor_tensor(
                        out=oo[:], in0=oo[:], in1=qf, op=ALU.add), inc=True)
                    g_mm = s.op("tensor", lambda h, hh=hh, sg=sg: h.matmul(
                        pQM[:, 384:512], wg2b[:, hh * 128:hh * 128 + 128],
                        g1_b[:, sg * 128:sg * 128 + 128], start=True, stop=True),
                        inc=True, waits=[("vector", o_d), ("vector", guard)])
                    s.op("scalar", lambda h: h.activation(
                        out=oth[:], in_=pQM[:, 384:512], func=AF.Exp, scale=-1.0),
                        waits=[("tensor", g_mm)])
                    s.op("scalar", lambda h: h.activation(
                        out=oth[:], in_=oth[:], func=AF.Ln, scale=1.0,
                        bias=c_onesc[:]))
                    g_th = s.op("scalar", lambda h: h.activation(
                        out=oth[:], in_=oth[:], func=AF.Exp, scale=-1.0), inc=True)
                    og_d = s.op("vector", lambda h: h.tensor_tensor(
                        out=oo[:], in0=oo[:], in1=oth[:], op=ALU.mult), inc=True,
                        waits=[("scalar", g_th)])
                    og_sq = s.op("scalar", lambda h: h.activation(
                        out=oo2[:], in_=oo[:], func=AF.Square), inc=True,
                        waits=[("vector", og_d)])
                    r_mm = s.op("tensor", lambda h: h.matmul(
                        pQM[0:1, 128:256], c_onesc[:], oo2[:],
                        start=True, stop=True), inc=True,
                        waits=[("scalar", og_sq)])
                    s.op("scalar", lambda h: h.activation(
                        out=strow[0:1, 512:640], in_=pQM[0:1, 128:256], func=AF.Ln,
                        scale=1.0 / DH, bias=c_epsA[0:1, :]),
                        waits=[("tensor", r_mm)])
                    r_rs = s.op("scalar", lambda h: h.activation(
                        out=strow[0:1, 512:640], in_=strow[0:1, 512:640],
                        func=AF.Exp, scale=-0.5), inc=True)
                    r_bc = s.op("tensor", lambda h: h.matmul(
                        pQM[:, 128:256], c_onesrow[0:1, :], strow[0:1, 512:640],
                        start=True, stop=True), inc=True, waits=[("scalar", r_rs)])
                    s.op("vector", lambda h: h.tensor_tensor(
                        out=oo[:], in0=oo[:], in1=pQM[:, 128:256], op=ALU.mult),
                        waits=[("tensor", r_bc)])
                    ob_d = s.op("vector", lambda h, hh=hh: h.tensor_scalar(
                        out=obuf[:, hh * 128:hh * 128 + 128], in0=oo[:],
                        scalar1=c_nw[:, hh:hh + 1], scalar2=None, op0=ALU.mult),
                        inc=True)
                    s.barrier("tensor", ["vector"])
                    s.barrier("scalar", ["vector"])
                    s.op("sync", lambda h, g=g, hh=hh: h.dma_start(
                        out=o_st5[g // GQ][g % 8, (g % GQ) // 8, hh, :, :],
                        in_=obuf[:, hh * 128:hh * 128 + 128]), inc=True,
                        waits=[("vector", ob_d)])
                scan_done[g] = s.now("vector")
                s.barrier("vector", ["sync"])

            globals_pull = [None]

            def emit_collective(q):
                ns = s.now("sync")
                import os as _os
                if _os.environ.get("KERN_NO_CC"):
                    s.op("gpsimd", lambda h, q=q: h.dma_start(
                        out=o_gath[q].ap(), in_=o_stage[q].ap()).then_inc(
                            sems["cc"], 1), inc=False, waits=[("sync", ns)])
                else:
                    def do_cc(h, q=q):
                        with allow_ici_all_to_all():
                            return h.collective_compute(
                                "AllToAll", ALU.bypass,
                                replica_groups=[list(range(NCORE))],
                                ins=[o_stage[q].ap().opt()],
                                outs=[o_gath[q].ap().opt()]).then_inc(
                                    sems["cc"], 1)
                    s.op("gpsimd", do_cc, inc=False, waits=[("sync", ns)])

            def c_wave(q):
                t0q = q * TS
                for _ in range(12):
                    yield
                for src in range(NCORE):
                    for hh in range(2):
                        dti = src * 2 + hh
                        wv = [("cc", q + 1), ("tensor", s.now("tensor")),
                              ("pool", s.now("pool"))] \
                            if (src == 0 and hh == 0) else ()
                        s.op("sync", lambda h, src=src, hh=hh, dti=dti, q=q:
                             h.dma_start(
                                 out=oslab[:, dti * TS:dti * TS + TS].rearrange(
                                     "p (l c) -> p l c", l=LGQ),
                                 in_=o_gt5[q][src, :, hh].rearrange(
                                     "l p c -> p l c")), inc=True, waits=wv)
                s.op("gpsimd", lambda h, t0q=t0q: h.dma_start(
                    out=xinb[:].rearrange("p (d c) -> p d c", d=16),
                    in_=xres[:, :, t0q:t0q + TS]), inc=True)
                ldxi = s.now("gpsimd")
                lds = s.now("sync")
                yield

                stat_t = {}
                wo_ld = {}
                pe_done = {}
                for t in range(2):
                    s.op("sync", lambda h, t=t: h.dma_start(
                        out=wrng[:, (t % 4) * 2048:(t % 4) * 2048 + 2048],
                        in_=wobf_dram.ap()[t]), inc=True,
                        waits=[pf_wait("wo", t)])
                    wo_ld[t] = s.now("sync")
                for t in range(16):
                    pq0 = 0
                    for d in range(16):
                        wv = [("sync", wo_ld[t]), ("sync", lds)] \
                            if d == 0 else ()
                        if d == 0 and t >= 1:
                            wv = list(wv) + [("scalar", stat_t[t - 1])]
                        s.op("tensor", lambda h, t=t, d=d: h.matmul(
                            pC[:, 0:TS],
                            wrng[:, (t % 4) * 2048 + d * 128:
                                 (t % 4) * 2048 + d * 128 + 128],
                            oslab[:, d * TS:d * TS + TS],
                            start=(d == 0), stop=(d == 15)), waits=wv)
                    pe_t = s.now("tensor")
                    pe_done[t] = pe_t
                    if t + 2 < 16:
                        s.op("sync", lambda h, t=t: h.dma_start(
                            out=wrng[:, ((t + 2) % 4) * 2048:
                                     ((t + 2) % 4) * 2048 + 2048],
                            in_=wobf_dram.ap()[t + 2]), inc=True,
                            waits=[("tensor", pe_t), pf_wait("wo", t + 2)])
                        wo_ld[t + 2] = s.now("sync")
                    s.op("scalar", lambda h, t=t: h.copy(
                        out=zdn[:, t * TS:t * TS + TS], in_=pC[:, 0:TS]),
                        waits=[("tensor", pe_t)])
                    s.op("scalar", lambda h, t=t: h.copy(
                        out=zb2[:, (t % 2) * TS:(t % 2) * TS + TS],
                        in_=pC[:, 0:TS]))
                    zev = s.op("scalar", lambda h, t=t: h.activation(
                        out=zq2[:, (t % 2) * TS:(t % 2) * TS + TS],
                        in_=zb2[:, (t % 2) * TS:(t % 2) * TS + TS],
                        func=AF.Square), inc=True)
                    s.op("tensor", lambda h, t=t: h.matmul(
                        pC[0:1, 256:256 + TS], c_onesb[:],
                        zb2[:, (t % 2) * TS:(t % 2) * TS + TS],
                        start=True, stop=True), waits=[("scalar", zev)])
                    s.op("tensor", lambda h, t=t: h.matmul(
                        pC[32:33, 256:256 + TS], c_onesb[:, 0:1],
                        zq2[:, (t % 2) * TS:(t % 2) * TS + TS],
                        start=True, stop=True))
                    smm = s.now("tensor")
                    s.op("scalar", lambda h: h.copy(
                        out=stmp[0:1, 0:TS], in_=pC[0:1, 256:256 + TS]),
                        waits=[("tensor", smm)])
                    sr = s.op("scalar", lambda h: h.copy(
                        out=stmp[0:1, TS:2 * TS], in_=pC[32:33, 256:256 + TS]),
                        inc=True)
                    if t == 0:
                        s.op("vector", lambda h: h.tensor_copy(
                            out=sac1[0:1, :], in_=stmp[0:1, 0:TS]),
                            inc=True, waits=[("scalar", sr)])
                        s.op("vector", lambda h: h.tensor_copy(
                            out=sac2[0:1, :], in_=stmp[0:1, TS:2 * TS]),
                            inc=True)
                    else:
                        s.op("vector", lambda h: h.tensor_tensor(
                            out=sac1[0:1, :], in0=sac1[0:1, :],
                            in1=stmp[0:1, 0:TS], op=ALU.add),
                            inc=True, waits=[("scalar", sr)])
                        s.op("vector", lambda h: h.tensor_tensor(
                            out=sac2[0:1, :], in0=sac2[0:1, :],
                            in1=stmp[0:1, TS:2 * TS], op=ALU.add), inc=True)
                    stat_t[t] = sr
                    yield

                def ln_rows_bcast(st):
                    pl = s.op("vector", lambda h: h.tensor_tensor(
                        out=stmp[0:1, 2 * TS:3 * TS], in0=sac1[0:1, :],
                        in1=sac1[0:1, :], op=ALU.mult), inc=True)
                    pl = s.op("vector", lambda h: h.scalar_tensor_tensor(
                        out=stmp[0:1, 2 * TS:3 * TS],
                        in0=stmp[0:1, 2 * TS:3 * TS], scalar=-1.0 / D,
                        in1=sac2[0:1, :], op0=ALU.mult, op1=ALU.add), inc=True)
                    s.op("scalar", lambda h: h.activation(
                        out=stmp[0:1, 3 * TS:4 * TS],
                        in_=stmp[0:1, 2 * TS:3 * TS], func=AF.Ln,
                        scale=1.0 / D, bias=c_epsM[0:1, :]),
                        waits=[("vector", pl)])
                    s.op("scalar", lambda h: h.activation(
                        out=stmp[0:1, 3 * TS:4 * TS],
                        in_=stmp[0:1, 3 * TS:4 * TS], func=AF.Exp, scale=-0.5))
                    mr = s.op("scalar", lambda h: h.activation(
                        out=stmp[0:1, 2 * TS:3 * TS], in_=sac1[0:1, :],
                        func=AF.Copy, scale=1.0 / D), inc=True)
                    b1 = s.op("tensor", lambda h: h.matmul(
                        pC[:, 256:256 + TS], c_onesrow[0:1, :],
                        stmp[0:1, 3 * TS:4 * TS], start=True, stop=True),
                        waits=[("scalar", mr)])
                    b1 = s.op("tensor", lambda h: h.matmul(
                        pC[:, 0:TS], c_onesrow[0:1, :],
                        stmp[0:1, 2 * TS:3 * TS], start=True, stop=True))
                    b1 = s.now("tensor")
                    s.op("scalar", lambda h, st=st: h.copy(
                        out=rB1[:, st * TS:st * TS + TS], in_=pC[:, 256:256 + TS]),
                        waits=[("tensor", b1), ("pool", s.now("pool"))])
                    bd = s.op("scalar", lambda h, st=st: h.copy(
                        out=csc1[:, st * TS:st * TS + TS], in_=pC[:, 0:TS]),
                        inc=True)
                    return bd

                bd = ln_rows_bcast(0)
                yield

                smm_t = {}
                for t in range(16):
                    p1 = s.op("gpsimd", lambda h, t=t: h.tensor_tensor(
                        out=csc2[:, 0:TS], in0=zdn[:, t * TS:t * TS + TS],
                        in1=csc1[:, 0:TS], op=ALU.subtract),
                        inc=False, waits=[("scalar", bd)] if t == 0 else (), sem="pool")
                    s.op("gpsimd", lambda h: h.tensor_tensor(
                        out=csc2[:, 0:TS], in0=csc2[:, 0:TS], in1=rB1[:, 0:TS],
                        op=ALU.mult), sem="pool")
                    x1d = s.op("gpsimd", lambda h, t=t: h.tensor_tensor(
                        out=x1sl[:, t * TS:t * TS + TS], in0=csc2[:, 0:TS],
                        in1=xinb[:, t * TS:t * TS + TS], op=ALU.add), inc=True,
                        waits=[("gpsimd", ldxi)] if t == 0 else (), sem="pool")
                    zwv = [("pool", x1d)]
                    if t >= 2:
                        zwv.append(("tensor", smm_t[t - 2]))
                    zev = s.op("scalar", lambda h, t=t: h.activation(
                        out=zq2[:, (t % 2) * TS:(t % 2) * TS + TS],
                        in_=x1sl[:, t * TS:t * TS + TS],
                        func=AF.Square), inc=True, waits=zwv)
                    s.op("tensor", lambda h, t=t: h.matmul(
                        pC[0:1, 256:256 + TS], c_onesb[:],
                        x1sl[:, t * TS:t * TS + TS],
                        start=(t == 0), stop=(t == 15)),
                        waits=[("scalar", zev)])
                    s.op("tensor", lambda h, t=t: h.matmul(
                        pC[32:33, 256:256 + TS], c_onesb[:, 0:1],
                        zq2[:, (t % 2) * TS:(t % 2) * TS + TS],
                        start=(t == 0), stop=(t == 15)))
                    smm_t[t] = s.now("tensor")
                    if t % 4 == 3:
                        yield
                smm = smm_t[15]
                s.op("scalar", lambda h: h.copy(
                    out=stmp[0:1, 0:TS], in_=pC[0:1, 256:256 + TS]),
                    waits=[("tensor", smm)])
                sr = s.op("scalar", lambda h: h.copy(
                    out=stmp[0:1, TS:2 * TS], in_=pC[32:33, 256:256 + TS]),
                    inc=True)
                s.op("vector", lambda h: h.tensor_copy(
                    out=sac1[0:1, :], in_=stmp[0:1, 0:TS]),
                    inc=True, waits=[("scalar", sr)])
                s.op("vector", lambda h: h.tensor_copy(
                    out=sac2[0:1, :], in_=stmp[0:1, TS:2 * TS]), inc=True)

                bd = ln_rows_bcast(1)
                import os as _os2
                if _os2.environ.get("KERN_DBG") and q == NQ - 1:
                    s.op("sync", lambda h: h.dma_start(
                        out=dbg[0:1, 0:TS], in_=sac1[0:1, :]), inc=True,
                        waits=[("scalar", bd)])
                    s.op("sync", lambda h: h.dma_start(
                        out=dbg[0:1, 512:512 + TS], in_=sac2[0:1, :]), inc=True)
                    s.op("sync", lambda h: h.dma_start(
                        out=dbg[0:1, 1024:1024 + TS],
                        in_=rB1[0:1, TS:2 * TS]), inc=True)
                    s.op("sync", lambda h: h.dma_start(
                        out=dbg[0:1, 1536:1536 + TS],
                        in_=csc1[0:1, TS:2 * TS]), inc=True)
                    s.op("gpsimd", lambda h: h.dma_start(
                        out=dbg[0:1, 2048:2048 + TS], in_=zq2[0:1, 0:TS]),
                        inc=True)
                    s.op("gpsimd", lambda h: h.dma_start(
                        out=dbg[0:1, 2560:2560 + TS],
                        in_=x1sl[0:1, 15 * TS:15 * TS + TS]), inc=True)
                yield
                for t in range(16):
                    p1 = s.op("gpsimd", lambda h, t=t: h.tensor_tensor(
                        out=csc2[:, 0:TS], in0=x1sl[:, t * TS:t * TS + TS],
                        in1=csc1[:, TS:2 * TS], op=ALU.subtract),
                        inc=False, waits=[("scalar", bd)] if t == 0 else (), sem="pool")
                    s.op("gpsimd", lambda h, t=t: h.tensor_tensor(
                        out=x2sl[:, t * TS:t * TS + TS], in0=csc2[:, 0:TS],
                        in1=rB1[:, TS:2 * TS], op=ALU.mult), inc=(t == 15),
                        sem="pool")
                    if t % 8 == 7:
                        yield
                x2_d = s.now("pool")

                guv_done = {}
                for ht in range(48):
                    sl4 = ht % 2
                    wv = [("pool", x2_d), pf_wait("g", ht)]
                    if ht >= 2:
                        wv.append(("tensor", guv_done[ht - 2]))
                    s.op("sync", lambda h, ht=ht, sl4=sl4: h.dma_start(
                        out=g8s[:, sl4 * 2048:sl4 * 2048 + 2048],
                        in_=g8_dram.ap()[:, ht * 2048:ht * 2048 + 2048]),
                        inc=True, waits=wv)
                    s.op("sync", lambda h, ht=ht, sl4=sl4: h.dma_start(
                        out=u8s[:, sl4 * 2048:sl4 * 2048 + 2048],
                        in_=u8_dram.ap()[:, ht * 2048:ht * 2048 + 2048]),
                        inc=True, waits=[pf_wait("u", ht)])
                    wgu = s.now("sync")
                    yield

                    for d in range(16):
                        wv = [("sync", wgu)] if d == 0 else ()
                        s.op("tensor", lambda h, d=d, sl4=sl4: h.matmul(
                            pC[:, 0:TS],
                            g8s[:, sl4 * 2048 + d * 128:
                                sl4 * 2048 + d * 128 + 128],
                            x2sl[:, d * TS:d * TS + TS],
                            start=(d == 0), stop=(d == 15)), waits=wv)
                    for d in range(16):
                        s.op("tensor", lambda h, d=d, sl4=sl4: h.matmul(
                            pC[:, 256:256 + TS],
                            u8s[:, sl4 * 2048 + d * 128:
                                sl4 * 2048 + d * 128 + 128],
                            x2sl[:, d * TS:d * TS + TS],
                            start=(d == 0), stop=(d == 15)))
                    gu_pe = s.now("tensor")
                    guv_done[ht] = gu_pe
                    s.op("scalar", lambda h, ht=ht: h.activation(
                        out=gth[:], in_=pC[:, 0:TS], func=AF.Exp,
                        scale=ngs[:, ht:ht + 1]), waits=[("tensor", gu_pe)])
                    s.op("scalar", lambda h: h.activation(
                        out=gth[:], in_=gth[:], func=AF.Ln, scale=1.0,
                        bias=c_onesc[:]))
                    s.op("scalar", lambda h: h.activation(
                        out=gth[:], in_=gth[:], func=AF.Exp, scale=-1.0))
                    s.op("scalar", lambda h, ht=ht: h.activation(
                        out=gxh[:], in_=pC[:, 0:TS], func=AF.Copy,
                        scale=gscb[:, ht:ht + 1]))
                    guc = s.op("scalar", lambda h, ht=ht: h.activation(
                        out=guv[:], in_=pC[:, 256:256 + TS], func=AF.Copy,
                        scale=uscb[:, ht:ht + 1]), inc=True)
                    s.op("gpsimd", lambda h: h.tensor_tensor(
                        out=gvv[:], in0=gth[:], in1=gxh[:], op=ALU.mult),
                        inc=False, waits=[("scalar", guc)] + (
                            [("tensor", s.now("tensor"))] if ht == 0 else []),
                        sem="pool")
                    s.op("gpsimd", lambda h, ht=ht: h.tensor_tensor(
                        out=fusl[:, ht * TS:ht * TS + TS], in0=gvv[:],
                        in1=guv[:], op=ALU.mult), inc=True, sem="pool")
                    s.barrier("tensor", ["scalar"])
                    yield
                fu_d = s.now("pool")

                dn_done = {}
                for t in range(16):
                    sl2 = t % 2
                    wv = [("pool", fu_d), pf_wait("d", t)]
                    if t >= 1:
                        wv.append(("tensor", dn_done[t - 1]))
                    s.op("sync", lambda h, t=t: h.dma_start(
                        out=d8s[:],
                        in_=d8_dram.ap()[:, t * 6144:t * 6144 + 6144]),
                        inc=True, waits=wv)
                    dw_d = s.now("sync")
                    yield
                    for half in range(2):
                        for dd in range(24):
                            ddv = half * 24 + dd
                            wv = [("sync", dw_d)] \
                                if (half == 0 and dd == 0) else ()
                            s.op("tensor", lambda h, ddv=ddv: h.matmul(
                                pC[:, 0:TS],
                                d8s[:, ddv * 128:ddv * 128 + 128],
                                fusl[:, ddv * TS:ddv * TS + TS],
                                start=(ddv == 0), stop=(ddv == 47)), waits=wv)
                        yield
                    dn_pe = s.now("tensor")
                    dn_done[t] = dn_pe
                    s.op("scalar", lambda h, t=t: h.activation(
                        out=zdn[:, t * TS:t * TS + TS], in_=pC[:, 0:TS],
                        func=AF.Copy, scale=dscb[:, t:t + 1]),
                        waits=[("tensor", dn_pe)])
                    s.op("scalar", lambda h, t=t: h.copy(
                        out=zb2[:, (t % 2) * TS:(t % 2) * TS + TS],
                        in_=zdn[:, t * TS:t * TS + TS]))
                    zev = s.op("scalar", lambda h, t=t: h.activation(
                        out=zq2[:, (t % 2) * TS:(t % 2) * TS + TS],
                        in_=zb2[:, (t % 2) * TS:(t % 2) * TS + TS],
                        func=AF.Square), inc=True)
                    s.op("tensor", lambda h, t=t: h.matmul(
                        pC[0:1, 256:256 + TS], c_onesb[:],
                        zb2[:, (t % 2) * TS:(t % 2) * TS + TS],
                        start=True, stop=True),
                        waits=[("scalar", zev), ("scalar", s.now("scalar"))])
                    s.op("tensor", lambda h, t=t: h.matmul(
                        pC[32:33, 256:256 + TS], c_onesb[:, 0:1],
                        zq2[:, (t % 2) * TS:(t % 2) * TS + TS],
                        start=True, stop=True))
                    smm = s.now("tensor")
                    s.op("scalar", lambda h: h.copy(
                        out=stmp[0:1, 0:TS], in_=pC[0:1, 256:256 + TS]),
                        waits=[("tensor", smm)])
                    sr = s.op("scalar", lambda h: h.copy(
                        out=stmp[0:1, TS:2 * TS], in_=pC[32:33, 256:256 + TS]),
                        inc=True)
                    if t == 0:
                        s.op("vector", lambda h: h.tensor_copy(
                            out=sac1[0:1, :], in_=stmp[0:1, 0:TS]),
                            inc=True, waits=[("scalar", sr)])
                        s.op("vector", lambda h: h.tensor_copy(
                            out=sac2[0:1, :], in_=stmp[0:1, TS:2 * TS]),
                            inc=True)
                    else:
                        s.op("vector", lambda h: h.tensor_tensor(
                            out=sac1[0:1, :], in0=sac1[0:1, :],
                            in1=stmp[0:1, 0:TS], op=ALU.add),
                            inc=True, waits=[("scalar", sr)])
                        s.op("vector", lambda h: h.tensor_tensor(
                            out=sac2[0:1, :], in0=sac2[0:1, :],
                            in1=stmp[0:1, TS:2 * TS], op=ALU.add), inc=True)
                    yield

                bd = ln_rows_bcast(2)
                yield
                trprev = {}
                for t in range(16):
                    cs0 = (t % 2) * TS
                    wv0 = [("scalar", bd)] if t == 0 else []
                    if t >= 2:
                        wv0.append(("tensor", trprev[t - 2]))
                    s.op("gpsimd", lambda h, t=t, cs0=cs0: h.tensor_tensor(
                        out=csc2[:, cs0:cs0 + TS], in0=zdn[:, t * TS:t * TS + TS],
                        in1=csc1[:, 2 * TS:3 * TS], op=ALU.subtract),
                        inc=False, waits=wv0, sem="pool")
                    s.op("gpsimd", lambda h, cs0=cs0: h.tensor_tensor(
                        out=csc2[:, cs0:cs0 + TS], in0=csc2[:, cs0:cs0 + TS],
                        in1=rB1[:, 2 * TS:3 * TS], op=ALU.mult), sem="pool")
                    ov = s.op("gpsimd", lambda h, t=t, cs0=cs0: h.tensor_tensor(
                        out=csc2[:, cs0:cs0 + TS], in0=csc2[:, cs0:cs0 + TS],
                        in1=x1sl[:, t * TS:t * TS + TS], op=ALU.add), inc=True, sem="pool")
                    for iq in range(TS // 128):
                        trq = s.op("tensor", lambda h, iq=iq, cs0=cs0: h.transpose(
                            pC.ap().bitcast(F32)[:, 0:128],
                            csc2[:, cs0 + iq * 128:cs0 + iq * 128 + 128],
                            c_idf[:]), inc=False,
                            waits=[("pool", ov), ("sync", outd.get(iq ^ 1, 0))]
                            if iq == 0 else [("sync", outd.get(iq ^ 1, 0))])
                        trq = s.now("tensor")
                        otd = s.op("scalar", lambda h, iq=iq: h.copy(
                            out=outp[:, iq * 128:iq * 128 + 128],
                            in_=pC.ap().bitcast(F32)[:, 0:128]), inc=True,
                            waits=[("tensor", trq)])
                        s.barrier("tensor", ["scalar"])
                        s.op("sync", lambda h, t=t, iq=iq, t0q=t0q: h.dma_start(
                            out=out[t0q + iq * 128: t0q + iq * 128 + 128,
                                    t * 128:t * 128 + 128],
                            in_=outp[:, iq * 128:iq * 128 + 128]), inc=True,
                            waits=[("scalar", otd)])
                        outd[iq] = s.now("sync")
                    trprev[t] = s.now("tensor")
                    if t % 4 == 3:
                        yield

            outd = {}
            import collections as _c
            pending = _c.deque()

            def pull(k):
                done = 0
                while pending and done < k:
                    try:
                        next(pending[0])
                        done += 1
                    except StopIteration:
                        pending.popleft()

            globals_pull[0] = pull

            for g in range(NG + 1):
                if g >= 1:
                    scan_group(g - 1)
                if g >= 1 and g % GQ == 0:
                    q = g // GQ - 1
                    emit_collective(q)
                    pending.append(c_wave(q))
                if g < NG:
                    qkv_group(g)
                    pf_pump(5)
            pf_pump(10 ** 9)
            while pending:
                pull(1000)
            s.barrier("gpsimd", ["sync", "vector", "scalar", "tensor"])

        s.emit(block, sems)
    return nc


def stage_inputs(inputs, T):
    Tc = T // NCORE
    x = np.ascontiguousarray(inputs["L0_attn_x"][0, :T].T.astype(np.float32))
    xT_s = np.ascontiguousarray(x.reshape(16, 128, T).transpose(1, 0, 2))

    Wq, Wk, Wv = (inputs["L0_attn_Wq"], inputs["L0_attn_Wk"], inputs["L0_attn_Wv"])
    Wbeta = inputs["L0_attn_Wbeta"]
    Wg1 = inputs["L0_attn_Wg1"]
    Wo_ = inputs["L0_attn_Wo"]
    Wg2_ = inputs["L0_attn_Wg2"]
    lnw, lnb = inputs["L0_attn_ln_w"], inputs["L0_attn_ln_b"]
    nw = inputs["L0_attn_norm_w"]
    gw = inputs["L0_ffn_gate_w"]
    uw = inputs["L0_ffn_up_w"]
    dw = inputs["L0_ffn_down_w"]

    wo_s = np.ascontiguousarray(
        Wo_.reshape(16, 128, 16, 128).transpose(2, 1, 0, 3)).reshape(16, 128, -1)
    gwt_s = np.ascontiguousarray(
        gw.reshape(16, 128, 48, 128).transpose(2, 1, 0, 3)).reshape(48, 128, -1)
    uwt_s = np.ascontiguousarray(
        uw.reshape(16, 128, 48, 128).transpose(2, 1, 0, 3)).reshape(48, 128, -1)
    dwt_s = np.ascontiguousarray(
        dw.reshape(48, 128, 16, 128).transpose(2, 1, 0, 3)).reshape(16, 128, -1)
    gsc_s = np.ascontiguousarray(inputs["L0_ffn_gate_scale"].reshape(48, 128).T)
    usc_s = np.ascontiguousarray(inputs["L0_ffn_up_scale"].reshape(48, 128).T)
    dsc_s = np.ascontiguousarray(inputs["L0_ffn_down_scale"].reshape(16, 128).T)
    sidx = np.arange(128)
    mkK_s = ((sidx[:, None] // CH) < (sidx[None, :] // CH)).astype(np.float32)
    mkG_s = (sidx[:, None] <= sidx[None, :]).astype(np.float32)
    mkKL_s = np.ascontiguousarray(mkK_s[0:64]); mkKH_s = np.ascontiguousarray(mkK_s[64:128])
    mkGL_s = np.ascontiguousarray(mkG_s[0:64]); mkGH_s = np.ascontiguousarray(mkG_s[64:128])

    per_core = []
    for c in range(NCORE):
        h0, h1 = 2 * c, 2 * c + 1
        cols = np.zeros((D, 1024), np.float32)
        cols[:, 0:128] = Wq[:, h0 * 128:(h0 + 1) * 128]
        cols[:, 128:256] = Wq[:, h1 * 128:(h1 + 1) * 128]
        cols[:, 256:384] = Wk[:, h0 * 128:(h0 + 1) * 128]
        cols[:, 384:512] = Wk[:, h1 * 128:(h1 + 1) * 128]
        cols[:, 512:640] = Wv[:, h0 * 128:(h0 + 1) * 128]
        cols[:, 640:768] = Wv[:, h1 * 128:(h1 + 1) * 128]
        cols[:, 768:896] = Wg1
        cols[:, 896] = 1.0
        cols[:, 896 + 32] = Wbeta[:, h0]
        cols[:, 896 + 64] = Wbeta[:, h1]
        wf_s = np.ascontiguousarray(
            cols.reshape(16, 128, 8, 128).transpose(1, 0, 2, 3)).reshape(128, -1)
        wg2_s = np.ascontiguousarray(np.concatenate(
            [Wg2_[:, h0 * 128:(h0 + 1) * 128], Wg2_[:, h1 * 128:(h1 + 1) * 128]],
            axis=1))
        lnw_cs = np.ascontiguousarray(
            np.stack([lnw[h0], lnw[h1]], axis=1).astype(np.float32))
        lnb_cs = np.ascontiguousarray(
            np.stack([lnb[h0], lnb[h1]], axis=1).astype(np.float32))
        nw_cs = np.ascontiguousarray(np.stack(
            [nw[h0 * 128:(h0 + 1) * 128], nw[h1 * 128:(h1 + 1) * 128]],
            axis=1).astype(np.float32))
        NLGs = Tc // 128
        xres_s = np.ascontiguousarray(np.concatenate(
            [xT_s[:, :, (8 * lg + c) * 128:(8 * lg + c) * 128 + 128]
             for lg in range(NLGs)], axis=2))

        per_core.append({
            "xT": xT_s, "wf": wf_s, "wo": wo_s, "wg2": wg2_s,
            "lnw_c": lnw_cs, "lnb_c": lnb_cs, "nw_c": nw_cs,
            "gwt": gwt_s, "uwt": uwt_s, "dwt": dwt_s,
            "gsc": gsc_s, "usc": usc_s, "dsc": dsc_s,
            "idn": np.eye(128, dtype=BF),
            "mkKL": mkKL_s, "mkKH": mkKH_s, "mkGL": mkGL_s, "mkGH": mkGH_s,
            "onesc": np.ones((128, 1), np.float32),
            "onesf": np.ones((128, 128), np.float32),
            "onesrow": np.ones((1, 128), np.float32),
            "xres": xres_s,
        })
    return per_core


def kernel(**inputs):
    T = inputs["L0_attn_x"].shape[1]
    nc = build_program(T)
    in_maps = stage_inputs(inputs, T)
    res = run_bass_kernel_spmd(nc, in_maps, core_ids=list(range(NCORE)))
    outs = [np.asarray(res.results[c]["out"]) for c in range(NCORE)]
    Tc = T // NCORE
    full = np.empty((T, D), np.float32)
    for c in range(NCORE):
        for lg in range(Tc // 128):
            g = 8 * lg + c
            full[g * 128:(g + 1) * 128] = outs[c][lg * 128:(lg + 1) * 128]
    return full[None].astype(np.float32)
